# version 06
# baseline (speedup 1.0000x reference)
import math
import numpy as np
import ml_dtypes
import concourse.bass as bass
import concourse.mybir as mybir
from concourse.bass_utils import run_bass_kernel_spmd

F32 = mybir.dt.float32
BF16 = mybir.dt.bfloat16
AF = mybir.ActivationFunctionType
ALU = mybir.AluOpType
AX = mybir.AxisListType
BF = ml_dtypes.bfloat16

S = 2048
D = 2048
NH = 16
DKV = 512
DR = 32
DH = 128
SP = 96
INTER = 8192
NT = 16
EPS = 1e-5
CSC = 1.0 / math.sqrt(128.0)
RSC = math.sqrt(128.0)



def _hadamard(n):
    h = np.array([[1.0]], dtype=np.float64)
    while h.shape[0] < n:
        h = np.block([[h, h], [h, -h]])
    return h.astype(np.float32)


def _host_consts():
    c = {}
    c["h128"] = _hadamard(128).astype(BF)
    c["h64"] = _hadamard(64).astype(BF)
    H = _hadamard(2048)
    c["h2048t"] = np.ascontiguousarray(
        H.reshape(16, 128, 16, 128).transpose(2, 1, 0, 3)).astype(BF)
    R = np.zeros((32, 32), dtype=np.float32)
    for i in range(16):
        R[i, i + 16] = -1.0
        R[i + 16, i] = 1.0
    M = np.zeros((128, 128), dtype=np.float32)
    for b in range(4):
        M[b * 32:(b + 1) * 32, b * 32:(b + 1) * 32] = R
    c["mrotT"] = M.T.copy().astype(BF)
    inv_freq = 1.0 / 10000.0 ** (np.arange(0, DR, 2, dtype=np.float64) / DR)
    pos = np.arange(S, dtype=np.float64)
    freqs = pos[:, None] * inv_freq[None, :]
    emb = np.concatenate([freqs, freqs], axis=1)
    cos_fm = np.cos(emb).T.astype(np.float32)
    sin_fm = np.sin(emb).T.astype(np.float32)
    c["cosk"] = np.tile(cos_fm, (4, 1)).astype(BF)
    c["sink"] = np.tile(sin_fm, (4, 1)).astype(BF)
    c["onesb"] = np.ones((1, 128), dtype=BF)
    c["onesf"] = np.ones((1, 128), dtype=np.float32)
    c["onescol"] = np.ones((128, 1), dtype=BF)
    c["identb"] = np.eye(128, dtype=BF)
    c["identf"] = np.eye(128, dtype=np.float32)
    return c



TRACE = {}
CUR_ENG = [None]
TAG = [""]
PHASE_MARKS = {}


class Sem:
    def __init__(self, nc, name):
        self.h = nc.semaphore(name).__enter__()
        self.name = name
        self.n = 0
        self._waited = {}

    def inc(self, instr, k=1):
        instr.then_inc(self.h, k)
        self.n += k
        if CUR_ENG[0] is not None:
            TRACE.setdefault(CUR_ENG[0], []).append(
                ("i", self.name, k, TAG[0]))
        return self.n

    def wait(self, eng, n=None):
        n = self.n if n is None else n
        if n <= 0:
            return
        key = str(eng.engine)
        if self._waited.get(key, -1) < n:
            eng.wait_ge(self.h, n)
            self._waited[key] = n
            if CUR_ENG[0] is not None:
                TRACE.setdefault(CUR_ENG[0], []).append(
                    ("w", self.name, n, TAG[0]))


def check_deadlock():
    pos = {e: 0 for e in TRACE}
    cnt = {}
    progress = True
    while progress:
        progress = False
        for e, ops in TRACE.items():
            while pos[e] < len(ops):
                kind, sname, val = ops[pos[e]][:3]
                if kind == "i":
                    cnt[sname] = cnt.get(sname, 0) + val
                    pos[e] += 1
                    progress = True
                elif cnt.get(sname, 0) >= val:
                    pos[e] += 1
                    progress = True
                else:
                    break
    stuck = {e: (pos[e], len(TRACE[e]), TRACE[e][pos[e]] if pos[e] < len(TRACE[e]) else None)
             for e in TRACE if pos[e] < len(TRACE[e])}
    if stuck:
        msg = [f"DEADLOCK: engine {e} stuck at op {p}/{n} waiting {w} (have {cnt.get(w[1], 0) if w else '?'})"
               for e, (p, n, w) in stuck.items()]
        raise RuntimeError("\n".join(msg))


class Arena:

    def __init__(self, nc, name, nelem, dtype):
        self.t = nc.alloc_sbuf_tensor(name, [128, nelem], dtype)
        self.n = nelem

    def v(self, off, n, p0=0, p1=128):
        assert 0 <= off and off + n <= self.n, (off, n, self.n)
        return self.t[p0:p1, off:off + n]


def build_program(taps=(), stop_after=None, opts=None):
    taps = set(taps)
    TRACE.clear()
    nc = bass.Bass("TRN2", target_bir_lowering=False)
    hc = _host_consts()

    x_d = nc.dram_tensor("x", [S, D], F32, kind="ExternalInput")
    w = {}
    for name, shape in [
        ("Wdkv", [D, DKV]), ("Wdq", [D, DKV]), ("Wuk", [DKV, NH * SP]),
        ("Wuv", [DKV, NH * DH]), ("Wuq", [DKV, NH * SP]), ("Wqr", [DKV, NH * DR]),
        ("Wkr", [D, NH * DR]), ("Wo", [D, D]),
        ("WupT", [D, INTER]), ("WgT", [D, INTER]), ("WdnT", [INTER, D]),
    ]:
        w[name] = nc.dram_tensor(name, shape, F32, kind="ExternalInput")
    bia = {}
    for name, shape in [
        ("bdkv", [128, 4]), ("bdq", [128, 4]), ("bkr", [128, 4]),
        ("buk", [96, 16]), ("buq", [96, 16]), ("bqr", [32, 16]),
        ("buv", [1, D]), ("bo", [128, 16]),
        ("suug", [128, 16]), ("wsu", [128, 64]), ("wsg", [128, 64]),
        ("sudp", [128, 64]), ("wdp", [128, 16]),
    ]:
        bia[name] = nc.dram_tensor(name, shape, F32, kind="ExternalInput")
    con = {}
    for name in ["h128", "h64", "h2048t", "mrotT", "cosk", "sink",
                 "onesb", "onesf", "onescol", "identb", "identf"]:
        arr = hc[name]
        dt = BF16 if arr.dtype == BF else F32
        con[name] = nc.dram_tensor(name, list(arr.shape), dt, kind="ExternalInput")

    out_d = nc.dram_tensor("out", [S, D], F32, kind="ExternalOutput")

    wo_bf = nc.dram_tensor("wo_bf", [16, 128, 16, 128], BF16)
    wdn_bf = nc.dram_tensor("wdn_bf", [16, 128, 64, 128], BF16)
    cq_scr = nc.dram_tensor("cq_scr", [4, 128, S], BF16)
    x1_scr = nc.dram_tensor("x1_scr", [S, D], F32)
    proj_scr = nc.dram_tensor("proj_scr", [S, D], BF16)
    z_scr = nc.dram_tensor("z_scr", [64, 128, S], BF16)
    yt_scr = nc.dram_tensor("yt_scr", [64, 128, S], BF16)

    tap_d = {}
    def tap(name, shape, dtype=F32):
        if name in taps:
            tap_d[name] = nc.dram_tensor("tap_" + name, list(shape), dtype,
                                         kind="ExternalOutput")
            return tap_d[name]
        return None

    sb_small = {}
    for name, shape, dt in [
        ("onesb", [1, 128], BF16), ("onesf", [1, 128], F32),
        ("onescol", [128, 1], BF16),
        ("identb", [128, 128], BF16), ("identf", [128, 128], F32),
        ("mrotT", [128, 128], BF16), ("h128", [128, 128], BF16),
        ("h64", [64, 64], BF16), ("eps", [128, 1], F32),
        ("bdkv", [128, 4], F32), ("bdq", [128, 4], F32), ("bkr", [128, 4], F32),
        ("buk", [96, 16], F32), ("buq", [96, 16], F32), ("bqr", [32, 16], F32),
        ("bo", [128, 16], F32), ("suug", [128, 16], F32),
        ("wsu", [128, 64], F32), ("wsg", [128, 64], F32),
        ("sudp", [128, 64], F32), ("wdp", [128, 16], F32),
        ("buvb", [1, D], BF16),
        ("stat", [128, 16], F32),
    ]:
        sb_small[name] = nc.alloc_sbuf_tensor("s_" + name, shape, dt)

    ABF = Arena(nc, "ABF", 90624, BF16)
    AFP = Arena(nc, "AFP", 5632, F32)

    PP0 = nc.alloc_psum_tensor("pp0", [128, 1024], F32)
    PP1 = nc.alloc_psum_tensor("pp1", [128, 1024], F32)
    PS4 = nc.alloc_psum_tensor("ps4", [128, 512], F32)
    PS5 = nc.alloc_psum_tensor("ps5", [128, 512], F32)
    PS = [PP0[:, 0:512], PP0[:, 512:1024], PP1[:, 0:512], PP1[:, 512:1024],
          PS4[:, 0:512], PS5[:, 0:512]]
    PT = [nc.alloc_psum_tensor(f"pt{i}", [128, 1024], BF16) for i in range(2)]

    qL = Sem(nc, "qL")
    qC = Sem(nc, "qC")
    sP = Sem(nc, "sP")
    sA = Sem(nc, "sA")
    sV = Sem(nc, "sV")
    sG = Sem(nc, "sG")
    qB = Sem(nc, "qB")

    ALLS = [qL, qC, sP, sA, sV, sG]

    st = dict(phase="")
    if opts:
        st.update(opts)

    blk = nc.Block("main").__enter__()

    def EMIT(engine_name, fn):
        CUR_ENG[0] = engine_name
        TAG[0] = st.get("phase", "") + ":" + fn.__qualname__.split(".")[-1]
        getattr(blk, engine_name)(fn)
        CUR_ENG[0] = None

    def phase_fence():
        for en in ["sync", "gpsimd", "tensor", "vector", "scalar"]:
            def _f(e):
                for s in ALLS:
                    s.wait(e)
            EMIT(en, _f)

    def p0():
        def _sy(sy):
            for name in ["onesb", "onesf", "onescol", "identb", "identf",
                         "mrotT", "h128", "h64"]:
                qL.inc(sy.dma_start(sb_small[name][:], con[name].ap()), 16)
            for name in ["bdkv", "bdq", "bkr", "buk", "buq", "bqr", "bo",
                         "suug", "wsu", "wsg", "sudp", "wdp"]:
                qL.inc(sy.dma_start(sb_small[name][:], bia[name].ap()), 16)
        EMIT("sync", _sy)

        buv_f = AFP.v(0, D, 0, 1)
        def _sy2(sy):
            qL.inc(sy.dma_start(buv_f, bia["buv"].ap()), 16)
        EMIT("sync", _sy2)
        def _v(v):
            qL.wait(v)
            v.memset(sb_small["eps"][:], EPS)
            sV.inc(v.tensor_copy(sb_small["buvb"][:], buv_f))
        EMIT("vector", _v)


    def emit_ln(in_ap, out_ap, dump1, dump2, deps, out_war=()):
        stt = sb_small["stat"]
        mean, e2, nvar, rstd, nmr = (stt[:, 0:1], stt[:, 1:2], stt[:, 2:3],
                                     stt[:, 3:4], stt[:, 4:5])
        def _v(v):
            for s, m in deps:
                s.wait(v, m)
            sV.inc(v.tensor_scalar(dump1, in_ap, 1.0 / D, None,
                                   op0=ALU.mult, op1=ALU.add, accum_out=mean))
        EMIT("vector", _v)
        def _a(a):
            for s, m in deps:
                s.wait(a, m)
            sA.inc(a.activation(dump2, in_ap, AF.Square,
                                bias=0.0, scale=1.0 / math.sqrt(D),
                                accum_out=e2))
        EMIT("scalar", _a)
        mk_v, mk_a = sV.n, sA.n
        def _v2(v):
            sV.wait(v, mk_v)
            sA.wait(v, mk_a)
            sV.inc(v.tensor_scalar(nvar, mean, mean, e2,
                                   op0=ALU.mult, op1=ALU.subtract))
        EMIT("vector", _v2)
        mk_v2 = sV.n
        def _a2(a):
            sV.wait(a, mk_v2)
            sA.inc(a.activation(nvar, nvar, AF.Ln,
                                bias=sb_small["eps"][:], scale=-1.0))
        EMIT("scalar", _a2)
        def _a3(a):
            sA.inc(a.activation(rstd, nvar, AF.Exp, bias=0.0, scale=-0.5))
        EMIT("scalar", _a3)
        mk_a3 = sA.n
        def _v3(v):
            sA.wait(v, mk_a3)
            sV.inc(v.tensor_scalar(nmr, mean, rstd, -1.0,
                                   op0=ALU.mult, op1=ALU.mult))
        EMIT("vector", _v3)
        mk_v3 = sV.n
        def _a4(a):
            sV.wait(a, mk_v3)
            for s, m in out_war:
                s.wait(a, m)
            sA.inc(a.activation(out_ap, in_ap, AF.Identity,
                                bias=nmr, scale=rstd))
        EMIT("scalar", _a4)
        return sA.n

    O_K, O_CKV, O_WDKV, O_WDQ, O_WKR = 0, 32768, 40960, 49152, 57344
    O_XLN, O_DSQ, O_XFM, O_CQS, O_KRT, O_ROT = (65536, 67584, 69632, 77824,
                                                79872, 81920)
    O_COSK, O_SINK = 83968, 86016
    O_DSQ2 = 88064

    def Kv(p0, p1, off, n):
        return ABF.v(O_K + off, n, p0, p1)

    def p1a():
        def _g(g):
            for nm, off in [("Wdkv", O_WDKV), ("Wdq", O_WDQ), ("Wkr", O_WKR)]:
                src = w[nm].ap().rearrange("(ft p) c -> p ft c", p=128)
                qC.inc(g.dma_start(
                    ABF.v(off, 8192).rearrange("p (ft c) -> p ft c", c=512),
                    src), 16)
        EMIT("gpsimd", _g)
        def _sy(sy):
            qL.inc(sy.dma_start(ABF.v(O_COSK, 2048), con["cosk"].ap()), 16)
            qL.inc(sy.dma_start(ABF.v(O_SINK, 2048), con["sink"].ap()), 16)
        EMIT("sync", _sy)

        def _gc(g):
            for dc in range(16):
                src = w["Wo"].ap()[:, dc * 128:(dc + 1) * 128].rearrange(
                    "(h p) j -> p h j", p=128)
                qB.inc(g.dma_start(wo_bf.ap()[dc], src), 16)
            for dc in range(16):
                src = w["WdnT"].ap()[:, dc * 128:(dc + 1) * 128].rearrange(
                    "(a p) j -> p a j", p=128)
                qB.inc(g.dma_start(wdn_bf.ap()[dc], src), 16)
        EMIT("gpsimd", _gc)
        st["p0wo"] = 16 * 16
        st["p0done"] = qB.n

        for tg in range(4):
            for pt in range(4):
                tt = tg * 4 + pt
                xb = AFP.v((tt % 2) * 2048, 2048)
                war = st.get(f"xbw{tt % 2}", 0)
                def _sy2(sy, tt=tt, xb=xb, war=war):
                    sA.wait(sy, war)
                    qL.inc(sy.dma_start(xb, x_d.ap()[tt * 128:(tt + 1) * 128, :]), 16)
                EMIT("sync", _sy2)
                ln_deps = [(qL, qL.n), (qC, qC.n)]
                xln = ABF.v(O_XLN, 2048)
                mk_ln = emit_ln(xb, xln, ABF.v(O_DSQ, 2048),
                                ABF.v(O_DSQ2, 2048), ln_deps,
                                out_war=[(sP, st.get("xlnP", 0))])
                st[f"xbw{tt % 2}"] = mk_ln
                p0, v0 = sP.n, sV.n
                def _t(t, xln=xln, mk_ln=mk_ln, v0=v0):
                    sA.wait(t, mk_ln)
                    for ft in range(16):
                        sV.wait(t, v0 if ft < 2 else v0 + ft - 1)
                        sP.inc(t.transpose(PT[ft % 2][:, 0:128],
                                           xln[:, ft * 128:(ft + 1) * 128],
                                           sb_small["identb"][:]))
                EMIT("tensor", _t)
                st["xlnP"] = sP.n
                def _v(v, pt=pt, p0=p0):
                    for ft in range(16):
                        sP.wait(v, p0 + ft + 1)
                        sV.inc(v.tensor_copy(
                            ABF.v(O_XFM + ft * 512 + pt * 128, 128),
                            PT[ft % 2][:, 0:128]))
                EMIT("vector", _v)

            mk_xfm = sV.n
            for kind, woff, boff in [("ckv", O_WDKV, "bdkv"),
                                     ("cq", O_WDQ, "bdq"),
                                     ("kr", O_WKR, "bkr")]:
                for ct in range(4):
                    psm = PS[ct % 2]
                    war_ps = st.get(f"psw{ct % 2}", 0)
                    def _t(t, woff=woff, ct=ct, psm=psm, mk_xfm=mk_xfm,
                           war_ps=war_ps):
                        sV.wait(t, mk_xfm)
                        sA.wait(t, war_ps)
                        for ft in range(16):
                            sP.inc(t.matmul(
                                psm[:, 0:512],
                                ABF.v(woff + ft * 512 + ct * 128, 128),
                                ABF.v(O_XFM + ft * 512, 512),
                                start=(ft == 0), stop=(ft == 15)))
                    EMIT("tensor", _t)
                    mk_mm = sP.n
                    war_cq = st.get("cqspill", 0)
                    war_kr = st.get("ropeV", 0)
                    def _a(a, kind=kind, ct=ct, tg=tg, psm=psm, boff=boff,
                           mk_mm=mk_mm, war_cq=war_cq, war_kr=war_kr):
                        sP.wait(a, mk_mm)
                        if kind == "cq":
                            qL.wait(a, war_cq)
                            dst = ABF.v(O_CQS + ct * 512, 512)
                        elif kind == "kr":
                            sV.wait(a, war_kr)
                            dst = ABF.v(O_KRT + ct * 512, 512)
                        else:
                            dst = ABF.v(O_CKV + ct * 2048 + tg * 512, 512)
                        bap = sb_small[boff][:, ct:ct + 1]
                        sA.inc(a.activation(dst, psm[:, 0:512], AF.Identity,
                                            bias=bap, scale=1.0))
                        st[f"psw{ct % 2}"] = sA.n
                    EMIT("scalar", _a)
            mk_cq = sA.n
            def _sy3(sy, tg=tg, mk_cq=mk_cq):
                sA.wait(sy, mk_cq)
                qL.inc(sy.dma_start(
                    cq_scr.ap()[:, :, tg * 512:(tg + 1) * 512].rearrange(
                        "c p t -> p c t"),
                    ABF.v(O_CQS, 2048).rearrange("p (c t) -> p c t", t=512)), 16)
            EMIT("sync", _sy3)
            st["cqspill"] = qL.n

            war_rot = st.get("rotw", 0)
            a0 = sA.n
            def _t2(t, mk_cq=mk_cq, war_rot=war_rot, a0=a0):
                sA.wait(t, mk_cq)
                sA.wait(t, war_rot)
                for rt in range(4):
                    if rt >= 2:
                        sA.wait(t, a0 + rt - 1)
                    sP.inc(t.matmul(PS[2 + rt % 2][:, 0:512],
                                    sb_small["mrotT"][:],
                                    ABF.v(O_KRT + rt * 512, 512),
                                    start=True, stop=True))
            EMIT("tensor", _t2)
            mk_rot = sP.n
            war_ropeV = st.get("ropeV", 0)
            def _a5(a, mk_rot=mk_rot, war_ropeV=war_ropeV):
                sV.wait(a, war_ropeV)
                for rt in range(4):
                    sP.wait(a, mk_rot - 3 + rt)
                    sA.inc(a.activation(ABF.v(O_ROT + rt * 512, 512),
                                        PS[2 + rt % 2][:, 0:512], AF.Identity,
                                        bias=0.0, scale=1.0))
            EMIT("scalar", _a5)
            st["rotw"] = sA.n
            mk_rotc = sA.n
            def _v2(v, tg=tg, mk_rotc=mk_rotc):
                sA.wait(v, mk_rotc)
                for rt in range(4):
                    kr = ABF.v(O_KRT + rt * 512, 512)
                    ro = ABF.v(O_ROT + rt * 512, 512)
                    ck = ABF.v(O_COSK + tg * 512, 512)
                    sk = ABF.v(O_SINK + tg * 512, 512)
                    v.tensor_mul(kr, kr, ck)
                    v.tensor_mul(ro, ro, sk)
                    sV.inc(v.tensor_add(kr, kr, ro))
                for h in range(NH):
                    rt, j = h // 4, h % 4
                    src = ABF.v(O_KRT + rt * 512, 512, j * 32, (j + 1) * 32)
                    dst = Kv(96, 128, h * 2048 + tg * 512, 512)
                    sV.inc(v.tensor_copy(dst, src))
            EMIT("vector", _v2)
            st["ropeV"] = sV.n

        tp = tap("xfm", [128, 16 * 512], BF16)
        if tp is not None:
            def _syx(sy):
                sV.wait(sy)
                qL.inc(sy.dma_start(tp.ap(), ABF.v(O_XFM, 8192)), 16)
            EMIT("sync", _syx)
        tp = tap("ckv", [128, 4 * S], BF16)
        if tp is not None:
            def _syc(sy):
                sA.wait(sy)
                qL.inc(sy.dma_start(tp.ap(), ABF.v(O_CKV, 8192)), 16)
            EMIT("sync", _syc)
        tp = tap("krope", [128, NH * S], BF16)
        if tp is not None:
            def _syr(sy):
                sV.wait(sy)
                qL.inc(sy.dma_start(tp.ap(), ABF.v(O_K, 32768)), 16)
            EMIT("sync", _syr)

    O_V, O_WUK, O_WUV = 40960, 73728, 79872

    def p1b():
        def _g(g):
            src = w["Wuk"].ap().rearrange("(ct p) n -> p ct n", p=128)
            qC.inc(g.dma_start(
                ABF.v(O_WUK, 6144).rearrange("p (ct n) -> p ct n", n=1536),
                src), 16)
            src = w["Wuv"].ap().rearrange("(ct p) n -> p ct n", p=128)
            qC.inc(g.dma_start(
                ABF.v(O_WUV, 8192).rearrange("p (ct n) -> p ct n", n=2048),
                src), 16)
        EMIT("gpsimd", _g)
        mk_w = qC.n

        for h in range(NH):
            for kc in range(4):
                psm = PS[kc % 2]
                def _t(t, h=h, kc=kc, psm=psm):
                    qC.wait(t, mk_w)
                    sA.wait(t, st.get(f"psw{kc % 2}", 0))
                    for ct in range(4):
                        sP.inc(t.matmul(
                            psm[0:96, 0:512],
                            ABF.v(O_WUK + ct * 1536 + h * 96, 96),
                            ABF.v(O_CKV + ct * 2048 + kc * 512, 512),
                            start=(ct == 0), stop=(ct == 3)))
                EMIT("tensor", _t)
                mk_mm = sP.n
                def _a(a, h=h, kc=kc, psm=psm, mk_mm=mk_mm):
                    sP.wait(a, mk_mm)
                    bap = sb_small["buk"]
                    dst0 = Kv(0, 64, h * 2048 + kc * 512, 512)
                    dst1 = Kv(64, 96, h * 2048 + kc * 512, 512)
                    sA.inc(a.activation(dst0, psm[0:64, 0:512], AF.Identity,
                                        bias=bap[0:64, h:h + 1], scale=1.0))
                    sA.inc(a.activation(dst1, psm[64:96, 0:512], AF.Identity,
                                        bias=bap[64:96, h:h + 1], scale=1.0))
                    st[f"psw{kc % 2}"] = sA.n
                EMIT("scalar", _a)
        tp = tap("K", [128, NH * S], BF16)
        if tp is not None:
            def _syk(sy):
                sA.wait(sy)
                sV.wait(sy)
                qL.inc(sy.dma_start(tp.ap(), ABF.v(O_K, 32768)), 16)
            EMIT("sync", _syk)

        for kt in range(NT):
            for dc in range(4):
                psm = PS[2 + dc % 2]
                def _t(t, kt=kt, dc=dc, psm=psm):
                    qC.wait(t, mk_w)
                    sV.wait(t, st.get(f"vw{dc % 2}", 0))
                    for ct in range(4):
                        sP.inc(t.matmul(
                            psm[:, 0:512],
                            ABF.v(O_CKV + ct * 2048 + kt * 128, 128),
                            ABF.v(O_WUV + ct * 2048 + dc * 512, 512),
                            start=(ct == 0), stop=False))
                    sP.inc(t.matmul(psm[:, 0:512], sb_small["onesb"][:],
                                    sb_small["buvb"][:, dc * 512:(dc + 1) * 512],
                                    start=False, stop=True))
                EMIT("tensor", _t)
                mk_mm = sP.n
                def _v(v, kt=kt, dc=dc, psm=psm, mk_mm=mk_mm):
                    sP.wait(v, mk_mm)
                    dst = ABF.v(O_V + kt * 2048 + dc * 512, 512)
                    sV.inc(v.tensor_copy(dst, psm[:, 0:512]))
                    st[f"vw{dc % 2}"] = sV.n
                EMIT("vector", _v)
        tp = tap("V", [128, NT * D], BF16)
        if tp is not None:
            def _syv(sy):
                sV.wait(sy)
                qL.inc(sy.dma_start(tp.ap(), ABF.v(O_V, 32768)), 16)
            EMIT("sync", _syv)

    O_CQ, O_WUQ = 32768, 34816
    O_OST = 32768
    O_QALL = 73728
    O_PRJ = 73728
    O_PROBS = 81920
    O_EDMP = O_PROBS + 1024
    O_WQR = O_PROBS + 2048
    O_WOA, O_WOB = 86016, 88064
    O_COSQ, O_SINQ = 86016, 86528
    O_LND1, O_LND2 = 32768, 36864
    A_SCL, A_XRE, A_RSB, A_LROW, A_RROW = 0, 2048, 4096, 4608, 5120

    def p1c():
        n_qc = int(st.get("p1c_qcs", 4))
        n_heads = int(st.get("p1c_heads", NH))
        stage = st.get("p1c_stage", "full")

        def load_qc(qcn, war_pe):
            q0n = qcn * 512
            def _sy(sy, qcn=qcn, q0n=q0n, war_pe=war_pe):
                sP.wait(sy, war_pe)
                qL.inc(sy.dma_start(
                    ABF.v(O_CQ, 2048).rearrange("p (c t) -> p c t", t=512),
                    cq_scr.ap()[:, :, q0n:q0n + 512].rearrange(
                        "c p t -> p c t")), 16)
                qL.inc(sy.dma_start(ABF.v(O_COSQ, 512, 0, 32),
                                    con["cosk"].ap()[0:32, q0n:q0n + 512]), 16)
                qL.inc(sy.dma_start(ABF.v(O_SINQ, 512, 0, 32),
                                    con["sink"].ap()[0:32, q0n:q0n + 512]), 16)
            EMIT("sync", _sy)
            def _g(g, qcn=qcn, war_pe=war_pe):
                sP.wait(g, war_pe)
                src = w["Wuq"].ap().rearrange("(ct p) n -> p ct n", p=128)
                qC.inc(g.dma_start(
                    ABF.v(O_WUQ, 6144).rearrange("p (ct n) -> p ct n",
                                                 n=1536), src), 16)
                src = w["Wqr"].ap().rearrange("(ct p) n -> p ct n", p=128)
                qC.inc(g.dma_start(
                    ABF.v(O_WQR, 2048).rearrange("p (ct n) -> p ct n",
                                                 n=512), src), 16)
            EMIT("gpsimd", _g)
            st[f"ldmark{qcn}"] = (qL.n, qC.n)

        def emit_finish(fh, fmk_pv):
            def _fa6(a, fmk_pv=fmk_pv):
                sP.wait(a, fmk_pv)
                sV.wait(a, st.get("p2_rec", 0))
                sA.inc(a.activation(AFP.v(A_LROW, 512, 0, 1),
                                    PS5[0:1, 0:512], AF.Identity,
                                    bias=0.0, scale=1.0))
            EMIT("scalar", _fa6)
            mk_le = sA.n
            def _fv5(v, mk_le=mk_le):
                sA.wait(v, mk_le)
                sP.wait(v, st.get("p2_bc", 0))
                sV.inc(v.reciprocal(AFP.v(A_RROW, 512, 0, 1),
                                    AFP.v(A_LROW, 512, 0, 1)))
            EMIT("vector", _fv5)
            st["p2_rec"] = sV.n
            mk_rec = sV.n
            def _ft7(t, mk_rec=mk_rec, mk_le=mk_le):
                sV.wait(t, mk_rec)
                sA.wait(t, mk_le)
                sP.inc(t.matmul(PS5[:, 0:512], sb_small["onesf"][:],
                                AFP.v(A_RROW, 512, 0, 1),
                                start=True, stop=True))
            EMIT("tensor", _ft7)
            st["p2_bc"] = sP.n
            mk_rb = sP.n
            war_rsb = st.get("p2_ov", 0)
            def _fa5(a, mk_rb=mk_rb, war_rsb=war_rsb):
                sP.wait(a, mk_rb)
                sV.wait(a, war_rsb)
                sA.inc(a.activation(AFP.v(A_RSB, 512), PS5[:, 0:512],
                                    AF.Identity, bias=0.0, scale=1.0))
            EMIT("scalar", _fa5)
            st["reclps"] = sA.n
            mk_rsb = sA.n
            def _fv7(v, fh=fh, fmk_pv=fmk_pv, mk_rsb=mk_rsb):
                sP.wait(v, fmk_pv)
                sA.wait(v, mk_rsb)
                sV.inc(v.tensor_mul(ABF.v(O_OST + fh * 512, 512),
                                    PS[4], AFP.v(A_RSB, 512)))
            EMIT("vector", _fv7)
            st["p2_ov"] = sV.n

        load_qc(0, 0)
        st["fin"] = None
        for qc in range(n_qc):
            q0 = qc * 512
            mk_ld = st[f"ldmark{qc}"]

            for h in range(NH):
                pb, pr2 = PS[h % 2], PS[2 + h % 2]
                qrt = ABF.v(O_PROBS + (h % 2) * 512, 512, 0, 32)
                war_ps = st.get(f"qb_ps{h % 2}", 0)
                war_v = st.get(f"qb_vv{h % 2}", 0)
                def _t(t, h=h, pb=pb, pr2=pr2, war_ps=war_ps):
                    qL.wait(t, mk_ld[0])
                    qC.wait(t, mk_ld[1])
                    sA.wait(t, war_ps)
                    for ct in range(4):
                        sP.inc(t.matmul(
                            pb[0:96, 0:512],
                            ABF.v(O_WUQ + ct * 1536 + h * 96, 96),
                            ABF.v(O_CQ + ct * 512, 512),
                            start=(ct == 0), stop=(ct == 3)))
                    for ct in range(4):
                        sP.inc(t.matmul(
                            pr2[0:32, 0:512],
                            ABF.v(O_WQR + ct * 512 + h * 32, 32),
                            ABF.v(O_CQ + ct * 512, 512),
                            start=(ct == 0), stop=(ct == 3)))
                EMIT("tensor", _t)
                mk_qmm = sP.n
                def _a(a, h=h, pb=pb, pr2=pr2, mk_qmm=mk_qmm, war_v=war_v,
                       qrt=qrt):
                    sP.wait(a, mk_qmm)
                    sV.wait(a, war_v)
                    qL.wait(a, st.get("prjspill", 0))
                    sA.inc(a.activation(
                        ABF.v(O_QALL + h * 512, 512, 0, 64), pb[0:64, 0:512],
                        AF.Identity, bias=sb_small["buq"][0:64, h:h + 1],
                        scale=1.0))
                    sA.inc(a.activation(
                        ABF.v(O_QALL + h * 512, 512, 64, 96), pb[64:96, 0:512],
                        AF.Identity, bias=sb_small["buq"][64:96, h:h + 1],
                        scale=1.0))
                    sA.inc(a.activation(
                        qrt, pr2[0:32, 0:512],
                        AF.Identity, bias=sb_small["bqr"][:, h:h + 1], scale=1.0))
                EMIT("scalar", _a)
                mk_qev = sA.n
                def _t2(t, pr2=pr2, mk_qev=mk_qev, qrt=qrt):
                    sA.wait(t, mk_qev)
                    sP.inc(t.matmul(pr2[0:32, 0:512],
                                    sb_small["mrotT"][0:32, 0:32],
                                    qrt, start=True, stop=True))
                EMIT("tensor", _t2)
                mk_rmm = sP.n
                def _a2(a, pr2=pr2, mk_rmm=mk_rmm, h=h):
                    sP.wait(a, mk_rmm)
                    sV.wait(a, st.get(f"qb_vv{h % 2}", 0))
                    sA.inc(a.activation(ABF.v(O_EDMP + (h % 2) * 512, 512, 0, 32),
                                        pr2[0:32, 0:512], AF.Identity,
                                        bias=0.0, scale=1.0))
                EMIT("scalar", _a2)
                st[f"qb_ps{h % 2}"] = sA.n
                mk_rev = sA.n
                def _v(v, h=h, mk_rev=mk_rev, qrt=qrt):
                    sA.wait(v, mk_rev)
                    ro = ABF.v(O_EDMP + (h % 2) * 512, 512, 0, 32)
                    v.tensor_mul(qrt, qrt, ABF.v(O_COSQ, 512, 0, 32))
                    v.tensor_mul(ro, ro, ABF.v(O_SINQ, 512, 0, 32))
                    sV.inc(v.tensor_add(ABF.v(O_QALL + h * 512, 512, 96, 128),
                                        qrt, ro))
                EMIT("vector", _v)
                st[f"qb_vv{h % 2}"] = sV.n
                st["qb_v"] = sV.n

            if stage == "qb":
                tp = tap_d.get("Qall")
                if tp is not None and qc == int(st.get("tap_qc", 0)):
                    def _syq(sy):
                        sV.wait(sy)
                        sA.wait(sy)
                        qL.inc(sy.dma_start(tp.ap(), ABF.v(O_QALL, 8192)), 16)
                    EMIT("sync", _syq)
                continue

            for h in range(n_heads):
                mk_qb = st["qb_v"]
                p2start = sP.n
                SP2 = (PP0, PP1)
                def pvl(t, j, h, war_ov, war_rcl):
                    if j == 0:
                        sV.wait(t, war_ov)
                        sA.wait(t, war_rcl)
                    po = O_PROBS + ((j // 2) % 3) * 1024 + (j % 2) * 512
                    sP.inc(t.matmul(
                        PS[4], ABF.v(O_V + j * 2048 + h * 128, 128),
                        ABF.v(po, 512), start=(j == 0), stop=(j == 15)))
                    sP.inc(t.matmul(
                        PS5[0:1, 0:512], sb_small["onescol"][:],
                        ABF.v(po, 512), start=(j == 0), stop=(j == 15),
                        skip_group_check=True))
                def _t5a(t, h=h, mk_qb=mk_qb):
                    sV.wait(t, mk_qb)
                    for p in range(2):
                        for s in range(2):
                            sP.inc(t.matmul(
                                SP2[p % 2][:, s * 512:(s + 1) * 512],
                                Kv(0, 128, h * 2048 + (2 * p + s) * 128, 128),
                                ABF.v(O_QALL + h * 512, 512),
                                start=True, stop=True))
                EMIT("tensor", _t5a)
                if st.get("fin") is not None:
                    emit_finish(*st["fin"])
                    st["fin"] = None
                war_ov = st.get("p2_ov", 0)
                war_rcl = st.get("reclps", 0)
                p2b = sP.n
                a0 = sA.n
                def _t5b(t, h=h, a0=a0, war_ov=war_ov, war_rcl=war_rcl):
                    for p in range(2, 8):
                        sA.wait(t, a0 + p - 1)
                        for s in range(2):
                            sP.inc(t.matmul(
                                SP2[p % 2][:, s * 512:(s + 1) * 512],
                                Kv(0, 128, h * 2048 + (2 * p + s) * 128, 128),
                                ABF.v(O_QALL + h * 512, 512),
                                start=True, stop=True))
                        pvl(t, 2 * p - 4, h, war_ov, war_rcl)
                        pvl(t, 2 * p - 3, h, war_ov, war_rcl)
                    sA.wait(t, a0 + 7)
                    pvl(t, 12, h, war_ov, war_rcl)
                    pvl(t, 13, h, war_ov, war_rcl)
                    sA.wait(t, a0 + 8)
                    pvl(t, 14, h, war_ov, war_rcl)
                    pvl(t, 15, h, war_ov, war_rcl)
                EMIT("tensor", _t5b)
                mk_pv = sP.n
                def _a4(a, a0=a0, p2start=p2start, p2b=p2b):
                    for p in range(8):
                        if p < 2:
                            sP.wait(a, p2start + 2 * (p + 1))
                        else:
                            sP.wait(a, p2b + 6 * p - 10)
                        sA.inc(a.activation(
                            ABF.v(O_PROBS + (p % 3) * 1024, 1024),
                            SP2[p % 2][:, 0:1024], AF.Exp,
                            bias=0.0, scale=CSC))
                EMIT("scalar", _a4)
                st["fin"] = (h, mk_pv)
            if st.get("fin") is not None:
                emit_finish(*st["fin"])
                st["fin"] = None

            tp = tap_d.get("Ost")
            if tp is not None and qc == int(st.get("tap_qc", 0)):
                def _syo(sy):
                    sV.wait(sy)
                    qL.inc(sy.dma_start(tp.ap(), ABF.v(O_OST, 8192)), 16)
                EMIT("sync", _syo)
                st["ost_tapped"] = qL.n
            if stage == "p2":
                continue

            mk_allO = st["p2_ov"]
            for dc in range(16):
                wob = O_WOA if dc % 2 == 0 else O_WOB
                war_wo = st.get(f"wow{dc % 2}", 0)
                def _sy2(sy, dc=dc, wob=wob, war_wo=war_wo, mk_allO=mk_allO):
                    sP.wait(sy, war_wo)
                    sV.wait(sy, mk_allO)
                    qB.wait(sy, st.get("p0wo", 0))
                    qL.inc(sy.dma_start(
                        ABF.v(wob, 2048),
                        wo_bf.ap()[dc].rearrange("p h j -> p (h j)")), 16)
                EMIT("sync", _sy2)
                mk_wol = qL.n
                wps = PS[4] if dc % 2 == 0 else PS[5]
                war_ps3 = st.get(f"projps{dc % 2}", 0)
                def _t8(t, dc=dc, wob=wob, wps=wps, mk_wol=mk_wol,
                        war_ps3=war_ps3, mk_allO=mk_allO):
                    qL.wait(t, mk_wol)
                    sV.wait(t, mk_allO)
                    sA.wait(t, war_ps3)
                    for h in range(NH):
                        sP.inc(t.matmul(wps[:, 0:512],
                                        ABF.v(wob + h * 128, 128),
                                        ABF.v(O_OST + h * 512, 512),
                                        start=(h == 0), stop=(h == 15)))
                    st[f"wow{dc % 2}"] = sP.n
                EMIT("tensor", _t8)
                mk_wmm = sP.n
                def _a6(a, dc=dc, wps=wps, mk_wmm=mk_wmm):
                    sP.wait(a, mk_wmm)
                    sV.wait(a, st.get(f"projtmp{dc % 2}", 0))
                    sA.inc(a.activation(ABF.v(O_PROBS + (dc % 2) * 512, 512),
                                        wps[:, 0:512],
                                        AF.Identity,
                                        bias=sb_small["bo"][:, dc:dc + 1],
                                        scale=1.0))
                EMIT("scalar", _a6)
                st[f"projps{dc % 2}"] = sA.n
                mk_pev = sA.n
                p0t, v0t = sP.n, sV.n
                def _t9(t, dc=dc, mk_pev=mk_pev, v0t=v0t):
                    sA.wait(t, mk_pev)
                    for ts in range(4):
                        sV.wait(t, v0t if ts < 2 else v0t + ts - 1)
                        sP.inc(t.transpose(
                            PT[ts % 2][:, 0:128],
                            ABF.v(O_PROBS + (dc % 2) * 512 + ts * 128, 128),
                            sb_small["identb"][:]))
                EMIT("tensor", _t9)
                def _v8(v, dc=dc, p0t=p0t):
                    for ts in range(4):
                        sP.wait(v, p0t + ts + 1)
                        sV.inc(v.tensor_copy(
                            ABF.v(O_PRJ + ts * 2048 + dc * 128, 128),
                            PT[ts % 2][:, 0:128]))
                    st[f"projtmp{dc % 2}"] = sV.n
                EMIT("vector", _v8)
                if dc == 15 and qc + 1 < n_qc:
                    load_qc(qc + 1, sP.n)

            mk_prj = sV.n
            def _sy3(sy, q0=q0, mk_prj=mk_prj):
                sV.wait(sy, mk_prj)
                qL.wait(sy, st.get("ost_tapped", 0))
                for ts in range(4):
                    qL.inc(sy.dma_start(
                        proj_scr.ap()[q0 + ts * 128:q0 + (ts + 1) * 128, :],
                        ABF.v(O_PRJ + ts * 2048, 2048)), 16)
            EMIT("sync", _sy3)
            st["prjspill"] = qL.n
            st["p1c_qread"] = sP.n

    tap("x1", [S, D], F32)
    tap("Qall", [128, 8192], BF16)
    tap("Ost", [128, 8192], BF16)

    F_XFM2, F_XHAD, F_HBUF, F_XLN, F_DMP1, F_DMP2 = (0, 32768, 65536, 69632,
                                                     71680, 73728)
    F_XLN2, F_LNA, F_PRJT = 75776, 77824, 79872

    def f1():
        for tt in range(NT):
            xb = AFP.v((tt % 2) * 2048, 2048)
            prjt = ABF.v(F_PRJT + (tt % 2) * 2048, 2048)
            war = st.get(f"f1xb{tt % 2}", 0)
            war_p = st.get(f"f1pj{tt % 2}", 0)
            def _sy(sy, tt=tt, xb=xb, prjt=prjt, war=war, war_p=war_p):
                sA.wait(sy, war)
                sA.wait(sy, war_p)
                qL.inc(sy.dma_start(xb, x_d.ap()[tt * 128:(tt + 1) * 128, :]),
                       16)
                qL.inc(sy.dma_start(
                    prjt, proj_scr.ap()[tt * 128:(tt + 1) * 128, :]), 16)
            EMIT("sync", _sy)
            ln_deps = [(qL, qL.n)]
            mk_lna = emit_ln(prjt, ABF.v(F_LNA, 2048), ABF.v(F_DMP1, 2048),
                             ABF.v(F_DMP2, 2048), ln_deps,
                             out_war=[(sV, st.get("f1add", 0))])
            st[f"f1pj{tt % 2}"] = mk_lna
            def _vadd(v, xb=xb, mk_lna=mk_lna):
                sA.wait(v, mk_lna)
                qL.wait(v)
                sV.inc(v.tensor_add(xb, xb, ABF.v(F_LNA, 2048)))
            EMIT("vector", _vadd)
            st["f1add"] = sV.n
            mk_x1 = sV.n
            def _syx1(sy, tt=tt, xb=xb, mk_x1=mk_x1):
                sV.wait(sy, mk_x1)
                qL.inc(sy.dma_start(x1_scr.ap()[tt * 128:(tt + 1) * 128, :],
                                    xb), 16)
                if "x1" in taps:
                    qL.inc(sy.dma_start(
                        tap_d["x1"].ap()[tt * 128:(tt + 1) * 128, :], xb), 16)
            EMIT("sync", _syx1)
            xln = ABF.v(F_XLN if tt % 2 == 0 else F_XLN2, 2048)
            mk_ln = emit_ln(xb, xln, ABF.v(F_DMP1, 2048), ABF.v(F_DMP2, 2048),
                            [(sV, mk_x1)],
                            out_war=[(sP, st.get(f"f1xlnP{tt % 2}", 0))])
            st[f"f1xb{tt % 2}"] = mk_ln
            p0t, v0t = sP.n, sV.n
            def _t(t, xln=xln, mk_ln=mk_ln, v0t=v0t):
                sA.wait(t, mk_ln)
                for ft in range(16):
                    sV.wait(t, v0t if ft < 2 else v0t + ft - 1)
                    sP.inc(t.transpose(PT[ft % 2][:, 0:128],
                                       xln[:, ft * 128:(ft + 1) * 128],
                                       sb_small["identb"][:]))
            EMIT("tensor", _t)
            st[f"f1xlnP{tt % 2}"] = sP.n
            def _v(v, tt=tt, p0t=p0t):
                for ft in range(16):
                    sP.wait(v, p0t + ft + 1)
                    sV.inc(v.tensor_scalar(
                        ABF.v(F_XFM2 + ft * 2048 + tt * 128, 128),
                        PT[ft % 2][:, 0:128],
                        sb_small["suug"][:, ft:ft + 1], None, op0=ALU.mult))
            EMIT("vector", _v)
        mk_xfm2 = sV.n

        for fpc in range(16):
            hb = ABF.v(F_HBUF + (fpc % 2) * 2048, 2048)
            war = st.get(f"f1hb{fpc % 2}", 0)
            def _sy2(sy, fpc=fpc, hb=hb, war=war):
                sP.wait(sy, war)
                qL.inc(sy.dma_start(
                    hb.rearrange("p (ft j) -> p ft j", j=128),
                    con["h2048t"].ap()[fpc]), 16)
            EMIT("sync", _sy2)
            mk_h = qL.n
            for tc in range(4):
                psm = PS[tc % 2]
                war_ps = st.get(f"f1ps{tc % 2}", 0)
                def _t2(t, fpc=fpc, tc=tc, hb=hb, psm=psm, mk_h=mk_h,
                        war_ps=war_ps, mk_xfm2=mk_xfm2):
                    qL.wait(t, mk_h)
                    sV.wait(t, mk_xfm2)
                    sA.wait(t, war_ps)
                    for ft in range(16):
                        sP.inc(t.matmul(
                            psm[:, 0:512],
                            hb[:, ft * 128:(ft + 1) * 128],
                            ABF.v(F_XFM2 + ft * 2048 + tc * 512, 512),
                            start=(ft == 0), stop=(ft == 15)))
                EMIT("tensor", _t2)
                st[f"f1hb{fpc % 2}"] = sP.n
                mk_mm = sP.n
                def _a(a, fpc=fpc, tc=tc, psm=psm, mk_mm=mk_mm):
                    sP.wait(a, mk_mm)
                    sA.inc(a.activation(
                        ABF.v(F_XHAD + fpc * 2048 + tc * 512, 512),
                        psm[:, 0:512], AF.Identity, bias=0.0,
                        scale=1.0 / math.sqrt(2048.0)))
                    st[f"f1ps{tc % 2}"] = sA.n
                EMIT("scalar", _a)
        tp = tap_d.get("xhad")
        if tp is not None:
            def _syx(sy):
                sA.wait(sy)
                qL.inc(sy.dma_start(tp.ap(), ABF.v(F_XHAD, 32768)), 16)
            EMIT("sync", _syx)

    F_WUP, F_WG, F_USB, F_GSB, F_YSB, F_ZSB = (0, 4096, 8192, 9216, 10240, 11264)

    def f2():
        def emit_h128(Ap, tcp):
            it = Ap * 4 + tcp
            mk_y = st["f2y"]
            def _t2(t, it=it, mk_y=mk_y):
                sV.wait(t, mk_y)
                sA.wait(t, st.get("f2zev", 0))
                sP.inc(t.matmul(PS[4], sb_small["h128"][:],
                                ABF.v(F_YSB + (it % 2) * 512, 512),
                                start=True, stop=True))
            EMIT("tensor", _t2)
            st["f2hps"] = sP.n
            mk_z = sP.n
            zsb = ABF.v(F_ZSB + (it % 2) * 512, 512)
            war_z = st.get(f"f2z{it % 2}", 0)
            def _a2(a, mk_z=mk_z, zsb=zsb, war_z=war_z):
                sP.wait(a, mk_z)
                qL.wait(a, war_z)
                sA.inc(a.activation(zsb, PS[4], AF.Identity,
                                    bias=0.0, scale=1.0 / math.sqrt(128.0)))
            EMIT("scalar", _a2)
            st["f2zev"] = sA.n
            mk_ze = sA.n
            def _sy(sy, Ap=Ap, tcp=tcp, zsb=zsb, mk_ze=mk_ze):
                sA.wait(sy, mk_ze)
                qL.inc(sy.dma_start(
                    z_scr.ap()[Ap, :, tcp * 512:(tcp + 1) * 512], zsb), 16)
            EMIT("sync", _sy)
            st[f"f2z{it % 2}"] = qL.n

        for A in range(64):
            wub = ABF.v(F_WUP + (A % 2) * 2048, 2048)
            wgb = ABF.v(F_WG + (A % 2) * 2048, 2048)
            war = st.get(f"f2w{A % 2}", 0)
            def _g(g, A=A, wub=wub, wgb=wgb, war=war):
                sP.wait(g, war)
                src = w["WupT"].ap()[:, A * 128:(A + 1) * 128].rearrange(
                    "(ft p) i -> p ft i", p=128)
                qC.inc(g.dma_start(
                    wub.rearrange("p (ft i) -> p ft i", i=128), src), 16)
                src = w["WgT"].ap()[:, A * 128:(A + 1) * 128].rearrange(
                    "(ft p) i -> p ft i", p=128)
                qC.inc(g.dma_start(
                    wgb.rearrange("p (ft i) -> p ft i", i=128), src), 16)
            EMIT("gpsimd", _g)
            mk_w = qC.n
            for tc in range(4):
                it = A * 4 + tc
                bu, bg = PS[2 * (it % 2)], PS[2 * (it % 2) + 1]
                war_u = st.get(f"f2ev{it % 2}", 0)
                def _t(t, tc=tc, wub=wub, wgb=wgb, bu=bu, bg=bg, mk_w=mk_w,
                       war_u=war_u):
                    qC.wait(t, mk_w)
                    sA.wait(t, war_u)
                    for ft in range(16):
                        sP.inc(t.matmul(bu[:, 0:512],
                                        wub[:, ft * 128:(ft + 1) * 128],
                                        ABF.v(F_XHAD + ft * 2048 + tc * 512, 512),
                                        start=(ft == 0), stop=(ft == 15)))
                    for ft in range(16):
                        sP.inc(t.matmul(bg[:, 0:512],
                                        wgb[:, ft * 128:(ft + 1) * 128],
                                        ABF.v(F_XHAD + ft * 2048 + tc * 512, 512),
                                        start=(ft == 0), stop=(ft == 15)))
                EMIT("tensor", _t)
                st[f"f2w{A % 2}"] = sP.n
                mk_ug = sP.n
                if it > 0:
                    emit_h128((it - 1) // 4, (it - 1) % 4)
                war_y = st.get(f"f2yw{it % 2}", 0)
                def _a(a, A=A, it=it, bu=bu, bg=bg, mk_ug=mk_ug, war_y=war_y):
                    sP.wait(a, mk_ug - 16)
                    sV.wait(a, war_y)
                    sA.inc(a.activation(ABF.v(F_USB + (it % 2) * 512, 512),
                                        bu[:, 0:512],
                                        AF.Identity, bias=0.0,
                                        scale=sb_small["wsu"][:, A:A + 1]))
                    sP.wait(a, mk_ug)
                    sA.inc(a.activation(ABF.v(F_GSB + (it % 2) * 512, 512),
                                        bg[:, 0:512],
                                        AF.Silu, bias=0.0,
                                        scale=sb_small["wsg"][:, A:A + 1]))
                EMIT("scalar", _a)
                st[f"f2ev{it % 2}"] = sA.n
                mk_ug2 = sA.n
                def _v(v, A=A, it=it, mk_ug2=mk_ug2):
                    sA.wait(v, mk_ug2)
                    sP.wait(v, st.get("f2hps", 0))
                    sV.inc(v.scalar_tensor_tensor(
                        ABF.v(F_YSB + (it % 2) * 512, 512),
                        ABF.v(F_USB + (it % 2) * 512, 512),
                        sb_small["sudp"][:, A:A + 1],
                        ABF.v(F_GSB + (it % 2) * 512, 512),
                        op0=ALU.mult, op1=ALU.mult))
                EMIT("vector", _v)
                st[f"f2yw{it % 2}"] = sV.n
                st["f2y"] = sV.n
        emit_h128(63, 3)

    F_ZLD, F_YST = 0, 32768

    def f3a():
        for bg in range(4):
            for tc in range(4):
                it = bg * 4 + tc
                zb = ABF.v(F_ZLD + (it % 2) * 16384, 16384)
                war = st.get(f"f3az{it % 2}", 0)
                def _sy(sy, bg=bg, tc=tc, zb=zb, war=war):
                    sP.wait(sy, war)
                    qL.inc(sy.dma_start(
                        zb.rearrange("p (b t) -> p b t", t=512)[0:64],
                        z_scr.ap()[:, bg * 32:(bg + 1) * 32,
                                   tc * 512:(tc + 1) * 512]), 16)
                EMIT("sync", _sy)
                mk_zl = qL.n
                p0f = sP.n
                a0f, v0f = sA.n, sV.n
                war_st = st.get("f3stg", 0)
                def _t(t, zb=zb, mk_zl=mk_zl, a0f=a0f, v0f=v0f):
                    qL.wait(t, mk_zl)
                    zv = zb.rearrange("p (b t) -> p b t", t=512)
                    for b2 in range(32):
                        if b2 >= 4:
                            if (b2 - 4) % 2 == 0:
                                sA.wait(t, a0f + (b2 - 4) // 2 + 1)
                            else:
                                sV.wait(t, v0f + (b2 - 4) // 2 + 1)
                        sP.inc(t.matmul(PS[b2 % 4][0:64, 0:512],
                                        sb_small["h64"][:], zv[0:64, b2],
                                        start=True, stop=True))
                EMIT("tensor", _t)
                def _a(a, p0f=p0f, war_st=war_st):
                    qL.wait(a, war_st)
                    for b2 in range(0, 32, 2):
                        sP.wait(a, p0f + b2 + 1)
                        sA.inc(a.activation(ABF.v(F_YST + b2 * 512, 512, 0, 64),
                                            PS[b2 % 4][0:64, 0:512], AF.Identity,
                                            bias=0.0, scale=0.125))
                EMIT("scalar", _a)
                def _v(v, p0f=p0f, war_st=war_st):
                    qL.wait(v, war_st)
                    for b2 in range(1, 32, 2):
                        sP.wait(v, p0f + b2 + 1)
                        sV.inc(v.tensor_scalar(
                            ABF.v(F_YST + b2 * 512, 512, 0, 64),
                            PS[b2 % 4][0:64, 0:512], 0.125, None, op0=ALU.mult))
                EMIT("vector", _v)
                mk_ev, mk_vv = sA.n, sV.n
                def _sy2(sy, bg=bg, tc=tc, mk_ev=mk_ev, mk_vv=mk_vv):
                    sA.wait(sy, mk_ev)
                    sV.wait(sy, mk_vv)
                    qL.inc(sy.dma_start(
                        yt_scr.ap()[:, bg * 32:(bg + 1) * 32,
                                    tc * 512:(tc + 1) * 512],
                        ABF.v(F_YST, 16384).rearrange(
                            "p (b t) -> p b t", t=512)[0:64]), 16)
                EMIT("sync", _sy2)
                st["f3stg"] = qL.n

    F_YT, F_WDN, F_FFM, F_FTM, F_FBF, F_LD1, F_LD2 = (0, 32768, 49152, 57344,
                                                      65536, 66048, 68096)

    def f3b():
        for tc in range(4):
            war = st.get("f3byt", 0)
            def _sy(sy, tc=tc, war=war):
                sP.wait(sy, war)
                qL.inc(sy.dma_start(
                    ABF.v(F_YT, 32768).rearrange("p (a t) -> p a t", t=512),
                    yt_scr.ap()[:, :, tc * 512:(tc + 1) * 512].rearrange(
                        "a b t -> b a t")), 16)
            EMIT("sync", _sy)
            mk_yt = qL.n
            for dc in range(16):
                it = tc * 16 + dc
                wdb = ABF.v(F_WDN + (it % 2) * 8192, 8192)
                war_w = st.get(f"f3bw{it % 2}", 0)
                def _sy2(sy, dc=dc, wdb=wdb, war_w=war_w):
                    sP.wait(sy, war_w)
                    qB.wait(sy, st.get("p0done", 0))
                    qL.inc(sy.dma_start(
                        wdb.rearrange("p (a j) -> p a j", j=128),
                        wdn_bf.ap()[dc]), 16)
                EMIT("sync", _sy2)
                mk_wl = qL.n
                psm = PS[dc % 2]
                war_ps = st.get(f"f3bps{dc % 2}", 0)
                def _t(t, dc=dc, wdb=wdb, psm=psm, mk_wl=mk_wl, mk_yt=mk_yt,
                       war_ps=war_ps):
                    qL.wait(t, mk_wl)
                    sA.wait(t, war_ps)
                    for a2 in range(64):
                        sP.inc(t.matmul(psm[:, 0:512],
                                        wdb[:, a2 * 128:(a2 + 1) * 128],
                                        ABF.v(F_YT + a2 * 512, 512),
                                        start=(a2 == 0), stop=(a2 == 63)))
                    st[f"f3bw{it % 2}"] = sP.n
                EMIT("tensor", _t)
                mk_mm = sP.n
                def _a(a, dc=dc, psm=psm, mk_mm=mk_mm):
                    sP.wait(a, mk_mm)
                    sA.inc(a.activation(ABF.v(F_FFM + dc * 512, 512),
                                        psm[:, 0:512], AF.Identity, bias=0.0,
                                        scale=sb_small["wdp"][:, dc:dc + 1]))
                    st[f"f3bps{dc % 2}"] = sA.n
                EMIT("scalar", _a)
                mk_ev = sA.n
                p0t, v0t = sP.n, sV.n
                def _t2(t, dc=dc, mk_ev=mk_ev, v0t=v0t):
                    sA.wait(t, mk_ev)
                    for ts in range(4):
                        sV.wait(t, v0t if ts < 2 else v0t + ts - 1)
                        sP.inc(t.transpose(
                            PT[ts % 2][:, 0:128],
                            ABF.v(F_FFM + dc * 512 + ts * 128, 128),
                            sb_small["identb"][:]))
                EMIT("tensor", _t2)
                def _v(v, dc=dc, p0t=p0t):
                    sA.wait(v, st.get("f3bln", 0))
                    for ts in range(4):
                        sP.wait(v, p0t + ts + 1)
                        sV.inc(v.tensor_copy(
                            ABF.v(F_FTM + ts * 2048 + dc * 128, 128),
                            PT[ts % 2][:, 0:128]))
                EMIT("vector", _v)
            mk_ftm = sV.n
            for ts in range(4):
                r0 = tc * 512 + ts * 128
                def _sy3(sy, r0=r0):
                    qL.wait(sy, st.get("f3bst", 0))
                    sV.wait(sy, st.get("f3badd", 0))
                    qL.inc(sy.dma_start(AFP.v(2048, 2048),
                                        x1_scr.ap()[r0:r0 + 128, :]), 16)
                EMIT("sync", _sy3)
                deps = [(sV, mk_ftm), (qL, qL.n)]
                mk_ln = emit_ln(ABF.v(F_FTM + ts * 2048, 2048),
                                AFP.v(0, 2048), ABF.v(F_LD1, 2048),
                                ABF.v(F_LD2, 2048), deps,
                                out_war=[(sV, st.get("f3badd", 0))])
                def _v2(v, mk_ln=mk_ln):
                    sA.wait(v, mk_ln)
                    qL.wait(v)
                    sV.inc(v.tensor_add(AFP.v(2048, 2048), AFP.v(2048, 2048),
                                        AFP.v(0, 2048)))
                EMIT("vector", _v2)
                st["f3badd"] = sV.n
                mk_x2 = sV.n
                def _sy4(sy, r0=r0, mk_x2=mk_x2):
                    sV.wait(sy, mk_x2)
                    qL.inc(sy.dma_start(out_d.ap()[r0:r0 + 128, :],
                                        AFP.v(2048, 2048)), 16)
                EMIT("sync", _sy4)
                st["f3bst"] = qL.n
                st["f3bln"] = mk_ln
            st["f3byt"] = sP.n

    tap("xhad", [128, 32768], BF16)

    phases = [("p0", p0), ("p1a", p1a), ("p1b", p1b), ("p1c", p1c),
              ("f1", f1), ("f2", f2), ("f3a", f3a), ("f3b", f3b)]
    PHASE_MARKS.clear()
    for name, fn in phases:
        st["phase"] = name
        fn()
        phase_fence()
        PHASE_MARKS[name] = nc.n_instructions()
        if stop_after == name:
            break

    def _fin(sy):
        for s in ALLS:
            s.wait(sy)
        qB.wait(sy)
    EMIT("sync", _fin)
    blk.__exit__(None, None, None)
    check_deadlock()
    return nc, tap_d



def _prep_inputs(inputs):
    g = lambda k: np.asarray(inputs[k])
    hc = _host_consts()
    per_core_common = {
        "Wdkv": g("L0_attn_W_dkv"), "Wdq": g("L0_attn_W_dq"),
        "Wuk": g("L0_attn_W_uk"), "Wuv": g("L0_attn_W_uv"),
        "Wuq": g("L0_attn_W_uq"), "Wqr": g("L0_attn_W_qr"),
        "Wkr": g("L0_attn_W_kr"), "Wo": g("L0_attn_W_o"),
        "WupT": np.ascontiguousarray(g("L0_ffn_W_up").T),
        "WgT": np.ascontiguousarray(g("L0_ffn_W_gate").T),
        "WdnT": np.ascontiguousarray(g("L0_ffn_W_down").T),
        "bdkv": np.ascontiguousarray(g("L0_attn_b_dkv").reshape(4, 128).T),
        "bdq": np.ascontiguousarray(g("L0_attn_b_dq").reshape(4, 128).T),
        "bkr": np.ascontiguousarray(g("L0_attn_b_kr").reshape(4, 128).T),
        "buk": np.ascontiguousarray(g("L0_attn_b_uk").reshape(16, 96).T),
        "buq": np.ascontiguousarray(g("L0_attn_b_uq").reshape(16, 96).T),
        "bqr": np.ascontiguousarray(g("L0_attn_b_qr").reshape(16, 32).T),
        "buv": np.ascontiguousarray(g("L0_attn_b_uv").reshape(1, 2048)),
        "bo": np.ascontiguousarray(g("L0_attn_b_o").reshape(16, 128).T),
        "suug": np.ascontiguousarray(g("L0_ffn_SU_ug").reshape(16, 128).T),
        "wsu": np.ascontiguousarray(
            g("L0_ffn_Wscale_ug")[:INTER].reshape(64, 128).T),
        "wsg": np.ascontiguousarray(
            g("L0_ffn_Wscale_ug")[INTER:].reshape(64, 128).T),
        "sudp": np.ascontiguousarray(g("L0_ffn_SU_dp").reshape(64, 128).T),
        "wdp": np.ascontiguousarray(g("L0_ffn_Wscale_dp").reshape(16, 128).T),
    }
    for k in ["h128", "h64", "h2048t", "mrotT", "cosk", "sink",
              "onesb", "onesf", "onescol", "identb", "identf"]:
        per_core_common[k] = hc[k]
    x = np.asarray(inputs["L0_attn_x"])
    in_maps = []
    for c in range(8):
        m = dict(per_core_common)
        m["x"] = np.ascontiguousarray(x[c])
        in_maps.append(m)
    return in_maps


def kernel(**inputs):
    nc, _ = build_program()
    in_maps = _prep_inputs(inputs)
    res = run_bass_kernel_spmd(nc, in_maps, core_ids=list(range(8)))
    return np.stack([np.asarray(r["out"], dtype=np.float32)
                     for r in res.results], axis=0)


# version 08
# speedup vs baseline: 1.0461x; 1.0461x over previous
import math
import numpy as np
import ml_dtypes
import concourse.bass as bass
import concourse.mybir as mybir
from concourse.bass_utils import run_bass_kernel_spmd

F32 = mybir.dt.float32
BF16 = mybir.dt.bfloat16
AF = mybir.ActivationFunctionType
ALU = mybir.AluOpType
AX = mybir.AxisListType
BF = ml_dtypes.bfloat16

S = 2048
D = 2048
NH = 16
DKV = 512
DR = 32
DH = 128
SP = 96
INTER = 8192
NT = 16
EPS = 1e-5
CSC = 1.0 / math.sqrt(128.0)
RSC = math.sqrt(128.0)



def _hadamard(n):
    h = np.array([[1.0]], dtype=np.float64)
    while h.shape[0] < n:
        h = np.block([[h, h], [h, -h]])
    return h.astype(np.float32)


def _host_consts():
    c = {}
    c["h128"] = _hadamard(128).astype(BF)
    c["h64"] = _hadamard(64).astype(BF)
    H = _hadamard(2048)
    c["h2048t"] = np.ascontiguousarray(
        H.reshape(16, 128, 16, 128).transpose(2, 1, 0, 3)).astype(BF)
    R = np.zeros((32, 32), dtype=np.float32)
    for i in range(16):
        R[i, i + 16] = -1.0
        R[i + 16, i] = 1.0
    M = np.zeros((128, 128), dtype=np.float32)
    for b in range(4):
        M[b * 32:(b + 1) * 32, b * 32:(b + 1) * 32] = R
    c["mrotT"] = M.T.copy().astype(BF)
    inv_freq = 1.0 / 10000.0 ** (np.arange(0, DR, 2, dtype=np.float64) / DR)
    pos = np.arange(S, dtype=np.float64)
    freqs = pos[:, None] * inv_freq[None, :]
    emb = np.concatenate([freqs, freqs], axis=1)
    cos_fm = np.cos(emb).T.astype(np.float32)
    sin_fm = np.sin(emb).T.astype(np.float32)
    c["cosk"] = np.tile(cos_fm, (4, 1)).astype(BF)
    c["sink"] = np.tile(sin_fm, (4, 1)).astype(BF)
    c["onesb"] = np.ones((1, 128), dtype=BF)
    c["onesf"] = np.ones((1, 128), dtype=np.float32)
    c["onescol"] = np.ones((128, 1), dtype=BF)
    c["identb"] = np.eye(128, dtype=BF)
    c["identf"] = np.eye(128, dtype=np.float32)
    return c



TRACE = {}
CUR_ENG = [None]
TAG = [""]
PHASE_MARKS = {}


class Sem:
    def __init__(self, nc, name):
        self.h = nc.semaphore(name).__enter__()
        self.name = name
        self.n = 0
        self._waited = {}

    def inc(self, instr, k=1):
        instr.then_inc(self.h, k)
        self.n += k
        if CUR_ENG[0] is not None:
            TRACE.setdefault(CUR_ENG[0], []).append(
                ("i", self.name, k, TAG[0]))
        return self.n

    def wait(self, eng, n=None):
        n = self.n if n is None else n
        if n <= 0:
            return
        key = str(eng.engine)
        if self._waited.get(key, -1) < n:
            eng.wait_ge(self.h, n)
            self._waited[key] = n
            if CUR_ENG[0] is not None:
                TRACE.setdefault(CUR_ENG[0], []).append(
                    ("w", self.name, n, TAG[0]))


def check_deadlock():
    pos = {e: 0 for e in TRACE}
    cnt = {}
    progress = True
    while progress:
        progress = False
        for e, ops in TRACE.items():
            while pos[e] < len(ops):
                kind, sname, val = ops[pos[e]][:3]
                if kind == "i":
                    cnt[sname] = cnt.get(sname, 0) + val
                    pos[e] += 1
                    progress = True
                elif cnt.get(sname, 0) >= val:
                    pos[e] += 1
                    progress = True
                else:
                    break
    stuck = {e: (pos[e], len(TRACE[e]), TRACE[e][pos[e]] if pos[e] < len(TRACE[e]) else None)
             for e in TRACE if pos[e] < len(TRACE[e])}
    if stuck:
        msg = [f"DEADLOCK: engine {e} stuck at op {p}/{n} waiting {w} (have {cnt.get(w[1], 0) if w else '?'})"
               for e, (p, n, w) in stuck.items()]
        raise RuntimeError("\n".join(msg))


class Arena:

    def __init__(self, nc, name, nelem, dtype):
        self.t = nc.alloc_sbuf_tensor(name, [128, nelem], dtype)
        self.n = nelem

    def v(self, off, n, p0=0, p1=128):
        assert 0 <= off and off + n <= self.n, (off, n, self.n)
        return self.t[p0:p1, off:off + n]


def build_program(taps=(), stop_after=None, opts=None):
    taps = set(taps)
    TRACE.clear()
    nc = bass.Bass("TRN2", target_bir_lowering=False)
    hc = _host_consts()

    x_d = nc.dram_tensor("x", [S, D], F32, kind="ExternalInput")
    w = {}
    for name, shape in [
        ("Wdkv", [D, DKV]), ("Wdq", [D, DKV]), ("Wuk", [DKV, NH * SP]),
        ("Wuv", [DKV, NH * DH]), ("Wuq", [DKV, NH * SP]), ("Wqr", [DKV, NH * DR]),
        ("Wkr", [D, NH * DR]), ("Wo", [D, D]),
        ("WupT", [D, INTER]), ("WgT", [D, INTER]), ("WdnT", [INTER, D]),
    ]:
        w[name] = nc.dram_tensor(name, shape, F32, kind="ExternalInput")
    bia = {}
    for name, shape in [
        ("bdkv", [128, 4]), ("bdq", [128, 4]), ("bkr", [128, 4]),
        ("buk", [96, 16]), ("buq", [96, 16]), ("bqr", [32, 16]),
        ("buv", [1, D]), ("bo", [128, 16]),
        ("suug", [128, 16]), ("wsu", [128, 64]), ("wsg", [128, 64]),
        ("sudp", [128, 64]), ("wdp", [128, 16]),
    ]:
        bia[name] = nc.dram_tensor(name, shape, F32, kind="ExternalInput")
    con = {}
    for name in ["h128", "h64", "h2048t", "mrotT", "cosk", "sink",
                 "onesb", "onesf", "onescol", "identb", "identf"]:
        arr = hc[name]
        dt = BF16 if arr.dtype == BF else F32
        con[name] = nc.dram_tensor(name, list(arr.shape), dt, kind="ExternalInput")

    out_d = nc.dram_tensor("out", [S, D], F32, kind="ExternalOutput")

    wo_bf = nc.dram_tensor("wo_bf", [16, 128, 16, 128], BF16)
    wdn_bf = nc.dram_tensor("wdn_bf", [16, 128, 64, 128], BF16)
    cq_scr = nc.dram_tensor("cq_scr", [4, 128, S], BF16)
    x1_scr = nc.dram_tensor("x1_scr", [S, D], F32)
    proj_scr = nc.dram_tensor("proj_scr", [S, D], BF16)
    z_scr = nc.dram_tensor("z_scr", [64, 128, S], BF16)
    yt_scr = nc.dram_tensor("yt_scr", [64, 128, S], BF16)

    tap_d = {}
    def tap(name, shape, dtype=F32):
        if name in taps:
            tap_d[name] = nc.dram_tensor("tap_" + name, list(shape), dtype,
                                         kind="ExternalOutput")
            return tap_d[name]
        return None

    sb_small = {}
    for name, shape, dt in [
        ("onesb", [1, 128], BF16), ("onesf", [1, 128], F32),
        ("onescol", [128, 1], BF16),
        ("identb", [128, 128], BF16), ("identf", [128, 128], F32),
        ("mrotT", [128, 128], BF16), ("h128", [128, 128], BF16),
        ("h64", [64, 64], BF16), ("eps", [128, 1], F32),
        ("bdkv", [128, 4], F32), ("bdq", [128, 4], F32), ("bkr", [128, 4], F32),
        ("buk", [96, 16], F32), ("buq", [96, 16], F32), ("bqr", [32, 16], F32),
        ("bo", [128, 16], F32), ("suug", [128, 16], F32),
        ("wsu", [128, 64], F32), ("wsg", [128, 64], F32),
        ("sudp", [128, 64], F32), ("wdp", [128, 16], F32),
        ("buvb", [1, D], BF16),
        ("pevac", [128, 512], BF16),
        ("stat", [128, 16], F32),
    ]:
        sb_small[name] = nc.alloc_sbuf_tensor("s_" + name, shape, dt)

    ABF = Arena(nc, "ABF", 90624, BF16)
    AFP = Arena(nc, "AFP", 5632, F32)

    PP0 = nc.alloc_psum_tensor("pp0", [128, 1024], F32)
    PP1 = nc.alloc_psum_tensor("pp1", [128, 1024], F32)
    PS4 = nc.alloc_psum_tensor("ps4", [128, 512], F32)
    PS5 = nc.alloc_psum_tensor("ps5", [128, 512], F32)
    PS = [PP0[:, 0:512], PP0[:, 512:1024], PP1[:, 0:512], PP1[:, 512:1024],
          PS4[:, 0:512], PS5[:, 0:512]]
    PT = [nc.alloc_psum_tensor(f"pt{i}", [128, 1024], BF16) for i in range(2)]

    qL = Sem(nc, "qL")
    qC = Sem(nc, "qC")
    sP = Sem(nc, "sP")
    sA = Sem(nc, "sA")
    sV = Sem(nc, "sV")
    sG = Sem(nc, "sG")
    qB = Sem(nc, "qB")

    ALLS = [qL, qC, sP, sA, sV, sG]

    st = dict(phase="")
    if opts:
        st.update(opts)

    blk = nc.Block("main").__enter__()

    def EMIT(engine_name, fn):
        CUR_ENG[0] = engine_name
        TAG[0] = st.get("phase", "") + ":" + fn.__qualname__.split(".")[-1]
        getattr(blk, engine_name)(fn)
        CUR_ENG[0] = None

    def phase_fence():
        for en in ["sync", "gpsimd", "tensor", "vector", "scalar"]:
            def _f(e):
                for s in ALLS:
                    s.wait(e)
            EMIT(en, _f)

    def p0():
        def _sy(sy):
            for name in ["onesb", "onesf", "onescol", "identb", "identf",
                         "mrotT", "h128", "h64"]:
                qL.inc(sy.dma_start(sb_small[name][:], con[name].ap()), 16)
            for name in ["bdkv", "bdq", "bkr", "buk", "buq", "bqr", "bo",
                         "suug", "wsu", "wsg", "sudp", "wdp"]:
                qL.inc(sy.dma_start(sb_small[name][:], bia[name].ap()), 16)
        EMIT("sync", _sy)

        buv_f = AFP.v(0, D, 0, 1)
        def _sy2(sy):
            qL.inc(sy.dma_start(buv_f, bia["buv"].ap()), 16)
        EMIT("sync", _sy2)
        def _v(v):
            qL.wait(v)
            v.memset(sb_small["eps"][:], EPS)
            sV.inc(v.tensor_copy(sb_small["buvb"][:], buv_f))
        EMIT("vector", _v)


    def emit_ln(in_ap, out_ap, dump1, dump2, deps, out_war=()):
        stt = sb_small["stat"]
        mean, e2, nvar, rstd, nmr = (stt[:, 0:1], stt[:, 1:2], stt[:, 2:3],
                                     stt[:, 3:4], stt[:, 4:5])
        def _v(v):
            for s, m in deps:
                s.wait(v, m)
            sV.inc(v.tensor_scalar(dump1, in_ap, 1.0 / D, None,
                                   op0=ALU.mult, op1=ALU.add, accum_out=mean))
        EMIT("vector", _v)
        def _a(a):
            for s, m in deps:
                s.wait(a, m)
            sA.inc(a.activation(dump2, in_ap, AF.Square,
                                bias=0.0, scale=1.0 / math.sqrt(D),
                                accum_out=e2))
        EMIT("scalar", _a)
        mk_v, mk_a = sV.n, sA.n
        def _v2(v):
            sV.wait(v, mk_v)
            sA.wait(v, mk_a)
            sV.inc(v.tensor_scalar(nvar, mean, mean, e2,
                                   op0=ALU.mult, op1=ALU.subtract))
        EMIT("vector", _v2)
        mk_v2 = sV.n
        def _a2(a):
            sV.wait(a, mk_v2)
            sA.inc(a.activation(nvar, nvar, AF.Ln,
                                bias=sb_small["eps"][:], scale=-1.0))
        EMIT("scalar", _a2)
        def _a3(a):
            sA.inc(a.activation(rstd, nvar, AF.Exp, bias=0.0, scale=-0.5))
        EMIT("scalar", _a3)
        mk_a3 = sA.n
        def _v3(v):
            sA.wait(v, mk_a3)
            sV.inc(v.tensor_scalar(nmr, mean, rstd, -1.0,
                                   op0=ALU.mult, op1=ALU.mult))
        EMIT("vector", _v3)
        mk_v3 = sV.n
        def _a4(a):
            sV.wait(a, mk_v3)
            for s, m in out_war:
                s.wait(a, m)
            sA.inc(a.activation(out_ap, in_ap, AF.Identity,
                                bias=nmr, scale=rstd))
        EMIT("scalar", _a4)
        return sA.n

    O_K, O_CKV, O_WDKV, O_WDQ, O_WKR = 0, 32768, 40960, 49152, 57344
    O_XLN, O_DSQ, O_XFM, O_CQS, O_KRT, O_ROT = (65536, 67584, 69632, 77824,
                                                79872, 81920)
    O_COSK, O_SINK = 83968, 86016
    O_DSQ2 = 88064

    def Kv(p0, p1, off, n):
        return ABF.v(O_K + off, n, p0, p1)

    def p1a():
        def _g(g):
            for nm, off in [("Wdkv", O_WDKV), ("Wdq", O_WDQ), ("Wkr", O_WKR)]:
                src = w[nm].ap().rearrange("(ft p) c -> p ft c", p=128)
                qC.inc(g.dma_start(
                    ABF.v(off, 8192).rearrange("p (ft c) -> p ft c", c=512),
                    src), 16)
        EMIT("gpsimd", _g)
        def _sy(sy):
            qL.inc(sy.dma_start(ABF.v(O_COSK, 2048), con["cosk"].ap()), 16)
            qL.inc(sy.dma_start(ABF.v(O_SINK, 2048), con["sink"].ap()), 16)
        EMIT("sync", _sy)

        def _gc(g):
            for dc in range(16):
                src = w["Wo"].ap()[:, dc * 128:(dc + 1) * 128].rearrange(
                    "(h p) j -> p h j", p=128)
                qB.inc(g.dma_start(wo_bf.ap()[dc], src), 16)
            for dc in range(16):
                src = w["WdnT"].ap()[:, dc * 128:(dc + 1) * 128].rearrange(
                    "(a p) j -> p a j", p=128)
                qB.inc(g.dma_start(wdn_bf.ap()[dc], src), 16)
        EMIT("gpsimd", _gc)
        st["p0wo"] = 16 * 16
        st["p0done"] = qB.n

        for tg in range(4):
            for pt in range(4):
                tt = tg * 4 + pt
                xb = AFP.v((tt % 2) * 2048, 2048)
                war = st.get(f"xbw{tt % 2}", 0)
                def _sy2(sy, tt=tt, xb=xb, war=war):
                    sA.wait(sy, war)
                    qL.inc(sy.dma_start(xb, x_d.ap()[tt * 128:(tt + 1) * 128, :]), 16)
                EMIT("sync", _sy2)
                ln_deps = [(qL, qL.n), (qC, qC.n)]
                xln = ABF.v(O_XLN, 2048)
                mk_ln = emit_ln(xb, xln, ABF.v(O_DSQ, 2048),
                                ABF.v(O_DSQ2, 2048), ln_deps,
                                out_war=[(sP, st.get("xlnP", 0))])
                st[f"xbw{tt % 2}"] = mk_ln
                p0, v0 = sP.n, sV.n
                def _t(t, xln=xln, mk_ln=mk_ln, v0=v0):
                    sA.wait(t, mk_ln)
                    for ft in range(16):
                        sV.wait(t, v0 if ft < 2 else v0 + ft - 1)
                        sP.inc(t.transpose(PT[ft % 2][:, 0:128],
                                           xln[:, ft * 128:(ft + 1) * 128],
                                           sb_small["identb"][:]))
                EMIT("tensor", _t)
                st["xlnP"] = sP.n
                def _v(v, pt=pt, p0=p0):
                    for ft in range(16):
                        sP.wait(v, p0 + ft + 1)
                        sV.inc(v.tensor_copy(
                            ABF.v(O_XFM + ft * 512 + pt * 128, 128),
                            PT[ft % 2][:, 0:128]))
                EMIT("vector", _v)

            mk_xfm = sV.n
            for kind, woff, boff in [("ckv", O_WDKV, "bdkv"),
                                     ("cq", O_WDQ, "bdq"),
                                     ("kr", O_WKR, "bkr")]:
                for ct in range(4):
                    psm = PS[ct % 2]
                    war_ps = st.get(f"psw{ct % 2}", 0)
                    def _t(t, woff=woff, ct=ct, psm=psm, mk_xfm=mk_xfm,
                           war_ps=war_ps):
                        sV.wait(t, mk_xfm)
                        sA.wait(t, war_ps)
                        for ft in range(16):
                            sP.inc(t.matmul(
                                psm[:, 0:512],
                                ABF.v(woff + ft * 512 + ct * 128, 128),
                                ABF.v(O_XFM + ft * 512, 512),
                                start=(ft == 0), stop=(ft == 15)))
                    EMIT("tensor", _t)
                    mk_mm = sP.n
                    war_cq = st.get("cqspill", 0)
                    war_kr = st.get("ropeV", 0)
                    def _a(a, kind=kind, ct=ct, tg=tg, psm=psm, boff=boff,
                           mk_mm=mk_mm, war_cq=war_cq, war_kr=war_kr):
                        sP.wait(a, mk_mm)
                        if kind == "cq":
                            qL.wait(a, war_cq)
                            dst = ABF.v(O_CQS + ct * 512, 512)
                        elif kind == "kr":
                            sV.wait(a, war_kr)
                            dst = ABF.v(O_KRT + ct * 512, 512)
                        else:
                            dst = ABF.v(O_CKV + ct * 2048 + tg * 512, 512)
                        bap = sb_small[boff][:, ct:ct + 1]
                        sA.inc(a.activation(dst, psm[:, 0:512], AF.Identity,
                                            bias=bap, scale=1.0))
                        st[f"psw{ct % 2}"] = sA.n
                    EMIT("scalar", _a)
            mk_cq = sA.n
            def _sy3(sy, tg=tg, mk_cq=mk_cq):
                sA.wait(sy, mk_cq)
                qL.inc(sy.dma_start(
                    cq_scr.ap()[:, :, tg * 512:(tg + 1) * 512].rearrange(
                        "c p t -> p c t"),
                    ABF.v(O_CQS, 2048).rearrange("p (c t) -> p c t", t=512)), 16)
            EMIT("sync", _sy3)
            st["cqspill"] = qL.n

            war_rot = st.get("rotw", 0)
            a0 = sA.n
            def _t2(t, mk_cq=mk_cq, war_rot=war_rot, a0=a0):
                sA.wait(t, mk_cq)
                sA.wait(t, war_rot)
                for rt in range(4):
                    if rt >= 2:
                        sA.wait(t, a0 + rt - 1)
                    sP.inc(t.matmul(PS[2 + rt % 2][:, 0:512],
                                    sb_small["mrotT"][:],
                                    ABF.v(O_KRT + rt * 512, 512),
                                    start=True, stop=True))
            EMIT("tensor", _t2)
            mk_rot = sP.n
            war_ropeV = st.get("ropeV", 0)
            def _a5(a, mk_rot=mk_rot, war_ropeV=war_ropeV):
                sV.wait(a, war_ropeV)
                for rt in range(4):
                    sP.wait(a, mk_rot - 3 + rt)
                    sA.inc(a.activation(ABF.v(O_ROT + rt * 512, 512),
                                        PS[2 + rt % 2][:, 0:512], AF.Identity,
                                        bias=0.0, scale=1.0))
            EMIT("scalar", _a5)
            st["rotw"] = sA.n
            mk_rotc = sA.n
            def _v2(v, tg=tg, mk_rotc=mk_rotc):
                sA.wait(v, mk_rotc)
                for rt in range(4):
                    kr = ABF.v(O_KRT + rt * 512, 512)
                    ro = ABF.v(O_ROT + rt * 512, 512)
                    ck = ABF.v(O_COSK + tg * 512, 512)
                    sk = ABF.v(O_SINK + tg * 512, 512)
                    v.tensor_mul(kr, kr, ck)
                    v.tensor_mul(ro, ro, sk)
                    sV.inc(v.tensor_add(kr, kr, ro))
                for h in range(NH):
                    rt, j = h // 4, h % 4
                    src = ABF.v(O_KRT + rt * 512, 512, j * 32, (j + 1) * 32)
                    dst = Kv(96, 128, h * 2048 + tg * 512, 512)
                    sV.inc(v.tensor_copy(dst, src))
            EMIT("vector", _v2)
            st["ropeV"] = sV.n

        tp = tap("xfm", [128, 16 * 512], BF16)
        if tp is not None:
            def _syx(sy):
                sV.wait(sy)
                qL.inc(sy.dma_start(tp.ap(), ABF.v(O_XFM, 8192)), 16)
            EMIT("sync", _syx)
        tp = tap("ckv", [128, 4 * S], BF16)
        if tp is not None:
            def _syc(sy):
                sA.wait(sy)
                qL.inc(sy.dma_start(tp.ap(), ABF.v(O_CKV, 8192)), 16)
            EMIT("sync", _syc)
        tp = tap("krope", [128, NH * S], BF16)
        if tp is not None:
            def _syr(sy):
                sV.wait(sy)
                qL.inc(sy.dma_start(tp.ap(), ABF.v(O_K, 32768)), 16)
            EMIT("sync", _syr)

    O_V, O_WUK, O_WUV = 40960, 73728, 79872

    def p1b():
        def _g(g):
            src = w["Wuk"].ap().rearrange("(ct p) n -> p ct n", p=128)
            qC.inc(g.dma_start(
                ABF.v(O_WUK, 6144).rearrange("p (ct n) -> p ct n", n=1536),
                src), 16)
            src = w["Wuv"].ap().rearrange("(ct p) n -> p ct n", p=128)
            qC.inc(g.dma_start(
                ABF.v(O_WUV, 8192).rearrange("p (ct n) -> p ct n", n=2048),
                src), 16)
        EMIT("gpsimd", _g)
        mk_w = qC.n

        for h in range(NH):
            for kc in range(4):
                psm = PS[kc % 2]
                def _t(t, h=h, kc=kc, psm=psm):
                    qC.wait(t, mk_w)
                    sA.wait(t, st.get(f"psw{kc % 2}", 0))
                    for ct in range(4):
                        sP.inc(t.matmul(
                            psm[0:96, 0:512],
                            ABF.v(O_WUK + ct * 1536 + h * 96, 96),
                            ABF.v(O_CKV + ct * 2048 + kc * 512, 512),
                            start=(ct == 0), stop=(ct == 3)))
                EMIT("tensor", _t)
                mk_mm = sP.n
                def _a(a, h=h, kc=kc, psm=psm, mk_mm=mk_mm):
                    sP.wait(a, mk_mm)
                    bap = sb_small["buk"]
                    dst0 = Kv(0, 64, h * 2048 + kc * 512, 512)
                    dst1 = Kv(64, 96, h * 2048 + kc * 512, 512)
                    sA.inc(a.activation(dst0, psm[0:64, 0:512], AF.Identity,
                                        bias=bap[0:64, h:h + 1], scale=1.0))
                    sA.inc(a.activation(dst1, psm[64:96, 0:512], AF.Identity,
                                        bias=bap[64:96, h:h + 1], scale=1.0))
                    st[f"psw{kc % 2}"] = sA.n
                EMIT("scalar", _a)
        tp = tap("K", [128, NH * S], BF16)
        if tp is not None:
            def _syk(sy):
                sA.wait(sy)
                sV.wait(sy)
                qL.inc(sy.dma_start(tp.ap(), ABF.v(O_K, 32768)), 16)
            EMIT("sync", _syk)

        for kt in range(NT):
            for dc in range(4):
                psm = PS[2 + dc % 2]
                def _t(t, kt=kt, dc=dc, psm=psm):
                    qC.wait(t, mk_w)
                    sV.wait(t, st.get(f"vw{dc % 2}", 0))
                    for ct in range(4):
                        sP.inc(t.matmul(
                            psm[:, 0:512],
                            ABF.v(O_CKV + ct * 2048 + kt * 128, 128),
                            ABF.v(O_WUV + ct * 2048 + dc * 512, 512),
                            start=(ct == 0), stop=False))
                    sP.inc(t.matmul(psm[:, 0:512], sb_small["onesb"][:],
                                    sb_small["buvb"][:, dc * 512:(dc + 1) * 512],
                                    start=False, stop=True))
                EMIT("tensor", _t)
                mk_mm = sP.n
                def _v(v, kt=kt, dc=dc, psm=psm, mk_mm=mk_mm):
                    sP.wait(v, mk_mm)
                    dst = ABF.v(O_V + kt * 2048 + dc * 512, 512)
                    sV.inc(v.tensor_copy(dst, psm[:, 0:512]))
                    st[f"vw{dc % 2}"] = sV.n
                EMIT("vector", _v)
        tp = tap("V", [128, NT * D], BF16)
        if tp is not None:
            def _syv(sy):
                sV.wait(sy)
                qL.inc(sy.dma_start(tp.ap(), ABF.v(O_V, 32768)), 16)
            EMIT("sync", _syv)

    O_CQ, O_WUQ = 32768, 34816
    O_OST = 32768
    O_QALL = 73728
    O_PRJ = 73728
    O_PROBS = 81920
    O_EDMP = O_PROBS + 1024
    O_WOA, O_WOB = 81920, 83968
    O_CQ2 = 86016
    O_WQR = 88064
    O_CS = 90112
    O_SINQ = 32768
    O_LND1, O_LND2 = 32768, 36864
    A_SCL, A_XRE, A_RSB, A_LROW, A_RROW = 0, 2048, 4096, 4608, 5120

    def p1c():
        n_qc = int(st.get("p1c_qcs", 4))
        n_heads = int(st.get("p1c_heads", NH))
        stage = st.get("p1c_stage", "full")

        def load_cq(qcn, war_pe, war_dve):
            q0n = qcn * 512
            def _sy(sy, qcn=qcn, q0n=q0n, war_pe=war_pe, war_dve=war_dve):
                sP.wait(sy, war_pe)
                sV.wait(sy, war_dve)
                qL.inc(sy.dma_start(
                    ABF.v(O_CQ2, 2048).rearrange("p (c t) -> p c t", t=512),
                    cq_scr.ap()[:, :, q0n:q0n + 512].rearrange(
                        "c p t -> p c t")), 16)
                qL.inc(sy.dma_start(ABF.v(O_CS, 512, 0, 32),
                                    con["cosk"].ap()[0:32, q0n:q0n + 512]), 16)
            EMIT("sync", _sy)
            def _g(g, qcn=qcn, war_pe=war_pe, war_dve=war_dve):
                sP.wait(g, war_pe)
                sV.wait(g, war_dve)
                src = w["Wqr"].ap().rearrange("(ct p) n -> p ct n", p=128)
                qC.inc(g.dma_start(
                    ABF.v(O_WQR, 2048).rearrange("p (ct n) -> p ct n",
                                                 n=512), src), 16)
            EMIT("gpsimd", _g)
            st[f"ldcq{qcn}"] = (qL.n, qC.n)

        def load_wuq(qcn, war_pe):
            q0n = qcn * 512
            def _g(g, war_pe=war_pe, q0n=q0n):
                sP.wait(g, war_pe)
                src = w["Wuq"].ap().rearrange("(ct p) n -> p ct n", p=128)
                qC.inc(g.dma_start(
                    ABF.v(O_WUQ, 6144).rearrange("p (ct n) -> p ct n",
                                                 n=1536), src), 16)
                qC.inc(g.dma_start(ABF.v(O_SINQ, 512, 0, 32),
                                   con["sink"].ap()[0:32, q0n:q0n + 512]), 16)
            EMIT("gpsimd", _g)
            st[f"ldwuq{qcn}"] = qC.n

        def emit_finish(fh, fmk_pv):
            def _fa6(a, fmk_pv=fmk_pv):
                sP.wait(a, fmk_pv)
                sP.wait(a, st.get("p2_bc", 0))
                sA.inc(a.activation(AFP.v(A_LROW, 512, 0, 1),
                                    PS5[0:1, 0:512], AF.Identity,
                                    bias=0.0, scale=1.0))
            EMIT("scalar", _fa6)
            mk_le = sA.n
            def _ft7(t, mk_le=mk_le):
                sA.wait(t, mk_le)
                sV.wait(t, st.get("p2_rec", 0))
                sP.inc(t.matmul(PS5[:, 0:512], sb_small["onesf"][:],
                                AFP.v(A_LROW, 512, 0, 1),
                                start=True, stop=True))
            EMIT("tensor", _ft7)
            st["p2_bc"] = sP.n
            mk_rb = sP.n
            war_rsb = st.get("p2_ov", 0)
            def _fv5(v, mk_rb=mk_rb, war_rsb=war_rsb):
                sP.wait(v, mk_rb)
                sV.wait(v, war_rsb)
                sV.inc(v.reciprocal(AFP.v(A_RSB, 512), PS5[:, 0:512]))
            EMIT("vector", _fv5)
            st["p2_rec"] = sV.n
            mk_rsb = sV.n
            def _fv7(v, fh=fh, fmk_pv=fmk_pv, mk_rsb=mk_rsb):
                sP.wait(v, fmk_pv)
                sV.wait(v, mk_rsb)
                sV.inc(v.tensor_mul(ABF.v(O_OST + fh * 512, 512),
                                    PS[4], AFP.v(A_RSB, 512)))
            EMIT("vector", _fv7)
            st["p2_ov"] = sV.n
            st["reclps"] = sA.n

        load_cq(0, 0, 0)
        load_wuq(0, 0)
        st["fin"] = None
        for qc in range(n_qc):
            q0 = qc * 512
            mk_ld = (st[f"ldcq{qc}"][0],
                     max(st[f"ldcq{qc}"][1], st[f"ldwuq{qc}"]))

            for h in range(NH):
                pb, pr2 = PS[h % 2], PS[2 + h % 2]
                qrt = ABF.v(O_PROBS + (h % 2) * 512, 512, 0, 32)
                war_ps = st.get(f"qb_ps{h % 2}", 0)
                war_v = st.get(f"qb_vv{h % 2}", 0)
                def _t(t, h=h, pb=pb, pr2=pr2, war_ps=war_ps):
                    qL.wait(t, mk_ld[0])
                    qC.wait(t, mk_ld[1])
                    sA.wait(t, war_ps)
                    for ct in range(4):
                        sP.inc(t.matmul(
                            pb[0:96, 0:512],
                            ABF.v(O_WUQ + ct * 1536 + h * 96, 96),
                            ABF.v(O_CQ2 + ct * 512, 512),
                            start=(ct == 0), stop=(ct == 3)))
                    for ct in range(4):
                        sP.inc(t.matmul(
                            pr2[0:32, 0:512],
                            ABF.v(O_WQR + ct * 512 + h * 32, 32),
                            ABF.v(O_CQ2 + ct * 512, 512),
                            start=(ct == 0), stop=(ct == 3)))
                EMIT("tensor", _t)
                mk_qmm = sP.n
                def _a(a, h=h, pb=pb, pr2=pr2, mk_qmm=mk_qmm, war_v=war_v,
                       qrt=qrt):
                    sP.wait(a, mk_qmm)
                    sV.wait(a, war_v)
                    qL.wait(a, st.get("prjspill", 0))
                    sA.inc(a.activation(
                        ABF.v(O_QALL + h * 512, 512, 0, 64), pb[0:64, 0:512],
                        AF.Identity, bias=sb_small["buq"][0:64, h:h + 1],
                        scale=1.0))
                    sA.inc(a.activation(
                        ABF.v(O_QALL + h * 512, 512, 64, 96), pb[64:96, 0:512],
                        AF.Identity, bias=sb_small["buq"][64:96, h:h + 1],
                        scale=1.0))
                    sA.inc(a.activation(
                        qrt, pr2[0:32, 0:512],
                        AF.Identity, bias=sb_small["bqr"][:, h:h + 1], scale=1.0))
                EMIT("scalar", _a)
                mk_qev = sA.n
                def _t2(t, pr2=pr2, mk_qev=mk_qev, qrt=qrt):
                    sA.wait(t, mk_qev)
                    sP.inc(t.matmul(pr2[0:32, 0:512],
                                    sb_small["mrotT"][0:32, 0:32],
                                    qrt, start=True, stop=True))
                EMIT("tensor", _t2)
                mk_rmm = sP.n
                def _a2(a, pr2=pr2, mk_rmm=mk_rmm, h=h):
                    sP.wait(a, mk_rmm)
                    sV.wait(a, st.get(f"qb_vv{h % 2}", 0))
                    sA.inc(a.activation(ABF.v(O_EDMP + (h % 2) * 512, 512, 0, 32),
                                        pr2[0:32, 0:512], AF.Identity,
                                        bias=0.0, scale=1.0))
                EMIT("scalar", _a2)
                st[f"qb_ps{h % 2}"] = sA.n
                mk_rev = sA.n
                def _v(v, h=h, mk_rev=mk_rev, qrt=qrt):
                    sA.wait(v, mk_rev)
                    ro = ABF.v(O_EDMP + (h % 2) * 512, 512, 0, 32)
                    v.tensor_mul(qrt, qrt, ABF.v(O_CS, 512, 0, 32))
                    v.tensor_mul(ro, ro, ABF.v(O_SINQ, 512, 0, 32))
                    sV.inc(v.tensor_add(ABF.v(O_QALL + h * 512, 512, 96, 128),
                                        qrt, ro))
                EMIT("vector", _v)
                st[f"qb_vv{h % 2}"] = sV.n
                st["qb_v"] = sV.n
            st["qb_end"] = sP.n

            if stage == "qb":
                tp = tap_d.get("Qall")
                if tp is not None and qc == int(st.get("tap_qc", 0)):
                    def _syq(sy):
                        sV.wait(sy)
                        sA.wait(sy)
                        qL.inc(sy.dma_start(tp.ap(), ABF.v(O_QALL, 8192)), 16)
                    EMIT("sync", _syq)
                continue

            for h in range(n_heads):
                mk_qb = st["qb_v"]
                p2start = sP.n
                SP2 = (PP0, PP1)
                def pvl(t, j, h, war_ov, war_rcl):
                    if j == 0:
                        sV.wait(t, war_ov)
                        sV.wait(t, war_rcl)
                    po = O_PROBS + ((j // 2) % 3) * 1024 + (j % 2) * 512
                    sP.inc(t.matmul(
                        PS[4], ABF.v(O_V + j * 2048 + h * 128, 128),
                        ABF.v(po, 512), start=(j == 0), stop=(j == 15)))
                    sP.inc(t.matmul(
                        PS5[0:1, 0:512], sb_small["onescol"][:],
                        ABF.v(po, 512), start=(j == 0), stop=(j == 15),
                        skip_group_check=True))
                def _t5a(t, h=h, mk_qb=mk_qb):
                    sV.wait(t, mk_qb)
                    for p in range(2):
                        for s in range(2):
                            sP.inc(t.matmul(
                                SP2[p % 2][:, s * 512:(s + 1) * 512],
                                Kv(0, 128, h * 2048 + (2 * p + s) * 128, 128),
                                ABF.v(O_QALL + h * 512, 512),
                                start=True, stop=True))
                EMIT("tensor", _t5a)
                if st.get("fin") is not None:
                    emit_finish(*st["fin"])
                    st["fin"] = None
                war_ov = st.get("p2_ov", 0)
                war_rcl = st.get("p2_rec", 0)
                p2b = sP.n
                a0 = sA.n
                def _t5b(t, h=h, a0=a0, war_ov=war_ov, war_rcl=war_rcl):
                    for p in range(2, 8):
                        sA.wait(t, a0 + p - 1)
                        for s in range(2):
                            sP.inc(t.matmul(
                                SP2[p % 2][:, s * 512:(s + 1) * 512],
                                Kv(0, 128, h * 2048 + (2 * p + s) * 128, 128),
                                ABF.v(O_QALL + h * 512, 512),
                                start=True, stop=True))
                        pvl(t, 2 * p - 4, h, war_ov, war_rcl)
                        pvl(t, 2 * p - 3, h, war_ov, war_rcl)
                    sA.wait(t, a0 + 7)
                    pvl(t, 12, h, war_ov, war_rcl)
                    pvl(t, 13, h, war_ov, war_rcl)
                    sA.wait(t, a0 + 8)
                    pvl(t, 14, h, war_ov, war_rcl)
                    pvl(t, 15, h, war_ov, war_rcl)
                EMIT("tensor", _t5b)
                mk_pv = sP.n
                def _a4(a, a0=a0, p2start=p2start, p2b=p2b):
                    for p in range(8):
                        if p < 2:
                            sP.wait(a, p2start + 2 * (p + 1))
                        else:
                            sP.wait(a, p2b + 6 * p - 10)
                        sA.inc(a.activation(
                            ABF.v(O_PROBS + (p % 3) * 1024, 1024),
                            SP2[p % 2][:, 0:1024], AF.Exp,
                            bias=0.0, scale=CSC))
                EMIT("scalar", _a4)
                st["fin"] = (h, mk_pv)
                if h == 8 and qc + 1 < n_qc:
                    load_cq(qc + 1, st["qb_end"], st["qb_v"])
            if st.get("fin") is not None:
                emit_finish(*st["fin"])
                st["fin"] = None

            tp = tap_d.get("Ost")
            if tp is not None and qc == int(st.get("tap_qc", 0)):
                def _syo(sy):
                    sV.wait(sy)
                    qL.inc(sy.dma_start(tp.ap(), ABF.v(O_OST, 8192)), 16)
                EMIT("sync", _syo)
                st["ost_tapped"] = qL.n
            if stage == "p2":
                continue

            mk_allO = st["p2_ov"]
            for dc in range(16):
                wob = O_WOA if dc % 2 == 0 else O_WOB
                war_wo = st.get(f"wow{dc % 2}", 0)
                def _sy2(sy, dc=dc, wob=wob, war_wo=war_wo, mk_allO=mk_allO):
                    sP.wait(sy, war_wo)
                    sV.wait(sy, mk_allO)
                    qB.wait(sy, st.get("p0wo", 0))
                    qL.inc(sy.dma_start(
                        ABF.v(wob, 2048),
                        wo_bf.ap()[dc].rearrange("p h j -> p (h j)")), 16)
                EMIT("sync", _sy2)
                mk_wol = qL.n
                wps = PS[4] if dc % 2 == 0 else PS[5]
                war_ps3 = st.get(f"projps{dc % 2}", 0)
                def _t8(t, dc=dc, wob=wob, wps=wps, mk_wol=mk_wol,
                        war_ps3=war_ps3, mk_allO=mk_allO):
                    qL.wait(t, mk_wol)
                    sV.wait(t, mk_allO)
                    sA.wait(t, war_ps3)
                    for h in range(NH):
                        sP.inc(t.matmul(wps[:, 0:512],
                                        ABF.v(wob + h * 128, 128),
                                        ABF.v(O_OST + h * 512, 512),
                                        start=(h == 0), stop=(h == 15)))
                    st[f"wow{dc % 2}"] = sP.n
                EMIT("tensor", _t8)
                mk_wmm = sP.n
                def _a6(a, dc=dc, wps=wps, mk_wmm=mk_wmm):
                    sP.wait(a, mk_wmm)
                    sV.wait(a, st.get(f"projtmp{dc % 2}", 0))
                    sA.inc(a.activation(sb_small["pevac"][:, 0:512],
                                        wps[:, 0:512],
                                        AF.Identity,
                                        bias=sb_small["bo"][:, dc:dc + 1],
                                        scale=1.0))
                EMIT("scalar", _a6)
                st[f"projps{dc % 2}"] = sA.n
                mk_pev = sA.n
                p0t, v0t = sP.n, sV.n
                def _t9(t, dc=dc, mk_pev=mk_pev, v0t=v0t):
                    sA.wait(t, mk_pev)
                    for ts in range(4):
                        sV.wait(t, v0t if ts < 2 else v0t + ts - 1)
                        sP.inc(t.transpose(
                            PT[ts % 2][:, 0:128],
                            sb_small["pevac"][:, ts * 128:(ts + 1) * 128],
                            sb_small["identb"][:]))
                EMIT("tensor", _t9)
                def _v8(v, dc=dc, p0t=p0t):
                    for ts in range(4):
                        sP.wait(v, p0t + ts + 1)
                        sV.inc(v.tensor_copy(
                            ABF.v(O_PRJ + ts * 2048 + dc * 128, 128),
                            PT[ts % 2][:, 0:128]))
                    st[f"projtmp{dc % 2}"] = sV.n
                EMIT("vector", _v8)
                if dc == 15 and qc + 1 < n_qc:
                    load_wuq(qc + 1, sP.n)

            mk_prj = sV.n
            def _sy3(sy, q0=q0, mk_prj=mk_prj):
                sV.wait(sy, mk_prj)
                qL.wait(sy, st.get("ost_tapped", 0))
                for ts in range(4):
                    qL.inc(sy.dma_start(
                        proj_scr.ap()[q0 + ts * 128:q0 + (ts + 1) * 128, :],
                        ABF.v(O_PRJ + ts * 2048, 2048)), 16)
            EMIT("sync", _sy3)
            st["prjspill"] = qL.n
            st["p1c_qread"] = sP.n

    tap("x1", [S, D], F32)
    tap("Qall", [128, 8192], BF16)
    tap("Ost", [128, 8192], BF16)

    F_XFM2, F_XHAD, F_HBUF, F_XLN, F_DMP1, F_DMP2 = (0, 32768, 65536, 69632,
                                                     71680, 73728)
    F_XLN2, F_LNA, F_PRJT = 75776, 77824, 79872

    def f1():
        for tt in range(NT):
            xb = AFP.v((tt % 2) * 2048, 2048)
            prjt = ABF.v(F_PRJT + (tt % 2) * 2048, 2048)
            war = st.get(f"f1xb{tt % 2}", 0)
            war_p = st.get(f"f1pj{tt % 2}", 0)
            def _sy(sy, tt=tt, xb=xb, war=war):
                sA.wait(sy, war)
                qL.inc(sy.dma_start(xb, x_d.ap()[tt * 128:(tt + 1) * 128, :]),
                       16)
            EMIT("sync", _sy)
            def _gp(g, tt=tt, prjt=prjt, war_p=war_p):
                sA.wait(g, war_p)
                qC.inc(g.dma_start(
                    prjt, proj_scr.ap()[tt * 128:(tt + 1) * 128, :]), 16)
            EMIT("gpsimd", _gp)
            ln_deps = [(qC, qC.n), (qL, qL.n)]
            mk_lna = emit_ln(prjt, ABF.v(F_LNA, 2048), ABF.v(F_DMP1, 2048),
                             ABF.v(F_DMP2, 2048), ln_deps,
                             out_war=[(sV, st.get("f1add", 0))])
            st[f"f1pj{tt % 2}"] = mk_lna
            def _vadd(v, xb=xb, mk_lna=mk_lna):
                sA.wait(v, mk_lna)
                qL.wait(v)
                sV.inc(v.tensor_add(xb, xb, ABF.v(F_LNA, 2048)))
            EMIT("vector", _vadd)
            st["f1add"] = sV.n
            mk_x1 = sV.n
            def _syx1(sy, tt=tt, xb=xb, mk_x1=mk_x1):
                sV.wait(sy, mk_x1)
                qL.inc(sy.dma_start(x1_scr.ap()[tt * 128:(tt + 1) * 128, :],
                                    xb), 16)
                if "x1" in taps:
                    qL.inc(sy.dma_start(
                        tap_d["x1"].ap()[tt * 128:(tt + 1) * 128, :], xb), 16)
            EMIT("sync", _syx1)
            xln = ABF.v(F_XLN if tt % 2 == 0 else F_XLN2, 2048)
            mk_ln = emit_ln(xb, xln, ABF.v(F_DMP1, 2048), ABF.v(F_DMP2, 2048),
                            [(sV, mk_x1)],
                            out_war=[(sP, st.get(f"f1xlnP{tt % 2}", 0))])
            st[f"f1xb{tt % 2}"] = mk_ln
            p0t, v0t = sP.n, sV.n
            def _t(t, xln=xln, mk_ln=mk_ln, v0t=v0t):
                sA.wait(t, mk_ln)
                for ft in range(16):
                    sV.wait(t, v0t if ft < 2 else v0t + ft - 1)
                    sP.inc(t.transpose(PT[ft % 2][:, 0:128],
                                       xln[:, ft * 128:(ft + 1) * 128],
                                       sb_small["identb"][:]))
            EMIT("tensor", _t)
            st[f"f1xlnP{tt % 2}"] = sP.n
            def _v(v, tt=tt, p0t=p0t):
                for ft in range(16):
                    sP.wait(v, p0t + ft + 1)
                    sV.inc(v.tensor_scalar(
                        ABF.v(F_XFM2 + ft * 2048 + tt * 128, 128),
                        PT[ft % 2][:, 0:128],
                        sb_small["suug"][:, ft:ft + 1], None, op0=ALU.mult))
            EMIT("vector", _v)
        mk_xfm2 = sV.n

        for fpc in range(16):
            hb = ABF.v(F_HBUF + (fpc % 2) * 2048, 2048)
            war = st.get(f"f1hb{fpc % 2}", 0)
            def _sy2(sy, fpc=fpc, hb=hb, war=war):
                sP.wait(sy, war)
                qL.inc(sy.dma_start(
                    hb.rearrange("p (ft j) -> p ft j", j=128),
                    con["h2048t"].ap()[fpc]), 16)
            EMIT("sync", _sy2)
            mk_h = qL.n
            for tc in range(4):
                psm = PS[tc % 2]
                war_ps = st.get(f"f1ps{tc % 2}", 0)
                def _t2(t, fpc=fpc, tc=tc, hb=hb, psm=psm, mk_h=mk_h,
                        war_ps=war_ps, mk_xfm2=mk_xfm2):
                    qL.wait(t, mk_h)
                    sV.wait(t, mk_xfm2)
                    sA.wait(t, war_ps)
                    for ft in range(16):
                        sP.inc(t.matmul(
                            psm[:, 0:512],
                            hb[:, ft * 128:(ft + 1) * 128],
                            ABF.v(F_XFM2 + ft * 2048 + tc * 512, 512),
                            start=(ft == 0), stop=(ft == 15)))
                EMIT("tensor", _t2)
                st[f"f1hb{fpc % 2}"] = sP.n
                mk_mm = sP.n
                def _a(a, fpc=fpc, tc=tc, psm=psm, mk_mm=mk_mm):
                    sP.wait(a, mk_mm)
                    sA.inc(a.activation(
                        ABF.v(F_XHAD + fpc * 2048 + tc * 512, 512),
                        psm[:, 0:512], AF.Identity, bias=0.0,
                        scale=1.0 / math.sqrt(2048.0)))
                    st[f"f1ps{tc % 2}"] = sA.n
                EMIT("scalar", _a)
        tp = tap_d.get("xhad")
        if tp is not None:
            def _syx(sy):
                sA.wait(sy)
                qL.inc(sy.dma_start(tp.ap(), ABF.v(F_XHAD, 32768)), 16)
            EMIT("sync", _syx)

    F_WUP, F_WG, F_USB, F_GSB, F_YSB, F_ZSB = (0, 4096, 8192, 9216, 10240, 11264)

    def f2():
        def emit_h128(Ap, tcp):
            it = Ap * 4 + tcp
            mk_y = st["f2y"]
            def _t2(t, it=it, mk_y=mk_y):
                sV.wait(t, mk_y)
                sA.wait(t, st.get("f2zev", 0))
                sP.inc(t.matmul(PS[4], sb_small["h128"][:],
                                ABF.v(F_YSB + (it % 2) * 512, 512),
                                start=True, stop=True))
            EMIT("tensor", _t2)
            st["f2hps"] = sP.n
            mk_z = sP.n
            zsb = ABF.v(F_ZSB + (it % 2) * 512, 512)
            war_z = st.get(f"f2z{it % 2}", 0)
            def _a2(a, mk_z=mk_z, zsb=zsb, war_z=war_z):
                sP.wait(a, mk_z)
                qL.wait(a, war_z)
                sA.inc(a.activation(zsb, PS[4], AF.Identity,
                                    bias=0.0, scale=1.0 / math.sqrt(128.0)))
            EMIT("scalar", _a2)
            st["f2zev"] = sA.n
            mk_ze = sA.n
            def _sy(sy, Ap=Ap, tcp=tcp, zsb=zsb, mk_ze=mk_ze):
                sA.wait(sy, mk_ze)
                qL.inc(sy.dma_start(
                    z_scr.ap()[Ap, :, tcp * 512:(tcp + 1) * 512], zsb), 16)
            EMIT("sync", _sy)
            st[f"f2z{it % 2}"] = qL.n

        for A in range(64):
            wub = ABF.v(F_WUP + (A % 2) * 2048, 2048)
            wgb = ABF.v(F_WG + (A % 2) * 2048, 2048)
            war = st.get(f"f2w{A % 2}", 0)
            def _g(g, A=A, wub=wub, wgb=wgb, war=war):
                sP.wait(g, war)
                src = w["WupT"].ap()[:, A * 128:(A + 1) * 128].rearrange(
                    "(ft p) i -> p ft i", p=128)
                qC.inc(g.dma_start(
                    wub.rearrange("p (ft i) -> p ft i", i=128), src), 16)
                src = w["WgT"].ap()[:, A * 128:(A + 1) * 128].rearrange(
                    "(ft p) i -> p ft i", p=128)
                qC.inc(g.dma_start(
                    wgb.rearrange("p (ft i) -> p ft i", i=128), src), 16)
            EMIT("gpsimd", _g)
            mk_w = qC.n
            for tc in range(4):
                it = A * 4 + tc
                bu, bg = PS[2 * (it % 2)], PS[2 * (it % 2) + 1]
                war_u = st.get(f"f2ev{it % 2}", 0)
                def _t(t, tc=tc, wub=wub, wgb=wgb, bu=bu, bg=bg, mk_w=mk_w,
                       war_u=war_u):
                    qC.wait(t, mk_w)
                    sA.wait(t, war_u)
                    for ft in range(16):
                        sP.inc(t.matmul(bu[:, 0:512],
                                        wub[:, ft * 128:(ft + 1) * 128],
                                        ABF.v(F_XHAD + ft * 2048 + tc * 512, 512),
                                        start=(ft == 0), stop=(ft == 15)))
                    for ft in range(16):
                        sP.inc(t.matmul(bg[:, 0:512],
                                        wgb[:, ft * 128:(ft + 1) * 128],
                                        ABF.v(F_XHAD + ft * 2048 + tc * 512, 512),
                                        start=(ft == 0), stop=(ft == 15)))
                EMIT("tensor", _t)
                st[f"f2w{A % 2}"] = sP.n
                mk_ug = sP.n
                if it > 0:
                    emit_h128((it - 1) // 4, (it - 1) % 4)
                war_y = st.get(f"f2yw{it % 2}", 0)
                def _a(a, A=A, it=it, bu=bu, bg=bg, mk_ug=mk_ug, war_y=war_y):
                    sP.wait(a, mk_ug - 16)
                    sV.wait(a, war_y)
                    sA.inc(a.activation(ABF.v(F_USB + (it % 2) * 512, 512),
                                        bu[:, 0:512],
                                        AF.Identity, bias=0.0,
                                        scale=sb_small["wsu"][:, A:A + 1]))
                    sP.wait(a, mk_ug)
                    sA.inc(a.activation(ABF.v(F_GSB + (it % 2) * 512, 512),
                                        bg[:, 0:512],
                                        AF.Silu, bias=0.0,
                                        scale=sb_small["wsg"][:, A:A + 1]))
                EMIT("scalar", _a)
                st[f"f2ev{it % 2}"] = sA.n
                mk_ug2 = sA.n
                def _v(v, A=A, it=it, mk_ug2=mk_ug2):
                    sA.wait(v, mk_ug2)
                    sP.wait(v, st.get("f2hps", 0))
                    sV.inc(v.scalar_tensor_tensor(
                        ABF.v(F_YSB + (it % 2) * 512, 512),
                        ABF.v(F_USB + (it % 2) * 512, 512),
                        sb_small["sudp"][:, A:A + 1],
                        ABF.v(F_GSB + (it % 2) * 512, 512),
                        op0=ALU.mult, op1=ALU.mult))
                EMIT("vector", _v)
                st[f"f2yw{it % 2}"] = sV.n
                st["f2y"] = sV.n
        emit_h128(63, 3)

    F_ZLD, F_YST = 0, 32768

    def f3a():
        for bg in range(4):
            for tc in range(4):
                it = bg * 4 + tc
                zb = ABF.v(F_ZLD + (it % 2) * 16384, 16384)
                war = st.get(f"f3az{it % 2}", 0)
                def _g(g, bg=bg, tc=tc, zb=zb, war=war):
                    sP.wait(g, war)
                    qC.inc(g.dma_start(
                        zb.rearrange("p (b t) -> p b t", t=512)[0:64],
                        z_scr.ap()[:, bg * 32:(bg + 1) * 32,
                                   tc * 512:(tc + 1) * 512]), 16)
                EMIT("gpsimd", _g)
                mk_zl = qC.n
                p0f = sP.n
                a0f, v0f = sA.n, sV.n
                war_st = st.get("f3stg", 0)
                def _t(t, zb=zb, mk_zl=mk_zl, a0f=a0f, v0f=v0f):
                    qC.wait(t, mk_zl)
                    zv = zb.rearrange("p (b t) -> p b t", t=512)
                    for b2 in range(32):
                        if b2 >= 4:
                            if (b2 - 4) % 2 == 0:
                                sA.wait(t, a0f + (b2 - 4) // 2 + 1)
                            else:
                                sV.wait(t, v0f + (b2 - 4) // 2 + 1)
                        sP.inc(t.matmul(PS[b2 % 4][0:64, 0:512],
                                        sb_small["h64"][:], zv[0:64, b2],
                                        start=True, stop=True))
                EMIT("tensor", _t)
                st[f"f3az{it % 2}"] = sP.n
                def _a(a, p0f=p0f, war_st=war_st):
                    qL.wait(a, war_st)
                    for b2 in range(0, 32, 2):
                        sP.wait(a, p0f + b2 + 1)
                        sA.inc(a.activation(ABF.v(F_YST + b2 * 512, 512, 0, 64),
                                            PS[b2 % 4][0:64, 0:512], AF.Identity,
                                            bias=0.0, scale=0.125))
                EMIT("scalar", _a)
                def _v(v, p0f=p0f, war_st=war_st):
                    qL.wait(v, war_st)
                    for b2 in range(1, 32, 2):
                        sP.wait(v, p0f + b2 + 1)
                        sV.inc(v.tensor_scalar(
                            ABF.v(F_YST + b2 * 512, 512, 0, 64),
                            PS[b2 % 4][0:64, 0:512], 0.125, None, op0=ALU.mult))
                EMIT("vector", _v)
                mk_ev, mk_vv = sA.n, sV.n
                def _sy2(sy, bg=bg, tc=tc, mk_ev=mk_ev, mk_vv=mk_vv):
                    sA.wait(sy, mk_ev)
                    sV.wait(sy, mk_vv)
                    qL.inc(sy.dma_start(
                        yt_scr.ap()[:, bg * 32:(bg + 1) * 32,
                                    tc * 512:(tc + 1) * 512],
                        ABF.v(F_YST, 16384).rearrange(
                            "p (b t) -> p b t", t=512)[0:64]), 16)
                EMIT("sync", _sy2)
                st["f3stg"] = qL.n

    F_YT, F_WDN, F_FFM, F_FTM, F_FBF, F_LD1, F_LD2 = (0, 32768, 49152, 57344,
                                                      65536, 66048, 68096)

    def f3b():
        for tc in range(4):
            war = st.get("f3byt", 0)
            def _sy(sy, tc=tc, war=war):
                sP.wait(sy, war)
                qC.inc(sy.dma_start(
                    ABF.v(F_YT, 32768).rearrange("p (a t) -> p a t", t=512),
                    yt_scr.ap()[:, :, tc * 512:(tc + 1) * 512].rearrange(
                        "a b t -> b a t")), 16)
            EMIT("gpsimd", _sy)
            mk_yt = qC.n
            for dc in range(16):
                it = tc * 16 + dc
                wdb = ABF.v(F_WDN + (it % 2) * 8192, 8192)
                war_w = st.get(f"f3bw{it % 2}", 0)
                wq = qC if dc % 2 == 0 else qL
                def _sy2(sy, dc=dc, wdb=wdb, war_w=war_w, wq=wq):
                    sP.wait(sy, war_w)
                    qB.wait(sy, st.get("p0done", 0))
                    wq.inc(sy.dma_start(
                        wdb.rearrange("p (a j) -> p a j", j=128),
                        wdn_bf.ap()[dc]), 16)
                EMIT("gpsimd" if dc % 2 == 0 else "sync", _sy2)
                mk_wl = wq.n
                psm = PS[dc % 2]
                war_ps = st.get(f"f3bps{dc % 2}", 0)
                def _t(t, dc=dc, wdb=wdb, psm=psm, mk_wl=mk_wl, mk_yt=mk_yt,
                       war_ps=war_ps, wq=wq):
                    wq.wait(t, mk_wl)
                    qC.wait(t, mk_yt)
                    sA.wait(t, war_ps)
                    for a2 in range(64):
                        sP.inc(t.matmul(psm[:, 0:512],
                                        wdb[:, a2 * 128:(a2 + 1) * 128],
                                        ABF.v(F_YT + a2 * 512, 512),
                                        start=(a2 == 0), stop=(a2 == 63)))
                    st[f"f3bw{it % 2}"] = sP.n
                EMIT("tensor", _t)
                mk_mm = sP.n
                def _a(a, dc=dc, psm=psm, mk_mm=mk_mm):
                    sP.wait(a, mk_mm)
                    sA.inc(a.activation(ABF.v(F_FFM + dc * 512, 512),
                                        psm[:, 0:512], AF.Identity, bias=0.0,
                                        scale=sb_small["wdp"][:, dc:dc + 1]))
                    st[f"f3bps{dc % 2}"] = sA.n
                EMIT("scalar", _a)
                mk_ev = sA.n
                p0t, v0t = sP.n, sV.n
                def _t2(t, dc=dc, mk_ev=mk_ev, v0t=v0t):
                    sA.wait(t, mk_ev)
                    for ts in range(4):
                        sV.wait(t, v0t if ts < 2 else v0t + ts - 1)
                        sP.inc(t.transpose(
                            PT[ts % 2][:, 0:128],
                            ABF.v(F_FFM + dc * 512 + ts * 128, 128),
                            sb_small["identb"][:]))
                EMIT("tensor", _t2)
                def _v(v, dc=dc, p0t=p0t):
                    sA.wait(v, st.get("f3bln", 0))
                    for ts in range(4):
                        sP.wait(v, p0t + ts + 1)
                        sV.inc(v.tensor_copy(
                            ABF.v(F_FTM + ts * 2048 + dc * 128, 128),
                            PT[ts % 2][:, 0:128]))
                EMIT("vector", _v)
            mk_ftm = sV.n
            for ts in range(4):
                r0 = tc * 512 + ts * 128
                def _sy3(sy, r0=r0):
                    qL.wait(sy, st.get("f3bst", 0))
                    sV.wait(sy, st.get("f3badd", 0))
                    qL.inc(sy.dma_start(AFP.v(2048, 2048),
                                        x1_scr.ap()[r0:r0 + 128, :]), 16)
                EMIT("sync", _sy3)
                deps = [(sV, mk_ftm), (qL, qL.n)]
                mk_ln = emit_ln(ABF.v(F_FTM + ts * 2048, 2048),
                                AFP.v(0, 2048), ABF.v(F_LD1, 2048),
                                ABF.v(F_LD2, 2048), deps,
                                out_war=[(sV, st.get("f3badd", 0))])
                def _v2(v, mk_ln=mk_ln):
                    sA.wait(v, mk_ln)
                    qL.wait(v)
                    sV.inc(v.tensor_add(AFP.v(2048, 2048), AFP.v(2048, 2048),
                                        AFP.v(0, 2048)))
                EMIT("vector", _v2)
                st["f3badd"] = sV.n
                mk_x2 = sV.n
                def _sy4(sy, r0=r0, mk_x2=mk_x2):
                    sV.wait(sy, mk_x2)
                    qL.inc(sy.dma_start(out_d.ap()[r0:r0 + 128, :],
                                        AFP.v(2048, 2048)), 16)
                EMIT("sync", _sy4)
                st["f3bst"] = qL.n
                st["f3bln"] = mk_ln
            st["f3byt"] = sP.n

    tap("xhad", [128, 32768], BF16)

    phases = [("p0", p0), ("p1a", p1a), ("p1b", p1b), ("p1c", p1c),
              ("f1", f1), ("f2", f2), ("f3a", f3a), ("f3b", f3b)]
    PHASE_MARKS.clear()
    for name, fn in phases:
        st["phase"] = name
        fn()
        phase_fence()
        PHASE_MARKS[name] = nc.n_instructions()
        if stop_after == name:
            break

    def _fin(sy):
        for s in ALLS:
            s.wait(sy)
        qB.wait(sy)
    EMIT("sync", _fin)
    blk.__exit__(None, None, None)
    check_deadlock()
    return nc, tap_d



def _prep_inputs(inputs):
    g = lambda k: np.asarray(inputs[k])
    hc = _host_consts()
    per_core_common = {
        "Wdkv": g("L0_attn_W_dkv"), "Wdq": g("L0_attn_W_dq"),
        "Wuk": g("L0_attn_W_uk"), "Wuv": g("L0_attn_W_uv"),
        "Wuq": g("L0_attn_W_uq"), "Wqr": g("L0_attn_W_qr"),
        "Wkr": g("L0_attn_W_kr"), "Wo": g("L0_attn_W_o"),
        "WupT": np.ascontiguousarray(g("L0_ffn_W_up").T),
        "WgT": np.ascontiguousarray(g("L0_ffn_W_gate").T),
        "WdnT": np.ascontiguousarray(g("L0_ffn_W_down").T),
        "bdkv": np.ascontiguousarray(g("L0_attn_b_dkv").reshape(4, 128).T),
        "bdq": np.ascontiguousarray(g("L0_attn_b_dq").reshape(4, 128).T),
        "bkr": np.ascontiguousarray(g("L0_attn_b_kr").reshape(4, 128).T),
        "buk": np.ascontiguousarray(g("L0_attn_b_uk").reshape(16, 96).T),
        "buq": np.ascontiguousarray(g("L0_attn_b_uq").reshape(16, 96).T),
        "bqr": np.ascontiguousarray(g("L0_attn_b_qr").reshape(16, 32).T),
        "buv": np.ascontiguousarray(g("L0_attn_b_uv").reshape(1, 2048)),
        "bo": np.ascontiguousarray(g("L0_attn_b_o").reshape(16, 128).T),
        "suug": np.ascontiguousarray(g("L0_ffn_SU_ug").reshape(16, 128).T),
        "wsu": np.ascontiguousarray(
            g("L0_ffn_Wscale_ug")[:INTER].reshape(64, 128).T),
        "wsg": np.ascontiguousarray(
            g("L0_ffn_Wscale_ug")[INTER:].reshape(64, 128).T),
        "sudp": np.ascontiguousarray(g("L0_ffn_SU_dp").reshape(64, 128).T),
        "wdp": np.ascontiguousarray(g("L0_ffn_Wscale_dp").reshape(16, 128).T),
    }
    for k in ["h128", "h64", "h2048t", "mrotT", "cosk", "sink",
              "onesb", "onesf", "onescol", "identb", "identf"]:
        per_core_common[k] = hc[k]
    x = np.asarray(inputs["L0_attn_x"])
    in_maps = []
    for c in range(8):
        m = dict(per_core_common)
        m["x"] = np.ascontiguousarray(x[c])
        in_maps.append(m)
    return in_maps


def kernel(**inputs):
    nc, _ = build_program()
    in_maps = _prep_inputs(inputs)
    res = run_bass_kernel_spmd(nc, in_maps, core_ids=list(range(8)))
    return np.stack([np.asarray(r["out"], dtype=np.float32)
                     for r in res.results], axis=0)


# version 12 (final)
# speedup vs baseline: 1.1268x; 1.0772x over previous
import math
import numpy as np
import ml_dtypes
import concourse.bass as bass
import concourse.mybir as mybir
from concourse.bass_utils import run_bass_kernel_spmd

F32 = mybir.dt.float32
BF16 = mybir.dt.bfloat16
AF = mybir.ActivationFunctionType
ALU = mybir.AluOpType
AX = mybir.AxisListType
BF = ml_dtypes.bfloat16

S = 2048
D = 2048
NH = 16
DKV = 512
DR = 32
DH = 128
SP = 96
INTER = 8192
NT = 16
EPS = 1e-5
CSC = 1.0 / math.sqrt(128.0)
RSC = math.sqrt(128.0)



def _hadamard(n):
    h = np.array([[1.0]], dtype=np.float64)
    while h.shape[0] < n:
        h = np.block([[h, h], [h, -h]])
    return h.astype(np.float32)


def _host_consts():
    c = {}
    c["h128"] = _hadamard(128).astype(BF)
    c["h64"] = _hadamard(64).astype(BF)
    H = _hadamard(2048)
    c["h2048t"] = np.ascontiguousarray(
        H.reshape(16, 128, 16, 128).transpose(2, 1, 0, 3)).astype(BF)
    R = np.zeros((32, 32), dtype=np.float32)
    for i in range(16):
        R[i, i + 16] = -1.0
        R[i + 16, i] = 1.0
    M = np.zeros((128, 128), dtype=np.float32)
    for b in range(4):
        M[b * 32:(b + 1) * 32, b * 32:(b + 1) * 32] = R
    c["mrotT"] = M.T.copy().astype(BF)
    inv_freq = 1.0 / 10000.0 ** (np.arange(0, DR, 2, dtype=np.float64) / DR)
    pos = np.arange(S, dtype=np.float64)
    freqs = pos[:, None] * inv_freq[None, :]
    emb = np.concatenate([freqs, freqs], axis=1)
    cos_fm = np.cos(emb).T.astype(np.float32)
    sin_fm = np.sin(emb).T.astype(np.float32)
    c["cosk"] = np.tile(cos_fm, (4, 1)).astype(BF)
    c["sink"] = np.tile(sin_fm, (4, 1)).astype(BF)
    c["onesb"] = np.ones((1, 128), dtype=BF)
    c["onesf"] = np.ones((1, 128), dtype=np.float32)
    c["onescol"] = np.ones((128, 1), dtype=BF)
    c["identb"] = np.eye(128, dtype=BF)
    c["identf"] = np.eye(128, dtype=np.float32)
    return c



TRACE = {}
CUR_ENG = [None]
TAG = [""]
PHASE_MARKS = {}


class Sem:
    def __init__(self, nc, name):
        self.h = nc.semaphore(name).__enter__()
        self.name = name
        self.n = 0
        self._waited = {}

    def inc(self, instr, k=1):
        instr.then_inc(self.h, k)
        self.n += k
        if CUR_ENG[0] is not None:
            TRACE.setdefault(CUR_ENG[0], []).append(
                ("i", self.name, k, TAG[0]))
        return self.n

    def wait(self, eng, n=None):
        n = self.n if n is None else n
        if n <= 0:
            return
        key = str(eng.engine)
        if self._waited.get(key, -1) < n:
            eng.wait_ge(self.h, n)
            self._waited[key] = n
            if CUR_ENG[0] is not None:
                TRACE.setdefault(CUR_ENG[0], []).append(
                    ("w", self.name, n, TAG[0]))


def check_deadlock():
    pos = {e: 0 for e in TRACE}
    cnt = {}
    progress = True
    while progress:
        progress = False
        for e, ops in TRACE.items():
            while pos[e] < len(ops):
                kind, sname, val = ops[pos[e]][:3]
                if kind == "i":
                    cnt[sname] = cnt.get(sname, 0) + val
                    pos[e] += 1
                    progress = True
                elif cnt.get(sname, 0) >= val:
                    pos[e] += 1
                    progress = True
                else:
                    break
    stuck = {e: (pos[e], len(TRACE[e]), TRACE[e][pos[e]] if pos[e] < len(TRACE[e]) else None)
             for e in TRACE if pos[e] < len(TRACE[e])}
    if stuck:
        msg = [f"DEADLOCK: engine {e} stuck at op {p}/{n} waiting {w} (have {cnt.get(w[1], 0) if w else '?'})"
               for e, (p, n, w) in stuck.items()]
        raise RuntimeError("\n".join(msg))


class Arena:

    def __init__(self, nc, name, nelem, dtype):
        self.t = nc.alloc_sbuf_tensor(name, [128, nelem], dtype)
        self.n = nelem

    def v(self, off, n, p0=0, p1=128):
        assert 0 <= off and off + n <= self.n, (off, n, self.n)
        return self.t[p0:p1, off:off + n]


def build_program(taps=(), stop_after=None, opts=None):
    taps = set(taps)
    TRACE.clear()
    nc = bass.Bass("TRN2", target_bir_lowering=False)
    hc = _host_consts()

    x_d = nc.dram_tensor("x", [S, D], F32, kind="ExternalInput")
    w = {}
    for name, shape in [
        ("Wdkv", [D, DKV]), ("Wdq", [D, DKV]), ("Wuk", [DKV, NH * SP]),
        ("Wuv", [DKV, NH * DH]), ("Wuq", [DKV, NH * SP]), ("Wqr", [DKV, NH * DR]),
        ("Wkr", [D, NH * DR]), ("Wo", [D, D]),
        ("WupT", [D, INTER]), ("WgT", [D, INTER]), ("WdnT", [INTER, D]),
    ]:
        w[name] = nc.dram_tensor(name, shape, F32, kind="ExternalInput")
    bia = {}
    for name, shape in [
        ("bdkv", [128, 4]), ("bdq", [128, 4]), ("bkr", [128, 4]),
        ("buk", [96, 16]), ("buq", [96, 16]), ("bqr", [32, 16]),
        ("buv", [1, D]), ("bo", [128, 16]),
        ("suug", [128, 16]), ("wsu", [128, 64]), ("wsg", [128, 64]),
        ("sudp", [128, 64]), ("wdp", [128, 16]),
    ]:
        bia[name] = nc.dram_tensor(name, shape, F32, kind="ExternalInput")
    con = {}
    for name in ["h128", "h64", "h2048t", "mrotT", "cosk", "sink",
                 "onesb", "onesf", "onescol", "identb", "identf"]:
        arr = hc[name]
        dt = BF16 if arr.dtype == BF else F32
        con[name] = nc.dram_tensor(name, list(arr.shape), dt, kind="ExternalInput")

    out_d = nc.dram_tensor("out", [S, D], F32, kind="ExternalOutput")

    wo_bf = nc.dram_tensor("wo_bf", [16, 128, 16, 128], BF16)
    wdn_bf = nc.dram_tensor("wdn_bf", [16, 128, 64, 128], BF16)
    cq_scr = nc.dram_tensor("cq_scr", [4, 128, S], BF16)
    x1_scr = nc.dram_tensor("x1_scr", [S, D], F32)
    proj_scr = nc.dram_tensor("proj_scr", [S, D], BF16)
    z_scr = nc.dram_tensor("z_scr", [64, 128, S], BF16)
    yt_scr = nc.dram_tensor("yt_scr", [64, 128, S], BF16)

    tap_d = {}
    def tap(name, shape, dtype=F32):
        if name in taps:
            tap_d[name] = nc.dram_tensor("tap_" + name, list(shape), dtype,
                                         kind="ExternalOutput")
            return tap_d[name]
        return None

    sb_small = {}
    for name, shape, dt in [
        ("onesb", [1, 128], BF16), ("onesf", [1, 128], F32),
        ("onescol", [128, 1], BF16),
        ("identb", [128, 128], BF16),
        ("mrotT", [128, 128], BF16), ("h128", [128, 128], BF16),
        ("h64", [64, 64], BF16), ("eps", [128, 1], F32),
        ("bdkv", [128, 4], F32), ("bdq", [128, 4], F32), ("bkr", [128, 4], F32),
        ("buk", [96, 16], F32), ("buq", [96, 16], F32), ("bqr", [32, 16], F32),
        ("bo", [128, 16], F32), ("suug", [128, 16], F32),
        ("wsu", [128, 64], F32), ("wsg", [128, 64], F32),
        ("sudp", [128, 64], F32), ("wdp", [128, 16], F32),
        ("buvb", [1, D], BF16),
        ("pevac", [128, 1024], BF16),
        ("stat", [128, 16], F32),
    ]:
        sb_small[name] = nc.alloc_sbuf_tensor("s_" + name, shape, dt)

    ABF = Arena(nc, "ABF", 90624, BF16)
    AFP = Arena(nc, "AFP", 5120, F32)

    PP0 = nc.alloc_psum_tensor("pp0", [128, 1024], F32)
    PP1 = nc.alloc_psum_tensor("pp1", [128, 1024], F32)
    PS4 = nc.alloc_psum_tensor("ps4", [128, 512], F32)
    PS5 = nc.alloc_psum_tensor("ps5", [128, 512], F32)
    PS = [PP0[:, 0:512], PP0[:, 512:1024], PP1[:, 0:512], PP1[:, 512:1024],
          PS4[:, 0:512], PS5[:, 0:512]]
    PT = [nc.alloc_psum_tensor(f"pt{i}", [128, 1024], BF16) for i in range(2)]

    qL = Sem(nc, "qL")
    qC = Sem(nc, "qC")
    sP = Sem(nc, "sP")
    sA = Sem(nc, "sA")
    sV = Sem(nc, "sV")
    sG = Sem(nc, "sG")
    qB = Sem(nc, "qB")

    ALLS = [qL, qC, sP, sA, sV, sG]

    st = dict(phase="")
    if opts:
        st.update(opts)

    blk = nc.Block("main").__enter__()

    def EMIT(engine_name, fn):
        CUR_ENG[0] = engine_name
        TAG[0] = st.get("phase", "") + ":" + fn.__qualname__.split(".")[-1]
        getattr(blk, engine_name)(fn)
        CUR_ENG[0] = None

    def phase_fence():
        for en in ["sync", "gpsimd", "tensor", "vector", "scalar"]:
            def _f(e):
                for s in ALLS:
                    s.wait(e)
            EMIT(en, _f)

    def p0():
        def _sy(sy):
            for name in ["onesb", "onesf", "onescol", "identb",
                         "mrotT", "h128", "h64"]:
                qL.inc(sy.dma_start(sb_small[name][:], con[name].ap()), 16)
            for name in ["bdkv", "bdq", "bkr", "buk", "buq", "bqr", "bo",
                         "suug", "wsu", "wsg", "sudp", "wdp"]:
                qL.inc(sy.dma_start(sb_small[name][:], bia[name].ap()), 16)
        EMIT("sync", _sy)

        buv_f = AFP.v(0, D, 0, 1)
        def _sy2(sy):
            qL.inc(sy.dma_start(buv_f, bia["buv"].ap()), 16)
        EMIT("sync", _sy2)
        def _v(v):
            qL.wait(v)
            v.memset(sb_small["eps"][:], EPS)
            sV.inc(v.tensor_copy(sb_small["buvb"][:], buv_f))
        EMIT("vector", _v)


    def _ln_cols(sbase):
        stt = sb_small["stat"]
        return (stt[:, sbase:sbase + 1], stt[:, sbase + 1:sbase + 2],
                stt[:, sbase + 2:sbase + 3], stt[:, sbase + 3:sbase + 4],
                stt[:, sbase + 4:sbase + 5])

    def emit_ln_stats(in_ap, dump1, dump2, deps, sbase=0):
        mean, e2, nvar, rstd, nmr = _ln_cols(sbase)
        def _v(v):
            for s, m in deps:
                s.wait(v, m)
            sV.inc(v.tensor_scalar(dump1, in_ap, 1.0 / D, None,
                                   op0=ALU.mult, op1=ALU.add, accum_out=mean))
        EMIT("vector", _v)
        def _a(a):
            for s, m in deps:
                s.wait(a, m)
            sA.inc(a.activation(dump2, in_ap, AF.Square,
                                bias=0.0, scale=1.0 / math.sqrt(D),
                                accum_out=e2))
        EMIT("scalar", _a)
        return (sV.n, sA.n)

    def emit_ln_tail(in_ap, out_ap, marks, out_war=(), sbase=0):
        mean, e2, nvar, rstd, nmr = _ln_cols(sbase)
        mk_v, mk_a = marks
        def _v2(v):
            sV.wait(v, mk_v)
            sA.wait(v, mk_a)
            sV.inc(v.tensor_scalar(nvar, mean, mean, e2,
                                   op0=ALU.mult, op1=ALU.subtract))
        EMIT("vector", _v2)
        mk_v2 = sV.n
        def _a2(a):
            sV.wait(a, mk_v2)
            sA.inc(a.activation(nvar, nvar, AF.Ln,
                                bias=sb_small["eps"][:], scale=-1.0))
        EMIT("scalar", _a2)
        def _a3(a):
            sA.inc(a.activation(rstd, nvar, AF.Exp, bias=0.0, scale=-0.5))
        EMIT("scalar", _a3)
        mk_a3 = sA.n
        def _v3(v):
            sA.wait(v, mk_a3)
            sV.inc(v.tensor_scalar(nmr, mean, rstd, -1.0,
                                   op0=ALU.mult, op1=ALU.mult))
        EMIT("vector", _v3)
        mk_v3 = sV.n
        def _a4(a):
            sV.wait(a, mk_v3)
            for s, m in out_war:
                s.wait(a, m)
            sA.inc(a.activation(out_ap, in_ap, AF.Identity,
                                bias=nmr, scale=rstd))
        EMIT("scalar", _a4)
        return sA.n

    def emit_ln(in_ap, out_ap, dump1, dump2, deps, out_war=(), sbase=0):
        marks = emit_ln_stats(in_ap, dump1, dump2, deps, sbase)
        return emit_ln_tail(in_ap, out_ap, marks, out_war, sbase)

    O_K, O_CKV, O_WDKV, O_WDQ, O_WKR = 0, 32768, 40960, 49152, 57344
    O_XLN, O_DSQ, O_XFM, O_CQS, O_KRT, O_ROT = (65536, 67584, 69632, 77824,
                                                79872, 81920)
    O_COSK, O_SINK = 83968, 86016
    O_DSQ2 = 88064

    def Kv(p0, p1, off, n):
        return ABF.v(O_K + off, n, p0, p1)

    def p1a():
        def _g(g):
            for nm, off in [("Wdkv", O_WDKV), ("Wdq", O_WDQ), ("Wkr", O_WKR)]:
                src = w[nm].ap().rearrange("(ft p) c -> p ft c", p=128)
                qC.inc(g.dma_start(
                    ABF.v(off, 8192).rearrange("p (ft c) -> p ft c", c=512),
                    src), 16)
        EMIT("gpsimd", _g)
        def _sy(sy):
            qL.inc(sy.dma_start(ABF.v(O_COSK, 2048), con["cosk"].ap()), 16)
            qL.inc(sy.dma_start(ABF.v(O_SINK, 2048), con["sink"].ap()), 16)
        EMIT("sync", _sy)

        def _gc(g):
            for dc in range(16):
                src = w["Wo"].ap()[:, dc * 128:(dc + 1) * 128].rearrange(
                    "(h p) j -> p h j", p=128)
                qB.inc(g.dma_start(wo_bf.ap()[dc], src), 16)
            for dc in range(16):
                src = w["WdnT"].ap()[:, dc * 128:(dc + 1) * 128].rearrange(
                    "(a p) j -> p a j", p=128)
                qB.inc(g.dma_start(wdn_bf.ap()[dc], src), 16)
        EMIT("gpsimd", _gc)
        st["p0wo"] = 16 * 16
        st["p0done"] = qB.n

        for tg in range(4):
            for pt in range(4):
                tt = tg * 4 + pt
                xb = AFP.v((tt % 2) * 2048, 2048)
                war = st.get(f"xbw{tt % 2}", 0)
                def _sy2(sy, tt=tt, xb=xb, war=war):
                    sA.wait(sy, war)
                    qL.inc(sy.dma_start(xb, x_d.ap()[tt * 128:(tt + 1) * 128, :]), 16)
                EMIT("sync", _sy2)
                ln_deps = [(qL, qL.n), (qC, qC.n)]
                xln = ABF.v(O_XLN, 2048)
                mk_ln = emit_ln(xb, xln, ABF.v(O_DSQ, 2048),
                                ABF.v(O_DSQ2, 2048), ln_deps,
                                out_war=[(sP, st.get("xlnP", 0))])
                st[f"xbw{tt % 2}"] = mk_ln
                p0, v0 = sP.n, sV.n
                def _t(t, xln=xln, mk_ln=mk_ln, v0=v0):
                    sA.wait(t, mk_ln)
                    for ft in range(16):
                        sV.wait(t, v0 if ft < 2 else v0 + ft - 1)
                        sP.inc(t.transpose(PT[ft % 2][:, 0:128],
                                           xln[:, ft * 128:(ft + 1) * 128],
                                           sb_small["identb"][:]))
                EMIT("tensor", _t)
                st["xlnP"] = sP.n
                def _v(v, pt=pt, p0=p0):
                    for ft in range(16):
                        sP.wait(v, p0 + ft + 1)
                        sV.inc(v.tensor_copy(
                            ABF.v(O_XFM + ft * 512 + pt * 128, 128),
                            PT[ft % 2][:, 0:128]))
                EMIT("vector", _v)

            mk_xfm = sV.n
            for kind, woff, boff in [("ckv", O_WDKV, "bdkv"),
                                     ("cq", O_WDQ, "bdq"),
                                     ("kr", O_WKR, "bkr")]:
                for ct in range(4):
                    psm = PS[ct % 2]
                    war_ps = st.get(f"psw{ct % 2}", 0)
                    def _t(t, woff=woff, ct=ct, psm=psm, mk_xfm=mk_xfm,
                           war_ps=war_ps):
                        sV.wait(t, mk_xfm)
                        sA.wait(t, war_ps)
                        for ft in range(16):
                            sP.inc(t.matmul(
                                psm[:, 0:512],
                                ABF.v(woff + ft * 512 + ct * 128, 128),
                                ABF.v(O_XFM + ft * 512, 512),
                                start=(ft == 0), stop=(ft == 15)))
                    EMIT("tensor", _t)
                    mk_mm = sP.n
                    war_cq = st.get("cqspill", 0)
                    war_kr = st.get("ropeV", 0)
                    def _a(a, kind=kind, ct=ct, tg=tg, psm=psm, boff=boff,
                           mk_mm=mk_mm, war_cq=war_cq, war_kr=war_kr):
                        sP.wait(a, mk_mm)
                        if kind == "cq":
                            qL.wait(a, war_cq)
                            dst = ABF.v(O_CQS + ct * 512, 512)
                        elif kind == "kr":
                            sV.wait(a, war_kr)
                            dst = ABF.v(O_KRT + ct * 512, 512)
                        else:
                            dst = ABF.v(O_CKV + ct * 2048 + tg * 512, 512)
                        bap = sb_small[boff][:, ct:ct + 1]
                        sA.inc(a.activation(dst, psm[:, 0:512], AF.Identity,
                                            bias=bap, scale=1.0))
                        st[f"psw{ct % 2}"] = sA.n
                    EMIT("scalar", _a)
            mk_cq = sA.n
            def _sy3(sy, tg=tg, mk_cq=mk_cq):
                sA.wait(sy, mk_cq)
                qL.inc(sy.dma_start(
                    cq_scr.ap()[:, :, tg * 512:(tg + 1) * 512].rearrange(
                        "c p t -> p c t"),
                    ABF.v(O_CQS, 2048).rearrange("p (c t) -> p c t", t=512)), 16)
            EMIT("sync", _sy3)
            st["cqspill"] = qL.n

            war_rot = st.get("rotw", 0)
            a0 = sA.n
            def _t2(t, mk_cq=mk_cq, war_rot=war_rot, a0=a0):
                sA.wait(t, mk_cq)
                sA.wait(t, war_rot)
                for rt in range(4):
                    if rt >= 2:
                        sA.wait(t, a0 + rt - 1)
                    sP.inc(t.matmul(PS[2 + rt % 2][:, 0:512],
                                    sb_small["mrotT"][:],
                                    ABF.v(O_KRT + rt * 512, 512),
                                    start=True, stop=True))
            EMIT("tensor", _t2)
            mk_rot = sP.n
            war_ropeV = st.get("ropeV", 0)
            def _a5(a, mk_rot=mk_rot, war_ropeV=war_ropeV):
                sV.wait(a, war_ropeV)
                for rt in range(4):
                    sP.wait(a, mk_rot - 3 + rt)
                    sA.inc(a.activation(ABF.v(O_ROT + rt * 512, 512),
                                        PS[2 + rt % 2][:, 0:512], AF.Identity,
                                        bias=0.0, scale=1.0))
            EMIT("scalar", _a5)
            st["rotw"] = sA.n
            mk_rotc = sA.n
            def _v2(v, tg=tg, mk_rotc=mk_rotc):
                sA.wait(v, mk_rotc)
                for rt in range(4):
                    kr = ABF.v(O_KRT + rt * 512, 512)
                    ro = ABF.v(O_ROT + rt * 512, 512)
                    ck = ABF.v(O_COSK + tg * 512, 512)
                    sk = ABF.v(O_SINK + tg * 512, 512)
                    v.tensor_mul(kr, kr, ck)
                    v.tensor_mul(ro, ro, sk)
                    sV.inc(v.tensor_add(kr, kr, ro))
                for h in range(NH):
                    rt, j = h // 4, h % 4
                    src = ABF.v(O_KRT + rt * 512, 512, j * 32, (j + 1) * 32)
                    dst = Kv(96, 128, h * 2048 + tg * 512, 512)
                    sV.inc(v.tensor_copy(dst, src))
            EMIT("vector", _v2)
            st["ropeV"] = sV.n

        tp = tap("xfm", [128, 16 * 512], BF16)
        if tp is not None:
            def _syx(sy):
                sV.wait(sy)
                qL.inc(sy.dma_start(tp.ap(), ABF.v(O_XFM, 8192)), 16)
            EMIT("sync", _syx)
        tp = tap("ckv", [128, 4 * S], BF16)
        if tp is not None:
            def _syc(sy):
                sA.wait(sy)
                qL.inc(sy.dma_start(tp.ap(), ABF.v(O_CKV, 8192)), 16)
            EMIT("sync", _syc)
        tp = tap("krope", [128, NH * S], BF16)
        if tp is not None:
            def _syr(sy):
                sV.wait(sy)
                qL.inc(sy.dma_start(tp.ap(), ABF.v(O_K, 32768)), 16)
            EMIT("sync", _syr)

    O_V, O_WUK, O_WUV = 40960, 73728, 79872

    def p1b():
        def _g(g):
            src = w["Wuk"].ap().rearrange("(ct p) n -> p ct n", p=128)
            qC.inc(g.dma_start(
                ABF.v(O_WUK, 6144).rearrange("p (ct n) -> p ct n", n=1536),
                src), 16)
            src = w["Wuv"].ap().rearrange("(ct p) n -> p ct n", p=128)
            qC.inc(g.dma_start(
                ABF.v(O_WUV, 8192).rearrange("p (ct n) -> p ct n", n=2048),
                src), 16)
        EMIT("gpsimd", _g)
        mk_w = qC.n

        for h in range(NH):
            for kc in range(4):
                psm = PS[kc % 2]
                def _t(t, h=h, kc=kc, psm=psm):
                    qC.wait(t, mk_w)
                    sA.wait(t, st.get(f"psw{kc % 2}", 0))
                    for ct in range(4):
                        sP.inc(t.matmul(
                            psm[0:96, 0:512],
                            ABF.v(O_WUK + ct * 1536 + h * 96, 96),
                            ABF.v(O_CKV + ct * 2048 + kc * 512, 512),
                            start=(ct == 0), stop=(ct == 3)))
                EMIT("tensor", _t)
                mk_mm = sP.n
                def _a(a, h=h, kc=kc, psm=psm, mk_mm=mk_mm):
                    sP.wait(a, mk_mm)
                    bap = sb_small["buk"]
                    dst0 = Kv(0, 64, h * 2048 + kc * 512, 512)
                    dst1 = Kv(64, 96, h * 2048 + kc * 512, 512)
                    sA.inc(a.activation(dst0, psm[0:64, 0:512], AF.Identity,
                                        bias=bap[0:64, h:h + 1], scale=1.0))
                    sA.inc(a.activation(dst1, psm[64:96, 0:512], AF.Identity,
                                        bias=bap[64:96, h:h + 1], scale=1.0))
                    st[f"psw{kc % 2}"] = sA.n
                EMIT("scalar", _a)
        tp = tap("K", [128, NH * S], BF16)
        if tp is not None:
            def _syk(sy):
                sA.wait(sy)
                sV.wait(sy)
                qL.inc(sy.dma_start(tp.ap(), ABF.v(O_K, 32768)), 16)
            EMIT("sync", _syk)

        for kt in range(NT):
            for dc in range(4):
                psm = PS[2 + dc % 2]
                def _t(t, kt=kt, dc=dc, psm=psm):
                    qC.wait(t, mk_w)
                    sV.wait(t, st.get(f"vw{dc % 2}", 0))
                    for ct in range(4):
                        sP.inc(t.matmul(
                            psm[:, 0:512],
                            ABF.v(O_CKV + ct * 2048 + kt * 128, 128),
                            ABF.v(O_WUV + ct * 2048 + dc * 512, 512),
                            start=(ct == 0), stop=False))
                    sP.inc(t.matmul(psm[:, 0:512], sb_small["onesb"][:],
                                    sb_small["buvb"][:, dc * 512:(dc + 1) * 512],
                                    start=False, stop=True))
                EMIT("tensor", _t)
                mk_mm = sP.n
                def _v(v, kt=kt, dc=dc, psm=psm, mk_mm=mk_mm):
                    sP.wait(v, mk_mm)
                    dst = ABF.v(O_V + kt * 2048 + dc * 512, 512)
                    sV.inc(v.tensor_copy(dst, psm[:, 0:512]))
                    st[f"vw{dc % 2}"] = sV.n
                EMIT("vector", _v)
        tp = tap("V", [128, NT * D], BF16)
        if tp is not None:
            def _syv(sy):
                sV.wait(sy)
                qL.inc(sy.dma_start(tp.ap(), ABF.v(O_V, 32768)), 16)
            EMIT("sync", _syv)

    O_CQ, O_WUQ = 32768, 34816
    O_OST = 32768
    O_QALL = 73728
    O_PRJ = 73728
    O_PROBS = 81920
    O_EDMP = O_PROBS + 1024
    O_WOA, O_WOB = 81920, 83968
    O_CQ2 = 86016
    O_WQR = 88064
    O_CS = 90112
    O_SINQ = 32768
    O_LND1, O_LND2 = 32768, 36864
    A_SCL, A_XRE, A_RSB, A_LROW, A_RROW = 0, 2048, 4096, 4608, 5120

    def p1c():
        n_qc = int(st.get("p1c_qcs", 4))
        n_heads = int(st.get("p1c_heads", NH))
        stage = st.get("p1c_stage", "full")

        def load_cq(qcn, war_pe, war_dve):
            q0n = qcn * 512
            def _sy(sy, qcn=qcn, q0n=q0n, war_pe=war_pe, war_dve=war_dve):
                sP.wait(sy, war_pe)
                sV.wait(sy, war_dve)
                qL.inc(sy.dma_start(
                    ABF.v(O_CQ2, 2048).rearrange("p (c t) -> p c t", t=512),
                    cq_scr.ap()[:, :, q0n:q0n + 512].rearrange(
                        "c p t -> p c t")), 16)
                qL.inc(sy.dma_start(ABF.v(O_CS, 512, 0, 32),
                                    con["cosk"].ap()[0:32, q0n:q0n + 512]), 16)
            EMIT("sync", _sy)
            def _g(g, qcn=qcn, war_pe=war_pe, war_dve=war_dve):
                sP.wait(g, war_pe)
                sV.wait(g, war_dve)
                src = w["Wqr"].ap().rearrange("(ct p) n -> p ct n", p=128)
                qC.inc(g.dma_start(
                    ABF.v(O_WQR, 2048).rearrange("p (ct n) -> p ct n",
                                                 n=512), src), 16)
            EMIT("gpsimd", _g)
            st[f"ldcq{qcn}"] = (qL.n, qC.n)

        def load_wuq(qcn, war_pe):
            q0n = qcn * 512
            def _g(g, war_pe=war_pe, q0n=q0n):
                sP.wait(g, war_pe)
                src = w["Wuq"].ap().rearrange("(ct p) n -> p ct n", p=128)
                qC.inc(g.dma_start(
                    ABF.v(O_WUQ, 6144).rearrange("p (ct n) -> p ct n",
                                                 n=1536), src), 16)
                qC.inc(g.dma_start(ABF.v(O_SINQ, 512, 0, 32),
                                   con["sink"].ap()[0:32, q0n:q0n + 512]), 16)
            EMIT("gpsimd", _g)
            st[f"ldwuq{qcn}"] = qC.n

        def emit_prj_tp(dc, mk_pev):
            p0t, v0t = sP.n, sV.n
            def _t9(t, dc=dc, mk_pev=mk_pev, v0t=v0t):
                sA.wait(t, mk_pev)
                for ts in range(4):
                    sV.wait(t, v0t if ts < 2 else v0t + ts - 1)
                    sP.inc(t.transpose(
                        PT[ts % 2][:, 0:128],
                        sb_small["pevac"][:, (dc % 2) * 512 + ts * 128:
                                          (dc % 2) * 512 + (ts + 1) * 128],
                        sb_small["identb"][:]))
            EMIT("tensor", _t9)
            def _v8(v, dc=dc, p0t=p0t):
                for ts in range(4):
                    sP.wait(v, p0t + ts + 1)
                    sV.inc(v.tensor_copy(
                        ABF.v(O_PRJ + ts * 2048 + dc * 128, 128),
                        PT[ts % 2][:, 0:128]))
                st[f"projtmp{dc % 2}"] = sV.n
            EMIT("vector", _v8)

        def emit_finish(fh, fmk_pv):
            def _fa6(a, fmk_pv=fmk_pv):
                sP.wait(a, fmk_pv)
                sP.wait(a, st.get("p2_bc", 0))
                sA.inc(a.activation(AFP.v(A_LROW, 512, 0, 1),
                                    PS5[0:1, 0:512], AF.Identity,
                                    bias=0.0, scale=1.0))
            EMIT("scalar", _fa6)
            mk_le = sA.n
            def _ft7(t, mk_le=mk_le):
                sA.wait(t, mk_le)
                sV.wait(t, st.get("p2_rec", 0))
                sP.inc(t.matmul(PS5[:, 0:512], sb_small["onesf"][:],
                                AFP.v(A_LROW, 512, 0, 1),
                                start=True, stop=True))
            EMIT("tensor", _ft7)
            st["p2_bc"] = sP.n
            mk_rb = sP.n
            war_rsb = st.get("p2_ov", 0)
            def _fv5(v, mk_rb=mk_rb, war_rsb=war_rsb):
                sP.wait(v, mk_rb)
                sV.wait(v, war_rsb)
                sV.inc(v.reciprocal(AFP.v(A_RSB, 512), PS5[:, 0:512]))
            EMIT("vector", _fv5)
            st["p2_rec"] = sV.n
            mk_rsb = sV.n
            def _fv7(v, fh=fh, fmk_pv=fmk_pv, mk_rsb=mk_rsb):
                sP.wait(v, fmk_pv)
                sV.wait(v, mk_rsb)
                sV.inc(v.tensor_mul(ABF.v(O_OST + fh * 512, 512),
                                    PS[4], AFP.v(A_RSB, 512)))
            EMIT("vector", _fv7)
            st["p2_ov"] = sV.n
            st["reclps"] = sA.n

        load_cq(0, 0, 0)
        load_wuq(0, 0)
        st["fin"] = None
        for qc in range(n_qc):
            q0 = qc * 512
            mk_ld = (st[f"ldcq{qc}"][0],
                     max(st[f"ldcq{qc}"][1], st[f"ldwuq{qc}"]))

            def emit_rope(h, mk_qev):
                pr2 = PS[2 + h % 2]
                qrt = ABF.v(O_PROBS + (h % 2) * 512, 512, 0, 32)
                def _t2(t, pr2=pr2, mk_qev=mk_qev, qrt=qrt):
                    sA.wait(t, mk_qev)
                    sP.inc(t.matmul(pr2[0:32, 0:512],
                                    sb_small["mrotT"][0:32, 0:32],
                                    qrt, start=True, stop=True))
                EMIT("tensor", _t2)
                mk_rmm = sP.n
                def _a2(a, pr2=pr2, mk_rmm=mk_rmm, h=h):
                    sP.wait(a, mk_rmm)
                    sV.wait(a, st.get(f"qb_vv{h % 2}", 0))
                    sA.inc(a.activation(ABF.v(O_EDMP + (h % 2) * 512, 512, 0, 32),
                                        pr2[0:32, 0:512], AF.Identity,
                                        bias=0.0, scale=1.0))
                EMIT("scalar", _a2)
                st[f"qb_ps{h % 2}"] = sA.n
                mk_rev = sA.n
                def _v(v, h=h, mk_rev=mk_rev, qrt=qrt):
                    sA.wait(v, mk_rev)
                    ro = ABF.v(O_EDMP + (h % 2) * 512, 512, 0, 32)
                    v.tensor_mul(qrt, qrt, ABF.v(O_CS, 512, 0, 32))
                    v.tensor_mul(ro, ro, ABF.v(O_SINQ, 512, 0, 32))
                    sV.inc(v.tensor_add(ABF.v(O_QALL + h * 512, 512, 96, 128),
                                        qrt, ro))
                EMIT("vector", _v)
                st[f"qb_vv{h % 2}"] = sV.n
                st["qb_v"] = sV.n

            for h in range(NH):
                pb, pr2 = PS[h % 2], PS[2 + h % 2]
                qrt = ABF.v(O_PROBS + (h % 2) * 512, 512, 0, 32)
                war_ps = st.get(f"qb_ps{h % 2}", 0)
                war_v = st.get(f"qb_vv{h % 2}", 0)
                def _t(t, h=h, pb=pb, pr2=pr2, war_ps=war_ps):
                    qL.wait(t, mk_ld[0])
                    qC.wait(t, mk_ld[1])
                    sA.wait(t, war_ps)
                    for ct in range(4):
                        sP.inc(t.matmul(
                            pb[0:96, 0:512],
                            ABF.v(O_WUQ + ct * 1536 + h * 96, 96),
                            ABF.v(O_CQ2 + ct * 512, 512),
                            start=(ct == 0), stop=(ct == 3)))
                    for ct in range(4):
                        sP.inc(t.matmul(
                            pr2[0:32, 0:512],
                            ABF.v(O_WQR + ct * 512 + h * 32, 32),
                            ABF.v(O_CQ2 + ct * 512, 512),
                            start=(ct == 0), stop=(ct == 3)))
                EMIT("tensor", _t)
                mk_qmm = sP.n
                pend = st.pop("qbpend", None)
                if pend is not None:
                    emit_rope(*pend)
                def _a(a, h=h, pb=pb, pr2=pr2, mk_qmm=mk_qmm, war_v=war_v,
                       qrt=qrt):
                    sP.wait(a, mk_qmm)
                    sV.wait(a, war_v)
                    qL.wait(a, st.get("prjspill", 0))
                    sA.inc(a.activation(
                        ABF.v(O_QALL + h * 512, 512, 0, 64), pb[0:64, 0:512],
                        AF.Identity, bias=sb_small["buq"][0:64, h:h + 1],
                        scale=1.0))
                    sA.inc(a.activation(
                        ABF.v(O_QALL + h * 512, 512, 64, 96), pb[64:96, 0:512],
                        AF.Identity, bias=sb_small["buq"][64:96, h:h + 1],
                        scale=1.0))
                    sA.inc(a.activation(
                        qrt, pr2[0:32, 0:512],
                        AF.Identity, bias=sb_small["bqr"][:, h:h + 1], scale=1.0))
                EMIT("scalar", _a)
                st["qbpend"] = (h, sA.n)
            emit_rope(*st.pop("qbpend"))
            st["qb_end"] = sP.n

            if stage == "qb":
                tp = tap_d.get("Qall")
                if tp is not None and qc == int(st.get("tap_qc", 0)):
                    def _syq(sy):
                        sV.wait(sy)
                        sA.wait(sy)
                        qL.inc(sy.dma_start(tp.ap(), ABF.v(O_QALL, 8192)), 16)
                    EMIT("sync", _syq)
                continue

            for h in range(n_heads):
                mk_qb = st["qb_v"]
                p2start = sP.n
                SP2 = (PP0, PP1)
                def pvl(t, j, h, war_ov, war_rcl):
                    if j == 0:
                        sV.wait(t, war_ov)
                        sV.wait(t, war_rcl)
                    po = O_PROBS + ((j // 2) % 3) * 1024 + (j % 2) * 512
                    sP.inc(t.matmul(
                        PS[4], ABF.v(O_V + j * 2048 + h * 128, 128),
                        ABF.v(po, 512), start=(j == 0), stop=(j == 15)))
                    sP.inc(t.matmul(
                        PS5[0:1, 0:512], sb_small["onescol"][:],
                        ABF.v(po, 512), start=(j == 0), stop=(j == 15),
                        skip_group_check=True))
                def _t5a(t, h=h, mk_qb=mk_qb):
                    sV.wait(t, mk_qb)
                    for p in range(2):
                        for s in range(2):
                            sP.inc(t.matmul(
                                SP2[p % 2][:, s * 512:(s + 1) * 512],
                                Kv(0, 128, h * 2048 + (2 * p + s) * 128, 128),
                                ABF.v(O_QALL + h * 512, 512),
                                start=True, stop=True))
                EMIT("tensor", _t5a)
                if st.get("fin") is not None:
                    emit_finish(*st["fin"])
                    st["fin"] = None
                war_ov = st.get("p2_ov", 0)
                war_rcl = st.get("p2_rec", 0)
                p2b = sP.n
                a0 = sA.n
                def _t5b(t, h=h, a0=a0, war_ov=war_ov, war_rcl=war_rcl):
                    for p in range(2, 8):
                        sA.wait(t, a0 + p - 1)
                        for s in range(2):
                            sP.inc(t.matmul(
                                SP2[p % 2][:, s * 512:(s + 1) * 512],
                                Kv(0, 128, h * 2048 + (2 * p + s) * 128, 128),
                                ABF.v(O_QALL + h * 512, 512),
                                start=True, stop=True))
                        pvl(t, 2 * p - 4, h, war_ov, war_rcl)
                        pvl(t, 2 * p - 3, h, war_ov, war_rcl)
                    sA.wait(t, a0 + 7)
                    pvl(t, 12, h, war_ov, war_rcl)
                    pvl(t, 13, h, war_ov, war_rcl)
                    sA.wait(t, a0 + 8)
                    pvl(t, 14, h, war_ov, war_rcl)
                    pvl(t, 15, h, war_ov, war_rcl)
                EMIT("tensor", _t5b)
                mk_pv = sP.n
                def _a4(a, a0=a0, p2start=p2start, p2b=p2b):
                    for p in range(8):
                        if p < 2:
                            sP.wait(a, p2start + 2 * (p + 1))
                        else:
                            sP.wait(a, p2b + 6 * p - 10)
                        sA.inc(a.activation(
                            ABF.v(O_PROBS + (p % 3) * 1024, 1024),
                            SP2[p % 2][:, 0:1024], AF.Exp,
                            bias=0.0, scale=CSC))
                EMIT("scalar", _a4)
                st["fin"] = (h, mk_pv)
                if h == 8 and qc + 1 < n_qc:
                    load_cq(qc + 1, st["qb_end"], st["qb_v"])
            if st.get("fin") is not None:
                emit_finish(*st["fin"])
                st["fin"] = None

            tp = tap_d.get("Ost")
            if tp is not None and qc == int(st.get("tap_qc", 0)):
                def _syo(sy):
                    sV.wait(sy)
                    qL.inc(sy.dma_start(tp.ap(), ABF.v(O_OST, 8192)), 16)
                EMIT("sync", _syo)
                st["ost_tapped"] = qL.n
            if stage == "p2":
                continue

            mk_allO = st["p2_ov"]
            for dc in range(16):
                wob = O_WOA if dc % 2 == 0 else O_WOB
                war_wo = st.get(f"wow{dc % 2}", 0)
                def _sy2(sy, dc=dc, wob=wob, war_wo=war_wo, mk_allO=mk_allO):
                    sP.wait(sy, war_wo)
                    sV.wait(sy, mk_allO)
                    qB.wait(sy, st.get("p0wo", 0))
                    qL.inc(sy.dma_start(
                        ABF.v(wob, 2048),
                        wo_bf.ap()[dc].rearrange("p h j -> p (h j)")), 16)
                EMIT("sync", _sy2)
                mk_wol = qL.n
                wps = PS[4] if dc % 2 == 0 else PS[5]
                war_ps3 = st.get(f"projps{dc % 2}", 0)
                def _t8(t, dc=dc, wob=wob, wps=wps, mk_wol=mk_wol,
                        war_ps3=war_ps3, mk_allO=mk_allO):
                    qL.wait(t, mk_wol)
                    sV.wait(t, mk_allO)
                    sA.wait(t, war_ps3)
                    for h in range(NH):
                        sP.inc(t.matmul(wps[:, 0:512],
                                        ABF.v(wob + h * 128, 128),
                                        ABF.v(O_OST + h * 512, 512),
                                        start=(h == 0), stop=(h == 15)))
                    st[f"wow{dc % 2}"] = sP.n
                EMIT("tensor", _t8)
                pend = st.pop("wopend", None)
                if pend is not None:
                    emit_prj_tp(*pend)
                mk_wmm = sP.n
                def _a6(a, dc=dc, wps=wps, mk_wmm=mk_wmm):
                    sP.wait(a, mk_wmm)
                    sV.wait(a, st.get(f"projtmp{dc % 2}", 0))
                    sA.inc(a.activation(
                        sb_small["pevac"][:, (dc % 2) * 512:(dc % 2) * 512 + 512],
                        wps[:, 0:512],
                        AF.Identity,
                        bias=sb_small["bo"][:, dc:dc + 1],
                        scale=1.0))
                EMIT("scalar", _a6)
                st[f"projps{dc % 2}"] = sA.n
                st["wopend"] = (dc, sA.n)
                if dc == 15 and qc + 1 < n_qc:
                    load_wuq(qc + 1, sP.n)
            emit_prj_tp(*st.pop("wopend"))

            mk_prj = sV.n
            def _sy3(sy, q0=q0, mk_prj=mk_prj):
                sV.wait(sy, mk_prj)
                qL.wait(sy, st.get("ost_tapped", 0))
                for ts in range(4):
                    qL.inc(sy.dma_start(
                        proj_scr.ap()[q0 + ts * 128:q0 + (ts + 1) * 128, :],
                        ABF.v(O_PRJ + ts * 2048, 2048)), 16)
            EMIT("sync", _sy3)
            st["prjspill"] = qL.n
            st["p1c_qread"] = sP.n

    tap("x1", [S, D], F32)
    tap("Qall", [128, 8192], BF16)
    tap("Ost", [128, 8192], BF16)

    F_XFM2, F_XHAD, F_HBUF, F_XLN, F_DMP1, F_DMP2 = (0, 32768, 65536, 69632,
                                                     71680, 73728)
    F_XLN2, F_LNA, F_PRJT = 75776, 77824, 79872

    def f1():
        for tt in range(NT):
            xb = AFP.v((tt % 2) * 2048, 2048)
            prjt = ABF.v(F_PRJT + (tt % 2) * 2048, 2048)
            war = st.get(f"f1xb{tt % 2}", 0)
            war_p = st.get(f"f1pj{tt % 2}", 0)
            def _sy(sy, tt=tt, xb=xb, war=war):
                sA.wait(sy, war)
                qL.inc(sy.dma_start(xb, x_d.ap()[tt * 128:(tt + 1) * 128, :]),
                       16)
            EMIT("sync", _sy)
            def _gp(g, tt=tt, prjt=prjt, war_p=war_p):
                sA.wait(g, war_p)
                qC.inc(g.dma_start(
                    prjt, proj_scr.ap()[tt * 128:(tt + 1) * 128, :]), 16)
            EMIT("gpsimd", _gp)
            ln_deps = [(qC, qC.n), (qL, qL.n)]
            mk_lna = emit_ln(prjt, ABF.v(F_LNA, 2048), ABF.v(F_DMP1, 2048),
                             ABF.v(F_DMP2, 2048), ln_deps,
                             out_war=[(sV, st.get("f1add", 0))])
            st[f"f1pj{tt % 2}"] = mk_lna
            def _vadd(v, xb=xb, mk_lna=mk_lna):
                sA.wait(v, mk_lna)
                qL.wait(v)
                sV.inc(v.tensor_add(xb, xb, ABF.v(F_LNA, 2048)))
            EMIT("vector", _vadd)
            st["f1add"] = sV.n
            mk_x1 = sV.n
            def _syx1(sy, tt=tt, xb=xb, mk_x1=mk_x1):
                sV.wait(sy, mk_x1)
                qL.inc(sy.dma_start(x1_scr.ap()[tt * 128:(tt + 1) * 128, :],
                                    xb), 16)
                if "x1" in taps:
                    qL.inc(sy.dma_start(
                        tap_d["x1"].ap()[tt * 128:(tt + 1) * 128, :], xb), 16)
            EMIT("sync", _syx1)
            xln = ABF.v(F_XLN if tt % 2 == 0 else F_XLN2, 2048)
            mk_ln = emit_ln(xb, xln, ABF.v(F_DMP1, 2048), ABF.v(F_DMP2, 2048),
                            [(sV, mk_x1)],
                            out_war=[(sP, st.get(f"f1xlnP{tt % 2}", 0))])
            st[f"f1xb{tt % 2}"] = mk_ln
            p0t, v0t = sP.n, sV.n
            def _t(t, xln=xln, mk_ln=mk_ln, v0t=v0t):
                sA.wait(t, mk_ln)
                for ft in range(16):
                    sV.wait(t, v0t if ft < 2 else v0t + ft - 1)
                    sP.inc(t.transpose(PT[ft % 2][:, 0:128],
                                       xln[:, ft * 128:(ft + 1) * 128],
                                       sb_small["identb"][:]))
            EMIT("tensor", _t)
            st[f"f1xlnP{tt % 2}"] = sP.n
            def _v(v, tt=tt, p0t=p0t):
                for ft in range(16):
                    sP.wait(v, p0t + ft + 1)
                    sV.inc(v.tensor_scalar(
                        ABF.v(F_XFM2 + ft * 2048 + tt * 128, 128),
                        PT[ft % 2][:, 0:128],
                        sb_small["suug"][:, ft:ft + 1], None, op0=ALU.mult))
            EMIT("vector", _v)
        mk_xfm2 = sV.n

        for fpc in range(16):
            hb = ABF.v(F_HBUF + (fpc % 2) * 2048, 2048)
            war = st.get(f"f1hb{fpc % 2}", 0)
            def _sy2(sy, fpc=fpc, hb=hb, war=war):
                sP.wait(sy, war)
                qL.inc(sy.dma_start(
                    hb.rearrange("p (ft j) -> p ft j", j=128),
                    con["h2048t"].ap()[fpc]), 16)
            EMIT("sync", _sy2)
            mk_h = qL.n
            for tc in range(4):
                psm = PS[tc % 2]
                war_ps = st.get(f"f1ps{tc % 2}", 0)
                def _t2(t, fpc=fpc, tc=tc, hb=hb, psm=psm, mk_h=mk_h,
                        war_ps=war_ps, mk_xfm2=mk_xfm2):
                    qL.wait(t, mk_h)
                    sV.wait(t, mk_xfm2)
                    sA.wait(t, war_ps)
                    for ft in range(16):
                        sP.inc(t.matmul(
                            psm[:, 0:512],
                            hb[:, ft * 128:(ft + 1) * 128],
                            ABF.v(F_XFM2 + ft * 2048 + tc * 512, 512),
                            start=(ft == 0), stop=(ft == 15)))
                EMIT("tensor", _t2)
                st[f"f1hb{fpc % 2}"] = sP.n
                mk_mm = sP.n
                def _a(a, fpc=fpc, tc=tc, psm=psm, mk_mm=mk_mm):
                    sP.wait(a, mk_mm)
                    sA.inc(a.activation(
                        ABF.v(F_XHAD + fpc * 2048 + tc * 512, 512),
                        psm[:, 0:512], AF.Identity, bias=0.0,
                        scale=1.0 / math.sqrt(2048.0)))
                    st[f"f1ps{tc % 2}"] = sA.n
                EMIT("scalar", _a)
        tp = tap_d.get("xhad")
        if tp is not None:
            def _syx(sy):
                sA.wait(sy)
                qL.inc(sy.dma_start(tp.ap(), ABF.v(F_XHAD, 32768)), 16)
            EMIT("sync", _syx)

    F_WUP, F_WG, F_USB, F_GSB, F_YSB, F_ZSB = (0, 4096, 8192, 9216, 10240, 11264)

    def f2():
        def emit_h128(Ap, tcp):
            it = Ap * 4 + tcp
            mk_y = st["f2y"]
            def _t2(t, it=it, mk_y=mk_y):
                sV.wait(t, mk_y)
                sA.wait(t, st.get("f2zev", 0))
                sP.inc(t.matmul(PS[4], sb_small["h128"][:],
                                ABF.v(F_YSB + (it % 2) * 512, 512),
                                start=True, stop=True))
            EMIT("tensor", _t2)
            st["f2hps"] = sP.n
            mk_z = sP.n
            zsb = ABF.v(F_ZSB + (it % 2) * 512, 512)
            war_z = st.get(f"f2z{it % 2}", 0)
            def _a2(a, mk_z=mk_z, zsb=zsb, war_z=war_z):
                sP.wait(a, mk_z)
                qL.wait(a, war_z)
                sA.inc(a.activation(zsb, PS[4], AF.Identity,
                                    bias=0.0, scale=1.0 / math.sqrt(128.0)))
            EMIT("scalar", _a2)
            st["f2zev"] = sA.n
            mk_ze = sA.n
            def _sy(sy, Ap=Ap, tcp=tcp, zsb=zsb, mk_ze=mk_ze):
                sA.wait(sy, mk_ze)
                qL.inc(sy.dma_start(
                    z_scr.ap()[Ap, :, tcp * 512:(tcp + 1) * 512], zsb), 16)
            EMIT("sync", _sy)
            st[f"f2z{it % 2}"] = qL.n

        for A in range(64):
            wub = ABF.v(F_WUP + (A % 2) * 2048, 2048)
            wgb = ABF.v(F_WG + (A % 2) * 2048, 2048)
            war = st.get(f"f2w{A % 2}", 0)
            def _g(g, A=A, wub=wub, wgb=wgb, war=war):
                sP.wait(g, war)
                src = w["WupT"].ap()[:, A * 128:(A + 1) * 128].rearrange(
                    "(ft p) i -> p ft i", p=128)
                qC.inc(g.dma_start(
                    wub.rearrange("p (ft i) -> p ft i", i=128), src), 16)
                src = w["WgT"].ap()[:, A * 128:(A + 1) * 128].rearrange(
                    "(ft p) i -> p ft i", p=128)
                qC.inc(g.dma_start(
                    wgb.rearrange("p (ft i) -> p ft i", i=128), src), 16)
            EMIT("gpsimd", _g)
            mk_w = qC.n
            for tc in range(4):
                it = A * 4 + tc
                bu, bg = PS[2 * (it % 2)], PS[2 * (it % 2) + 1]
                war_u = st.get(f"f2ev{it % 2}", 0)
                def _t(t, tc=tc, wub=wub, wgb=wgb, bu=bu, bg=bg, mk_w=mk_w,
                       war_u=war_u):
                    qC.wait(t, mk_w)
                    sA.wait(t, war_u)
                    for ft in range(16):
                        sP.inc(t.matmul(bu[:, 0:512],
                                        wub[:, ft * 128:(ft + 1) * 128],
                                        ABF.v(F_XHAD + ft * 2048 + tc * 512, 512),
                                        start=(ft == 0), stop=(ft == 15)))
                    for ft in range(16):
                        sP.inc(t.matmul(bg[:, 0:512],
                                        wgb[:, ft * 128:(ft + 1) * 128],
                                        ABF.v(F_XHAD + ft * 2048 + tc * 512, 512),
                                        start=(ft == 0), stop=(ft == 15)))
                EMIT("tensor", _t)
                st[f"f2w{A % 2}"] = sP.n
                mk_ug = sP.n
                if it > 0:
                    emit_h128((it - 1) // 4, (it - 1) % 4)
                war_y = st.get(f"f2yw{it % 2}", 0)
                def _a(a, A=A, it=it, bu=bu, bg=bg, mk_ug=mk_ug, war_y=war_y):
                    sP.wait(a, mk_ug - 16)
                    sV.wait(a, war_y)
                    sA.inc(a.activation(ABF.v(F_USB + (it % 2) * 512, 512),
                                        bu[:, 0:512],
                                        AF.Identity, bias=0.0,
                                        scale=sb_small["wsu"][:, A:A + 1]))
                    sP.wait(a, mk_ug)
                    sA.inc(a.activation(ABF.v(F_GSB + (it % 2) * 512, 512),
                                        bg[:, 0:512],
                                        AF.Silu, bias=0.0,
                                        scale=sb_small["wsg"][:, A:A + 1]))
                EMIT("scalar", _a)
                st[f"f2ev{it % 2}"] = sA.n
                mk_ug2 = sA.n
                def _v(v, A=A, it=it, mk_ug2=mk_ug2):
                    sA.wait(v, mk_ug2)
                    sP.wait(v, st.get("f2hps", 0))
                    sV.inc(v.scalar_tensor_tensor(
                        ABF.v(F_YSB + (it % 2) * 512, 512),
                        ABF.v(F_USB + (it % 2) * 512, 512),
                        sb_small["sudp"][:, A:A + 1],
                        ABF.v(F_GSB + (it % 2) * 512, 512),
                        op0=ALU.mult, op1=ALU.mult))
                EMIT("vector", _v)
                st[f"f2yw{it % 2}"] = sV.n
                st["f2y"] = sV.n
        emit_h128(63, 3)

    F_ZLD, F_YST = 0, 49152

    def f3a():
        for bg in range(4):
            for tc in range(4):
                it = bg * 4 + tc
                zb = ABF.v(F_ZLD + (it % 3) * 16384, 16384)
                ys = F_YST + (it % 2) * 16384
                war = st.get(f"f3az{it % 3}", 0)
                def _g(g, bg=bg, tc=tc, zb=zb, war=war):
                    sP.wait(g, war)
                    qC.inc(g.dma_start(
                        zb.rearrange("p (b t) -> p b t", t=512)[0:64],
                        z_scr.ap()[:, bg * 32:(bg + 1) * 32,
                                   tc * 512:(tc + 1) * 512]), 16)
                EMIT("gpsimd", _g)
                mk_zl = qC.n
                p0f = sP.n
                a0f, v0f = sA.n, sV.n
                war_st = st.get(f"f3stg{it % 2}", 0)
                def _t(t, zb=zb, mk_zl=mk_zl, a0f=a0f, v0f=v0f):
                    qC.wait(t, mk_zl)
                    sA.wait(t, st.get("f3aev", 0))
                    sV.wait(t, st.get("f3avv", 0))
                    zv = zb.rearrange("p (b t) -> p b t", t=512)
                    for b2 in range(32):
                        if b2 >= 4:
                            if (b2 - 4) % 2 == 0:
                                sA.wait(t, a0f + (b2 - 4) // 2 + 1)
                            else:
                                sV.wait(t, v0f + (b2 - 4) // 2 + 1)
                        sP.inc(t.matmul(PS[b2 % 4][0:64, 0:512],
                                        sb_small["h64"][:], zv[0:64, b2],
                                        start=True, stop=True))
                EMIT("tensor", _t)
                st[f"f3az{it % 3}"] = sP.n
                def _a(a, p0f=p0f, war_st=war_st, ys=ys):
                    qL.wait(a, war_st)
                    for b2 in range(0, 32, 2):
                        sP.wait(a, p0f + b2 + 1)
                        sA.inc(a.activation(ABF.v(ys + b2 * 512, 512, 0, 64),
                                            PS[b2 % 4][0:64, 0:512], AF.Identity,
                                            bias=0.0, scale=0.125))
                EMIT("scalar", _a)
                def _v(v, p0f=p0f, war_st=war_st, ys=ys):
                    qL.wait(v, war_st)
                    for b2 in range(1, 32, 2):
                        sP.wait(v, p0f + b2 + 1)
                        sV.inc(v.tensor_scalar(
                            ABF.v(ys + b2 * 512, 512, 0, 64),
                            PS[b2 % 4][0:64, 0:512], 0.125, None, op0=ALU.mult))
                EMIT("vector", _v)
                st["f3aev"], st["f3avv"] = sA.n, sV.n
                mk_ev, mk_vv = sA.n, sV.n
                def _sy2(sy, bg=bg, tc=tc, mk_ev=mk_ev, mk_vv=mk_vv, ys=ys):
                    sA.wait(sy, mk_ev)
                    sV.wait(sy, mk_vv)
                    qL.inc(sy.dma_start(
                        yt_scr.ap()[:, bg * 32:(bg + 1) * 32,
                                    tc * 512:(tc + 1) * 512],
                        ABF.v(ys, 16384).rearrange(
                            "p (b t) -> p b t", t=512)[0:64]), 16)
                EMIT("sync", _sy2)
                st[f"f3stg{it % 2}"] = qL.n

    F_YT, F_WDN, F_FFM, F_FTM, F_LD1, F_LD2 = (0, 32768, 57344,
                                               65536, 73728, 75776)

    def f3b():
        def emit_ffm_tp(dc, mk_ev):
            p0t, v0t = sP.n, sV.n
            def _t2(t, dc=dc, mk_ev=mk_ev, v0t=v0t):
                sA.wait(t, mk_ev)
                for ts in range(4):
                    sV.wait(t, v0t if ts < 2 else v0t + ts - 1)
                    sP.inc(t.transpose(
                        PT[ts % 2][:, 0:128],
                        ABF.v(F_FFM + dc * 512 + ts * 128, 128),
                        sb_small["identb"][:]))
            EMIT("tensor", _t2)
            def _v(v, dc=dc, p0t=p0t):
                sA.wait(v, st.get("f3bln", 0))
                for ts in range(4):
                    sP.wait(v, p0t + ts + 1)
                    sV.inc(v.tensor_copy(
                        ABF.v(F_FTM + ts * 2048 + dc * 128, 128),
                        PT[ts % 2][:, 0:128]))
            EMIT("vector", _v)

        def emit_wload(tc, dc):
            it = tc * 16 + dc
            wdb = ABF.v(F_WDN + (it % 3) * 8192, 8192)
            war_w = st.get(f"f3bw{it % 3}", 0)
            wq = qC if dc % 2 == 0 else qL
            def _sy2(sy, dc=dc, wdb=wdb, war_w=war_w, wq=wq):
                sP.wait(sy, war_w)
                qB.wait(sy, st.get("p0done", 0))
                wq.inc(sy.dma_start(
                    wdb.rearrange("p (a j) -> p a j", j=128),
                    wdn_bf.ap()[dc]), 16)
            EMIT("gpsimd" if dc % 2 == 0 else "sync", _sy2)
            return (wq, wq.n)

        for tc in range(4):
            war = st.get("f3byt", 0)
            wl = {}
            for dc0 in range(3):
                wl[dc0] = emit_wload(tc, dc0)
            def _sy(sy, tc=tc, war=war):
                sP.wait(sy, war)
                for ck in range(4):
                    qC.inc(sy.dma_start(
                        ABF.v(F_YT + ck * 8192, 8192).rearrange(
                            "p (a t) -> p a t", t=512),
                        yt_scr.ap()[ck * 16:(ck + 1) * 16, :,
                                    tc * 512:(tc + 1) * 512].rearrange(
                            "a b t -> b a t")), 16)
            EMIT("gpsimd", _sy)
            mk_yt = qC.n
            mk_yts = [mk_yt - 48, mk_yt - 32, mk_yt - 16, mk_yt]
            for dc in range(16):
                it = tc * 16 + dc
                wdb = ABF.v(F_WDN + (it % 3) * 8192, 8192)
                wq, mk_wl = wl[dc]
                psm = PS[dc % 2]
                war_ps = st.get(f"f3bps{dc % 2}", 0)
                def _t(t, dc=dc, wdb=wdb, psm=psm, mk_wl=mk_wl, mk_yt=mk_yt,
                       war_ps=war_ps, wq=wq, mk_yts=mk_yts):
                    wq.wait(t, mk_wl)
                    sA.wait(t, war_ps)
                    if dc > 0:
                        qC.wait(t, mk_yt)
                    for a2 in range(64):
                        if dc == 0 and a2 % 16 == 0:
                            qC.wait(t, mk_yts[a2 // 16])
                        sP.inc(t.matmul(psm[:, 0:512],
                                        wdb[:, a2 * 128:(a2 + 1) * 128],
                                        ABF.v(F_YT + a2 * 512, 512),
                                        start=(a2 == 0), stop=(a2 == 63)))
                    st[f"f3bw{it % 3}"] = sP.n
                EMIT("tensor", _t)
                if dc + 3 <= 15:
                    wl[dc + 3] = emit_wload(tc, dc + 3)
                pend = st.pop("f3bpend", None)
                if pend is not None:
                    emit_ffm_tp(*pend)
                mk_mm = sP.n
                def _a(a, dc=dc, psm=psm, mk_mm=mk_mm):
                    sP.wait(a, mk_mm)
                    sA.inc(a.activation(ABF.v(F_FFM + dc * 512, 512),
                                        psm[:, 0:512], AF.Identity, bias=0.0,
                                        scale=sb_small["wdp"][:, dc:dc + 1]))
                    st[f"f3bps{dc % 2}"] = sA.n
                EMIT("scalar", _a)
                st["f3bpend"] = (dc, sA.n)
            emit_ffm_tp(*st.pop("f3bpend"))
            mk_ftm = sV.n
            for ts in range(4):
                r0 = tc * 512 + ts * 128
                def _sy3(sy, r0=r0):
                    qL.wait(sy, st.get("f3bst", 0))
                    sV.wait(sy, st.get("f3badd", 0))
                    qL.inc(sy.dma_start(AFP.v(2048, 2048),
                                        x1_scr.ap()[r0:r0 + 128, :]), 16)
                EMIT("sync", _sy3)
                deps = [(sV, mk_ftm), (qL, qL.n)]
                mk_ln = emit_ln(ABF.v(F_FTM + ts * 2048, 2048),
                                AFP.v(0, 2048), ABF.v(F_LD1, 2048),
                                ABF.v(F_LD2, 2048), deps,
                                out_war=[(sV, st.get("f3badd", 0))])
                def _v2(v, mk_ln=mk_ln):
                    sA.wait(v, mk_ln)
                    qL.wait(v)
                    sV.inc(v.tensor_add(AFP.v(2048, 2048), AFP.v(2048, 2048),
                                        AFP.v(0, 2048)))
                EMIT("vector", _v2)
                st["f3badd"] = sV.n
                mk_x2 = sV.n
                def _sy4(sy, r0=r0, mk_x2=mk_x2):
                    sV.wait(sy, mk_x2)
                    qL.inc(sy.dma_start(out_d.ap()[r0:r0 + 128, :],
                                        AFP.v(2048, 2048)), 16)
                EMIT("sync", _sy4)
                st["f3bst"] = qL.n
                st["f3bln"] = mk_ln
            st["f3byt"] = sP.n

    tap("xhad", [128, 32768], BF16)

    phases = [("p0", p0), ("p1a", p1a), ("p1b", p1b), ("p1c", p1c),
              ("f1", f1), ("f2", f2), ("f3a", f3a), ("f3b", f3b)]
    PHASE_MARKS.clear()
    for name, fn in phases:
        st["phase"] = name
        fn()
        phase_fence()
        PHASE_MARKS[name] = nc.n_instructions()
        if stop_after == name:
            break

    def _fin(sy):
        for s in ALLS:
            s.wait(sy)
        qB.wait(sy)
    EMIT("sync", _fin)
    blk.__exit__(None, None, None)
    check_deadlock()
    return nc, tap_d



def _prep_inputs(inputs):
    g = lambda k: np.asarray(inputs[k])
    hc = _host_consts()
    per_core_common = {
        "Wdkv": g("L0_attn_W_dkv"), "Wdq": g("L0_attn_W_dq"),
        "Wuk": g("L0_attn_W_uk"), "Wuv": g("L0_attn_W_uv"),
        "Wuq": g("L0_attn_W_uq"), "Wqr": g("L0_attn_W_qr"),
        "Wkr": g("L0_attn_W_kr"), "Wo": g("L0_attn_W_o"),
        "WupT": np.ascontiguousarray(g("L0_ffn_W_up").T),
        "WgT": np.ascontiguousarray(g("L0_ffn_W_gate").T),
        "WdnT": np.ascontiguousarray(g("L0_ffn_W_down").T),
        "bdkv": np.ascontiguousarray(g("L0_attn_b_dkv").reshape(4, 128).T),
        "bdq": np.ascontiguousarray(g("L0_attn_b_dq").reshape(4, 128).T),
        "bkr": np.ascontiguousarray(g("L0_attn_b_kr").reshape(4, 128).T),
        "buk": np.ascontiguousarray(g("L0_attn_b_uk").reshape(16, 96).T),
        "buq": np.ascontiguousarray(g("L0_attn_b_uq").reshape(16, 96).T),
        "bqr": np.ascontiguousarray(g("L0_attn_b_qr").reshape(16, 32).T),
        "buv": np.ascontiguousarray(g("L0_attn_b_uv").reshape(1, 2048)),
        "bo": np.ascontiguousarray(g("L0_attn_b_o").reshape(16, 128).T),
        "suug": np.ascontiguousarray(g("L0_ffn_SU_ug").reshape(16, 128).T),
        "wsu": np.ascontiguousarray(
            g("L0_ffn_Wscale_ug")[:INTER].reshape(64, 128).T),
        "wsg": np.ascontiguousarray(
            g("L0_ffn_Wscale_ug")[INTER:].reshape(64, 128).T),
        "sudp": np.ascontiguousarray(g("L0_ffn_SU_dp").reshape(64, 128).T),
        "wdp": np.ascontiguousarray(g("L0_ffn_Wscale_dp").reshape(16, 128).T),
    }
    for k in ["h128", "h64", "h2048t", "mrotT", "cosk", "sink",
              "onesb", "onesf", "onescol", "identb", "identf"]:
        per_core_common[k] = hc[k]
    x = np.asarray(inputs["L0_attn_x"])
    in_maps = []
    for c in range(8):
        m = dict(per_core_common)
        m["x"] = np.ascontiguousarray(x[c])
        in_maps.append(m)
    return in_maps


def kernel(**inputs):
    nc, _ = build_program()
    in_maps = _prep_inputs(inputs)
    res = run_bass_kernel_spmd(nc, in_maps, core_ids=list(range(8)))
    return np.stack([np.asarray(r["out"], dtype=np.float32)
                     for r in res.results], axis=0)


# version 14
# speedup vs baseline: 1.2186x; 1.0814x over previous
import math
import numpy as np
import ml_dtypes
import concourse.bass as bass
import concourse.mybir as mybir
from concourse.bass_utils import run_bass_kernel_spmd

F32 = mybir.dt.float32
BF16 = mybir.dt.bfloat16
AF = mybir.ActivationFunctionType
ALU = mybir.AluOpType
AX = mybir.AxisListType
BF = ml_dtypes.bfloat16

S = 2048
D = 2048
NH = 16
DKV = 512
DR = 32
DH = 128
SP = 96
INTER = 8192
NT = 16
EPS = 1e-5
CSC = 1.0 / math.sqrt(128.0)
RSC = math.sqrt(128.0)



def _hadamard(n):
    h = np.array([[1.0]], dtype=np.float64)
    while h.shape[0] < n:
        h = np.block([[h, h], [h, -h]])
    return h.astype(np.float32)


def _host_consts():
    c = {}
    c["h128"] = _hadamard(128).astype(BF)
    c["h64"] = _hadamard(64).astype(BF)
    H = _hadamard(2048)
    c["h2048t"] = np.ascontiguousarray(
        H.reshape(16, 128, 16, 128).transpose(2, 1, 0, 3)).astype(BF)
    R = np.zeros((32, 32), dtype=np.float32)
    for i in range(16):
        R[i, i + 16] = -1.0
        R[i + 16, i] = 1.0
    M = np.zeros((128, 128), dtype=np.float32)
    for b in range(4):
        M[b * 32:(b + 1) * 32, b * 32:(b + 1) * 32] = R
    c["mrotT"] = M.T.copy().astype(BF)
    inv_freq = 1.0 / 10000.0 ** (np.arange(0, DR, 2, dtype=np.float64) / DR)
    pos = np.arange(S, dtype=np.float64)
    freqs = pos[:, None] * inv_freq[None, :]
    emb = np.concatenate([freqs, freqs], axis=1)
    cos_fm = np.cos(emb).T.astype(np.float32)
    sin_fm = np.sin(emb).T.astype(np.float32)
    c["cosk"] = np.tile(cos_fm, (4, 1)).astype(BF)
    c["sink"] = np.tile(sin_fm, (4, 1)).astype(BF)
    c["onesb"] = np.ones((1, 128), dtype=BF)
    c["onesf"] = np.ones((1, 128), dtype=np.float32)
    c["onescol"] = np.ones((128, 1), dtype=BF)
    c["identb"] = np.eye(128, dtype=BF)
    c["identf"] = np.eye(128, dtype=np.float32)
    return c



TRACE = {}
CUR_ENG = [None]
TAG = [""]
PHASE_MARKS = {}


class Sem:
    def __init__(self, nc, name):
        self.h = nc.semaphore(name).__enter__()
        self.name = name
        self.n = 0
        self._waited = {}

    def inc(self, instr, k=1):
        instr.then_inc(self.h, k)
        self.n += k
        if CUR_ENG[0] is not None:
            TRACE.setdefault(CUR_ENG[0], []).append(
                ("i", self.name, k, TAG[0]))
        return self.n

    def wait(self, eng, n=None):
        n = self.n if n is None else n
        if n <= 0:
            return
        key = str(eng.engine)
        if self._waited.get(key, -1) < n:
            eng.wait_ge(self.h, n)
            self._waited[key] = n
            if CUR_ENG[0] is not None:
                TRACE.setdefault(CUR_ENG[0], []).append(
                    ("w", self.name, n, TAG[0]))


def check_deadlock():
    pos = {e: 0 for e in TRACE}
    cnt = {}
    progress = True
    while progress:
        progress = False
        for e, ops in TRACE.items():
            while pos[e] < len(ops):
                kind, sname, val = ops[pos[e]][:3]
                if kind == "i":
                    cnt[sname] = cnt.get(sname, 0) + val
                    pos[e] += 1
                    progress = True
                elif cnt.get(sname, 0) >= val:
                    pos[e] += 1
                    progress = True
                else:
                    break
    stuck = {e: (pos[e], len(TRACE[e]), TRACE[e][pos[e]] if pos[e] < len(TRACE[e]) else None)
             for e in TRACE if pos[e] < len(TRACE[e])}
    if stuck:
        msg = [f"DEADLOCK: engine {e} stuck at op {p}/{n} waiting {w} (have {cnt.get(w[1], 0) if w else '?'})"
               for e, (p, n, w) in stuck.items()]
        raise RuntimeError("\n".join(msg))


class Arena:

    def __init__(self, nc, name, nelem, dtype):
        self.t = nc.alloc_sbuf_tensor(name, [128, nelem], dtype)
        self.n = nelem

    def v(self, off, n, p0=0, p1=128):
        assert 0 <= off and off + n <= self.n, (off, n, self.n)
        return self.t[p0:p1, off:off + n]


def build_program(taps=(), stop_after=None, opts=None):
    taps = set(taps)
    TRACE.clear()
    nc = bass.Bass("TRN2", target_bir_lowering=False)
    hc = _host_consts()

    x_d = nc.dram_tensor("x", [S, D], F32, kind="ExternalInput")
    w = {}
    for name, shape in [
        ("Wdkv", [D, DKV]), ("Wdq", [D, DKV]), ("Wuk", [DKV, NH * SP]),
        ("Wuv", [DKV, NH * DH]), ("Wuq", [DKV, NH * SP]), ("Wqr", [DKV, NH * DR]),
        ("Wkr", [D, NH * DR]), ("Wo", [D, D]),
        ("WupT", [D, INTER]), ("WgT", [D, INTER]), ("WdnT", [INTER, D]),
    ]:
        w[name] = nc.dram_tensor(name, shape, F32, kind="ExternalInput")
    bia = {}
    for name, shape in [
        ("bdkv", [128, 4]), ("bdq", [128, 4]), ("bkr", [128, 4]),
        ("buk", [96, 16]), ("buq", [96, 16]), ("bqr", [32, 16]),
        ("buv", [1, D]), ("bo", [128, 16]),
        ("suug", [128, 16]), ("wsu", [128, 64]), ("wsg", [128, 64]),
        ("sudp", [128, 64]), ("wdp", [128, 16]),
    ]:
        bia[name] = nc.dram_tensor(name, shape, F32, kind="ExternalInput")
    con = {}
    for name in ["h128", "h64", "h2048t", "mrotT", "cosk", "sink",
                 "onesb", "onesf", "onescol", "identb", "identf"]:
        arr = hc[name]
        dt = BF16 if arr.dtype == BF else F32
        con[name] = nc.dram_tensor(name, list(arr.shape), dt, kind="ExternalInput")

    out_d = nc.dram_tensor("out", [S, D], F32, kind="ExternalOutput")

    wo_bf = nc.dram_tensor("wo_bf", [16, 128, 16, 128], BF16)
    wdn_bf = nc.dram_tensor("wdn_bf", [16, 128, 64, 128], BF16)
    cq_scr = nc.dram_tensor("cq_scr", [4, 128, S], BF16)
    x1_scr = nc.dram_tensor("x1_scr", [S, D], F32)
    proj_scr = nc.dram_tensor("proj_scr", [S, D], BF16)
    z_scr = nc.dram_tensor("z_scr", [64, 128, S], BF16)
    yt_scr = nc.dram_tensor("yt_scr", [64, 128, S], BF16)

    tap_d = {}
    def tap(name, shape, dtype=F32):
        if name in taps:
            tap_d[name] = nc.dram_tensor("tap_" + name, list(shape), dtype,
                                         kind="ExternalOutput")
            return tap_d[name]
        return None

    sb_small = {}
    for name, shape, dt in [
        ("onesb", [1, 128], BF16), ("onesf", [1, 128], F32),
        ("onescol", [128, 1], BF16),
        ("identb", [128, 128], BF16),
        ("mrotT", [128, 128], BF16), ("h128", [128, 128], BF16),
        ("h64", [64, 64], BF16), ("eps", [128, 1], F32),
        ("bdkv", [128, 4], F32), ("bdq", [128, 4], F32), ("bkr", [128, 4], F32),
        ("buk", [96, 16], F32), ("buq", [96, 16], F32), ("bqr", [32, 16], F32),
        ("bo", [128, 16], F32), ("suug", [128, 16], F32),
        ("wsu", [128, 64], F32), ("wsg", [128, 64], F32),
        ("sudp", [128, 64], F32), ("wdp", [128, 16], F32),
        ("buvb", [1, D], BF16),
        ("pevac", [128, 1024], BF16),
        ("stat", [128, 16], F32),
    ]:
        sb_small[name] = nc.alloc_sbuf_tensor("s_" + name, shape, dt)

    ABF = Arena(nc, "ABF", 90624, BF16)
    AFP = Arena(nc, "AFP", 5120, F32)

    PP0 = nc.alloc_psum_tensor("pp0", [128, 1024], F32)
    PP1 = nc.alloc_psum_tensor("pp1", [128, 1024], F32)
    PS4 = nc.alloc_psum_tensor("ps4", [128, 512], F32)
    PS5 = nc.alloc_psum_tensor("ps5", [128, 512], F32)
    PS = [PP0[:, 0:512], PP0[:, 512:1024], PP1[:, 0:512], PP1[:, 512:1024],
          PS4[:, 0:512], PS5[:, 0:512]]
    PT = [nc.alloc_psum_tensor(f"pt{i}", [128, 1024], BF16) for i in range(2)]

    qL = Sem(nc, "qL")
    qC = Sem(nc, "qC")
    sP = Sem(nc, "sP")
    sA = Sem(nc, "sA")
    sV = Sem(nc, "sV")
    sG = Sem(nc, "sG")
    qB = Sem(nc, "qB")

    ALLS = [qL, qC, sP, sA, sV, sG]

    st = dict(phase="")
    if opts:
        st.update(opts)

    blk = nc.Block("main").__enter__()

    def EMIT(engine_name, fn):
        CUR_ENG[0] = engine_name
        TAG[0] = st.get("phase", "") + ":" + fn.__qualname__.split(".")[-1]
        getattr(blk, engine_name)(fn)
        CUR_ENG[0] = None

    def phase_fence():
        for en in ["sync", "gpsimd", "tensor", "vector", "scalar"]:
            def _f(e):
                for s in ALLS:
                    s.wait(e)
            EMIT(en, _f)

    def p0():
        def _sy(sy):
            for name in ["onesb", "onesf", "onescol", "identb",
                         "mrotT", "h128", "h64"]:
                qL.inc(sy.dma_start(sb_small[name][:], con[name].ap()), 16)
            for name in ["bdkv", "bdq", "bkr", "buk", "buq", "bqr", "bo",
                         "suug", "wsu", "wsg", "sudp", "wdp"]:
                qL.inc(sy.dma_start(sb_small[name][:], bia[name].ap()), 16)
        EMIT("sync", _sy)

        buv_f = AFP.v(0, D, 0, 1)
        def _sy2(sy):
            qL.inc(sy.dma_start(buv_f, bia["buv"].ap()), 16)
        EMIT("sync", _sy2)
        def _v(v):
            qL.wait(v)
            v.memset(sb_small["eps"][:], EPS)
            sV.inc(v.tensor_copy(sb_small["buvb"][:], buv_f))
        EMIT("vector", _v)


    def _ln_cols(sbase):
        stt = sb_small["stat"]
        return (stt[:, sbase:sbase + 1], stt[:, sbase + 1:sbase + 2],
                stt[:, sbase + 2:sbase + 3], stt[:, sbase + 3:sbase + 4],
                stt[:, sbase + 4:sbase + 5])

    def emit_ln_stats(in_ap, dump1, dump2, deps, sbase=0):
        mean, e2, nvar, rstd, nmr = _ln_cols(sbase)
        def _v(v):
            for s, m in deps:
                s.wait(v, m)
            sV.inc(v.tensor_scalar(dump1, in_ap, 1.0 / D, None,
                                   op0=ALU.mult, op1=ALU.add, accum_out=mean))
        EMIT("vector", _v)
        def _a(a):
            for s, m in deps:
                s.wait(a, m)
            sA.inc(a.activation(dump2, in_ap, AF.Square,
                                bias=0.0, scale=1.0 / math.sqrt(D),
                                accum_out=e2))
        EMIT("scalar", _a)
        return (sV.n, sA.n)

    def emit_ln_tail(in_ap, out_ap, marks, out_war=(), sbase=0):
        mean, e2, nvar, rstd, nmr = _ln_cols(sbase)
        mk_v, mk_a = marks
        def _v2(v):
            sV.wait(v, mk_v)
            sA.wait(v, mk_a)
            sV.inc(v.tensor_scalar(nvar, mean, mean, e2,
                                   op0=ALU.mult, op1=ALU.subtract))
        EMIT("vector", _v2)
        mk_v2 = sV.n
        def _a2(a):
            sV.wait(a, mk_v2)
            sA.inc(a.activation(nvar, nvar, AF.Ln,
                                bias=sb_small["eps"][:], scale=-1.0))
        EMIT("scalar", _a2)
        def _a3(a):
            sA.inc(a.activation(rstd, nvar, AF.Exp, bias=0.0, scale=-0.5))
        EMIT("scalar", _a3)
        mk_a3 = sA.n
        def _v3(v):
            sA.wait(v, mk_a3)
            sV.inc(v.tensor_scalar(nmr, mean, rstd, -1.0,
                                   op0=ALU.mult, op1=ALU.mult))
        EMIT("vector", _v3)
        mk_v3 = sV.n
        def _a4(a):
            sV.wait(a, mk_v3)
            for s, m in out_war:
                s.wait(a, m)
            sA.inc(a.activation(out_ap, in_ap, AF.Identity,
                                bias=nmr, scale=rstd))
        EMIT("scalar", _a4)
        return sA.n

    def emit_ln(in_ap, out_ap, dump1, dump2, deps, out_war=(), sbase=0):
        marks = emit_ln_stats(in_ap, dump1, dump2, deps, sbase)
        return emit_ln_tail(in_ap, out_ap, marks, out_war, sbase)

    O_K, O_CKV, O_WDKV, O_WDQ, O_WKR = 0, 32768, 40960, 49152, 57344
    O_XLN, O_DSQ, O_XFM, O_CQS, O_KRT, O_ROT = (65536, 67584, 69632, 77824,
                                                79872, 81920)
    O_COSK, O_SINK = 83968, 86016
    O_DSQ2 = 88064

    def Kv(p0, p1, off, n):
        return ABF.v(O_K + off, n, p0, p1)

    def p1a():
        def _g(g):
            for nm, off in [("Wdkv", O_WDKV), ("Wdq", O_WDQ), ("Wkr", O_WKR)]:
                src = w[nm].ap().rearrange("(ft p) c -> p ft c", p=128)
                qC.inc(g.dma_start(
                    ABF.v(off, 8192).rearrange("p (ft c) -> p ft c", c=512),
                    src), 16)
        EMIT("gpsimd", _g)
        def _sy(sy):
            qL.inc(sy.dma_start(ABF.v(O_COSK, 2048), con["cosk"].ap()), 16)
            qL.inc(sy.dma_start(ABF.v(O_SINK, 2048), con["sink"].ap()), 16)
        EMIT("sync", _sy)

        def _gc(g):
            for dc in range(16):
                src = w["Wo"].ap()[:, dc * 128:(dc + 1) * 128].rearrange(
                    "(h p) j -> p h j", p=128)
                qB.inc(g.dma_start(wo_bf.ap()[dc], src), 16)
            for dc in range(16):
                src = w["WdnT"].ap()[:, dc * 128:(dc + 1) * 128].rearrange(
                    "(a p) j -> p a j", p=128)
                qB.inc(g.dma_start(wdn_bf.ap()[dc], src), 16)
        EMIT("gpsimd", _gc)
        st["p0wo"] = 16 * 16
        st["p0done"] = qB.n

        for tg in range(4):
            for pt in range(4):
                tt = tg * 4 + pt
                xb = AFP.v((tt % 2) * 2048, 2048)
                war = st.get(f"xbw{tt % 2}", 0)
                def _sy2(sy, tt=tt, xb=xb, war=war):
                    sA.wait(sy, war)
                    qL.inc(sy.dma_start(xb, x_d.ap()[tt * 128:(tt + 1) * 128, :]), 16)
                EMIT("sync", _sy2)
                ln_deps = [(qL, qL.n), (qC, qC.n)]
                xln = ABF.v(O_XLN, 2048)
                mk_ln = emit_ln(xb, xln, ABF.v(O_DSQ, 2048),
                                ABF.v(O_DSQ2, 2048), ln_deps,
                                out_war=[(sP, st.get("xlnP", 0))])
                st[f"xbw{tt % 2}"] = mk_ln
                p0, v0 = sP.n, sV.n
                def _t(t, xln=xln, mk_ln=mk_ln, v0=v0):
                    sA.wait(t, mk_ln)
                    for ft in range(16):
                        sV.wait(t, v0 if ft < 2 else v0 + ft - 1)
                        sP.inc(t.transpose(PT[ft % 2][:, 0:128],
                                           xln[:, ft * 128:(ft + 1) * 128],
                                           sb_small["identb"][:]))
                EMIT("tensor", _t)
                st["xlnP"] = sP.n
                def _v(v, pt=pt, p0=p0):
                    for ft in range(16):
                        sP.wait(v, p0 + ft + 1)
                        sV.inc(v.tensor_copy(
                            ABF.v(O_XFM + ft * 512 + pt * 128, 128),
                            PT[ft % 2][:, 0:128]))
                EMIT("vector", _v)

            mk_xfm = sV.n
            for kind, woff, boff in [("ckv", O_WDKV, "bdkv"),
                                     ("cq", O_WDQ, "bdq"),
                                     ("kr", O_WKR, "bkr")]:
                for ct in range(4):
                    psm = PS[ct % 2]
                    war_ps = st.get(f"psw{ct % 2}", 0)
                    def _t(t, woff=woff, ct=ct, psm=psm, mk_xfm=mk_xfm,
                           war_ps=war_ps):
                        sV.wait(t, mk_xfm)
                        sA.wait(t, war_ps)
                        for ft in range(16):
                            sP.inc(t.matmul(
                                psm[:, 0:512],
                                ABF.v(woff + ft * 512 + ct * 128, 128),
                                ABF.v(O_XFM + ft * 512, 512),
                                start=(ft == 0), stop=(ft == 15)))
                    EMIT("tensor", _t)
                    mk_mm = sP.n
                    war_cq = st.get("cqspill", 0)
                    war_kr = st.get("ropeV", 0)
                    def _a(a, kind=kind, ct=ct, tg=tg, psm=psm, boff=boff,
                           mk_mm=mk_mm, war_cq=war_cq, war_kr=war_kr):
                        sP.wait(a, mk_mm)
                        if kind == "cq":
                            qL.wait(a, war_cq)
                            dst = ABF.v(O_CQS + ct * 512, 512)
                        elif kind == "kr":
                            sV.wait(a, war_kr)
                            dst = ABF.v(O_KRT + ct * 512, 512)
                        else:
                            dst = ABF.v(O_CKV + ct * 2048 + tg * 512, 512)
                        bap = sb_small[boff][:, ct:ct + 1]
                        sA.inc(a.activation(dst, psm[:, 0:512], AF.Identity,
                                            bias=bap, scale=1.0))
                        st[f"psw{ct % 2}"] = sA.n
                    EMIT("scalar", _a)
            mk_cq = sA.n
            def _sy3(sy, tg=tg, mk_cq=mk_cq):
                sA.wait(sy, mk_cq)
                qL.inc(sy.dma_start(
                    cq_scr.ap()[:, :, tg * 512:(tg + 1) * 512].rearrange(
                        "c p t -> p c t"),
                    ABF.v(O_CQS, 2048).rearrange("p (c t) -> p c t", t=512)), 16)
            EMIT("sync", _sy3)
            st["cqspill"] = qL.n

            war_rot = st.get("rotw", 0)
            a0 = sA.n
            def _t2(t, mk_cq=mk_cq, war_rot=war_rot, a0=a0):
                sA.wait(t, mk_cq)
                sA.wait(t, war_rot)
                for rt in range(4):
                    if rt >= 2:
                        sA.wait(t, a0 + rt - 1)
                    sP.inc(t.matmul(PS[2 + rt % 2][:, 0:512],
                                    sb_small["mrotT"][:],
                                    ABF.v(O_KRT + rt * 512, 512),
                                    start=True, stop=True))
            EMIT("tensor", _t2)
            mk_rot = sP.n
            war_ropeV = st.get("ropeV", 0)
            def _a5(a, mk_rot=mk_rot, war_ropeV=war_ropeV):
                sV.wait(a, war_ropeV)
                for rt in range(4):
                    sP.wait(a, mk_rot - 3 + rt)
                    sA.inc(a.activation(ABF.v(O_ROT + rt * 512, 512),
                                        PS[2 + rt % 2][:, 0:512], AF.Identity,
                                        bias=0.0, scale=1.0))
            EMIT("scalar", _a5)
            st["rotw"] = sA.n
            mk_rotc = sA.n
            def _v2(v, tg=tg, mk_rotc=mk_rotc):
                sA.wait(v, mk_rotc)
                for rt in range(4):
                    kr = ABF.v(O_KRT + rt * 512, 512)
                    ro = ABF.v(O_ROT + rt * 512, 512)
                    ck = ABF.v(O_COSK + tg * 512, 512)
                    sk = ABF.v(O_SINK + tg * 512, 512)
                    v.tensor_mul(kr, kr, ck)
                    v.tensor_mul(ro, ro, sk)
                    sV.inc(v.tensor_add(kr, kr, ro))
                for h in range(NH):
                    rt, j = h // 4, h % 4
                    src = ABF.v(O_KRT + rt * 512, 512, j * 32, (j + 1) * 32)
                    dst = Kv(96, 128, h * 2048 + tg * 512, 512)
                    sV.inc(v.tensor_copy(dst, src))
            EMIT("vector", _v2)
            st["ropeV"] = sV.n

        tp = tap("xfm", [128, 16 * 512], BF16)
        if tp is not None:
            def _syx(sy):
                sV.wait(sy)
                qL.inc(sy.dma_start(tp.ap(), ABF.v(O_XFM, 8192)), 16)
            EMIT("sync", _syx)
        tp = tap("ckv", [128, 4 * S], BF16)
        if tp is not None:
            def _syc(sy):
                sA.wait(sy)
                qL.inc(sy.dma_start(tp.ap(), ABF.v(O_CKV, 8192)), 16)
            EMIT("sync", _syc)
        tp = tap("krope", [128, NH * S], BF16)
        if tp is not None:
            def _syr(sy):
                sV.wait(sy)
                qL.inc(sy.dma_start(tp.ap(), ABF.v(O_K, 32768)), 16)
            EMIT("sync", _syr)

    O_V, O_WUK, O_WUV = 40960, 73728, 79872

    def p1b():
        def _g(g):
            src = w["Wuk"].ap().rearrange("(ct p) n -> p ct n", p=128)
            qC.inc(g.dma_start(
                ABF.v(O_WUK, 6144).rearrange("p (ct n) -> p ct n", n=1536),
                src), 16)
            src = w["Wuv"].ap().rearrange("(ct p) n -> p ct n", p=128)
            qC.inc(g.dma_start(
                ABF.v(O_WUV, 8192).rearrange("p (ct n) -> p ct n", n=2048),
                src), 16)
        EMIT("gpsimd", _g)
        mk_w = qC.n

        for h in range(NH):
            for kc in range(4):
                psm = PS[kc % 2]
                def _t(t, h=h, kc=kc, psm=psm):
                    qC.wait(t, mk_w)
                    sA.wait(t, st.get(f"psw{kc % 2}", 0))
                    for ct in range(4):
                        sP.inc(t.matmul(
                            psm[0:96, 0:512],
                            ABF.v(O_WUK + ct * 1536 + h * 96, 96),
                            ABF.v(O_CKV + ct * 2048 + kc * 512, 512),
                            start=(ct == 0), stop=(ct == 3)))
                EMIT("tensor", _t)
                mk_mm = sP.n
                def _a(a, h=h, kc=kc, psm=psm, mk_mm=mk_mm):
                    sP.wait(a, mk_mm)
                    bap = sb_small["buk"]
                    dst0 = Kv(0, 64, h * 2048 + kc * 512, 512)
                    dst1 = Kv(64, 96, h * 2048 + kc * 512, 512)
                    sA.inc(a.activation(dst0, psm[0:64, 0:512], AF.Identity,
                                        bias=bap[0:64, h:h + 1], scale=1.0))
                    sA.inc(a.activation(dst1, psm[64:96, 0:512], AF.Identity,
                                        bias=bap[64:96, h:h + 1], scale=1.0))
                    st[f"psw{kc % 2}"] = sA.n
                EMIT("scalar", _a)
        tp = tap("K", [128, NH * S], BF16)
        if tp is not None:
            def _syk(sy):
                sA.wait(sy)
                sV.wait(sy)
                qL.inc(sy.dma_start(tp.ap(), ABF.v(O_K, 32768)), 16)
            EMIT("sync", _syk)

        for kt in range(NT):
            for dc in range(4):
                psm = PS[2 + dc % 2]
                def _t(t, kt=kt, dc=dc, psm=psm):
                    qC.wait(t, mk_w)
                    sV.wait(t, st.get(f"vw{dc % 2}", 0))
                    for ct in range(4):
                        sP.inc(t.matmul(
                            psm[:, 0:512],
                            ABF.v(O_CKV + ct * 2048 + kt * 128, 128),
                            ABF.v(O_WUV + ct * 2048 + dc * 512, 512),
                            start=(ct == 0), stop=False))
                    sP.inc(t.matmul(psm[:, 0:512], sb_small["onesb"][:],
                                    sb_small["buvb"][:, dc * 512:(dc + 1) * 512],
                                    start=False, stop=True))
                EMIT("tensor", _t)
                mk_mm = sP.n
                def _v(v, kt=kt, dc=dc, psm=psm, mk_mm=mk_mm):
                    sP.wait(v, mk_mm)
                    dst = ABF.v(O_V + kt * 2048 + dc * 512, 512)
                    sV.inc(v.tensor_copy(dst, psm[:, 0:512]))
                    st[f"vw{dc % 2}"] = sV.n
                EMIT("vector", _v)
        tp = tap("V", [128, NT * D], BF16)
        if tp is not None:
            def _syv(sy):
                sV.wait(sy)
                qL.inc(sy.dma_start(tp.ap(), ABF.v(O_V, 32768)), 16)
            EMIT("sync", _syv)

    O_CQ, O_WUQ = 32768, 34816
    O_OST = 32768
    O_QALL = 73728
    O_PRJ = 73728
    O_PROBS = 81920
    O_EDMP = O_PROBS + 1024
    O_WOA, O_WOB = 81920, 83968
    O_CQ2 = 86016
    O_WQR = 88064
    O_CS = 90112
    O_SINQ = 32768
    O_LND1, O_LND2 = 32768, 36864
    A_SCL, A_XRE, A_RSB, A_LROW, A_RROW = 0, 2048, 4096, 4608, 5120

    def p1c():
        n_qc = int(st.get("p1c_qcs", 4))
        n_heads = int(st.get("p1c_heads", NH))
        stage = st.get("p1c_stage", "full")

        def load_cq(qcn, war_pe, war_dve):
            q0n = qcn * 512
            def _sy(sy, qcn=qcn, q0n=q0n, war_pe=war_pe, war_dve=war_dve):
                sP.wait(sy, war_pe)
                sV.wait(sy, war_dve)
                qL.inc(sy.dma_start(
                    ABF.v(O_CQ2, 2048).rearrange("p (c t) -> p c t", t=512),
                    cq_scr.ap()[:, :, q0n:q0n + 512].rearrange(
                        "c p t -> p c t")), 16)
                qL.inc(sy.dma_start(ABF.v(O_CS, 512, 0, 32),
                                    con["cosk"].ap()[0:32, q0n:q0n + 512]), 16)
            EMIT("sync", _sy)
            def _g(g, qcn=qcn, war_pe=war_pe, war_dve=war_dve):
                sP.wait(g, war_pe)
                sV.wait(g, war_dve)
                src = w["Wqr"].ap().rearrange("(ct p) n -> p ct n", p=128)
                qC.inc(g.dma_start(
                    ABF.v(O_WQR, 2048).rearrange("p (ct n) -> p ct n",
                                                 n=512), src), 16)
            EMIT("gpsimd", _g)
            st[f"ldcq{qcn}"] = (qL.n, qC.n)

        def load_wuq(qcn, war_pe):
            q0n = qcn * 512
            def _g(g, war_pe=war_pe, q0n=q0n):
                sP.wait(g, war_pe)
                src = w["Wuq"].ap().rearrange("(ct p) n -> p ct n", p=128)
                qC.inc(g.dma_start(
                    ABF.v(O_WUQ, 6144).rearrange("p (ct n) -> p ct n",
                                                 n=1536), src), 16)
                qC.inc(g.dma_start(ABF.v(O_SINQ, 512, 0, 32),
                                   con["sink"].ap()[0:32, q0n:q0n + 512]), 16)
            EMIT("gpsimd", _g)
            st[f"ldwuq{qcn}"] = qC.n

        def emit_prj_tp(dc, mk_pev):
            p0t, v0t = sP.n, sV.n
            def _t9(t, dc=dc, mk_pev=mk_pev, v0t=v0t):
                sA.wait(t, mk_pev)
                for ts in range(4):
                    sV.wait(t, v0t if ts < 2 else v0t + ts - 1)
                    sP.inc(t.transpose(
                        PT[ts % 2][:, 0:128],
                        sb_small["pevac"][:, (dc % 2) * 512 + ts * 128:
                                          (dc % 2) * 512 + (ts + 1) * 128],
                        sb_small["identb"][:]))
            EMIT("tensor", _t9)
            def _v8(v, dc=dc, p0t=p0t):
                for ts in range(4):
                    sP.wait(v, p0t + ts + 1)
                    sV.inc(v.tensor_copy(
                        ABF.v(O_PRJ + ts * 2048 + dc * 128, 128),
                        PT[ts % 2][:, 0:128]))
                st[f"projtmp{dc % 2}"] = sV.n
            EMIT("vector", _v8)

        def emit_finish(fh, fmk_pv):
            def _fa6(a, fmk_pv=fmk_pv):
                sP.wait(a, fmk_pv)
                sP.wait(a, st.get("p2_bc", 0))
                sA.inc(a.activation(AFP.v(A_LROW, 512, 0, 1),
                                    PS5[0:1, 0:512], AF.Identity,
                                    bias=0.0, scale=1.0))
            EMIT("scalar", _fa6)
            mk_le = sA.n
            def _ft7(t, mk_le=mk_le):
                sA.wait(t, mk_le)
                sV.wait(t, st.get("p2_rec", 0))
                sP.inc(t.matmul(PS5[:, 0:512], sb_small["onesf"][:],
                                AFP.v(A_LROW, 512, 0, 1),
                                start=True, stop=True))
            EMIT("tensor", _ft7)
            st["p2_bc"] = sP.n
            mk_rb = sP.n
            war_rsb = st.get("p2_ov", 0)
            def _fv5(v, mk_rb=mk_rb, war_rsb=war_rsb):
                sP.wait(v, mk_rb)
                sV.wait(v, war_rsb)
                sV.inc(v.reciprocal(AFP.v(A_RSB, 512), PS5[:, 0:512]))
            EMIT("vector", _fv5)
            st["p2_rec"] = sV.n
            mk_rsb = sV.n
            def _fv7(v, fh=fh, fmk_pv=fmk_pv, mk_rsb=mk_rsb):
                sP.wait(v, fmk_pv)
                sV.wait(v, mk_rsb)
                sV.inc(v.tensor_mul(ABF.v(O_OST + fh * 512, 512),
                                    PS[4], AFP.v(A_RSB, 512)))
            EMIT("vector", _fv7)
            st["p2_ov"] = sV.n
            st["reclps"] = sA.n

        load_cq(0, 0, 0)
        load_wuq(0, 0)
        st["fin"] = None
        for qc in range(n_qc):
            q0 = qc * 512
            mk_ld = (st[f"ldcq{qc}"][0],
                     max(st[f"ldcq{qc}"][1], st[f"ldwuq{qc}"]))

            def emit_rope(h, mk_qev):
                pr2 = PS[2 + h % 2]
                qrt = ABF.v(O_PROBS + (h % 2) * 512, 512, 0, 32)
                def _t2(t, pr2=pr2, mk_qev=mk_qev, qrt=qrt):
                    sA.wait(t, mk_qev)
                    sP.inc(t.matmul(pr2[0:32, 0:512],
                                    sb_small["mrotT"][0:32, 0:32],
                                    qrt, start=True, stop=True))
                EMIT("tensor", _t2)
                mk_rmm = sP.n
                def _v2e(v, pr2=pr2, mk_rmm=mk_rmm, h=h):
                    sP.wait(v, mk_rmm)
                    sV.inc(v.tensor_copy(
                        ABF.v(O_EDMP + (h % 2) * 512, 512, 0, 32),
                        pr2[0:32, 0:512]))
                EMIT("vector", _v2e)
                st[f"qb_ps{h % 2}"] = sV.n
                def _v(v, h=h, qrt=qrt):
                    ro = ABF.v(O_EDMP + (h % 2) * 512, 512, 0, 32)
                    v.tensor_mul(qrt, qrt, ABF.v(O_CS, 512, 0, 32))
                    v.tensor_mul(ro, ro, ABF.v(O_SINQ, 512, 0, 32))
                    sV.inc(v.tensor_add(ABF.v(O_QALL + h * 512, 512, 96, 128),
                                        qrt, ro))
                EMIT("vector", _v)
                st[f"qb_vv{h % 2}"] = sV.n
                st["qb_v"] = sV.n

            for h in range(NH):
                pb, pr2 = PS[h % 2], PS[2 + h % 2]
                qrt = ABF.v(O_PROBS + (h % 2) * 512, 512, 0, 32)
                war_ps = st.get(f"qb_ps{h % 2}", 0)
                war_v = st.get(f"qb_vv{h % 2}", 0)
                def _t(t, h=h, pb=pb, pr2=pr2, war_ps=war_ps):
                    qL.wait(t, mk_ld[0])
                    qC.wait(t, mk_ld[1])
                    sV.wait(t, war_ps)
                    for ct in range(4):
                        sP.inc(t.matmul(
                            pb[0:96, 0:512],
                            ABF.v(O_WUQ + ct * 1536 + h * 96, 96),
                            ABF.v(O_CQ2 + ct * 512, 512),
                            start=(ct == 0), stop=(ct == 3)))
                    for ct in range(4):
                        sP.inc(t.matmul(
                            pr2[0:32, 0:512],
                            ABF.v(O_WQR + ct * 512 + h * 32, 32),
                            ABF.v(O_CQ2 + ct * 512, 512),
                            start=(ct == 0), stop=(ct == 3)))
                EMIT("tensor", _t)
                mk_qmm = sP.n
                pend = st.pop("qbpend", None)
                if pend is not None:
                    emit_rope(*pend)
                def _a(a, h=h, pb=pb, pr2=pr2, mk_qmm=mk_qmm, war_v=war_v,
                       qrt=qrt):
                    sP.wait(a, mk_qmm)
                    sV.wait(a, war_v)
                    qL.wait(a, st.get("prjspill", 0))
                    sA.inc(a.activation(
                        ABF.v(O_QALL + h * 512, 512, 0, 64), pb[0:64, 0:512],
                        AF.Identity, bias=sb_small["buq"][0:64, h:h + 1],
                        scale=1.0))
                    sA.inc(a.activation(
                        ABF.v(O_QALL + h * 512, 512, 64, 96), pb[64:96, 0:512],
                        AF.Identity, bias=sb_small["buq"][64:96, h:h + 1],
                        scale=1.0))
                    sA.inc(a.activation(
                        qrt, pr2[0:32, 0:512],
                        AF.Identity, bias=sb_small["bqr"][:, h:h + 1], scale=1.0))
                EMIT("scalar", _a)
                st["qbpend"] = (h, sA.n)
            emit_rope(*st.pop("qbpend"))
            st["qb_end"] = sP.n

            if stage == "qb":
                tp = tap_d.get("Qall")
                if tp is not None and qc == int(st.get("tap_qc", 0)):
                    def _syq(sy):
                        sV.wait(sy)
                        sA.wait(sy)
                        qL.inc(sy.dma_start(tp.ap(), ABF.v(O_QALL, 8192)), 16)
                    EMIT("sync", _syq)
                continue

            for h in range(n_heads):
                mk_qb = st["qb_v"]
                p2start = sP.n
                SP2 = (PP0, PP1)
                def pvl(t, j, h, war_ov, war_rcl):
                    if j == 0:
                        sV.wait(t, war_ov)
                        sV.wait(t, war_rcl)
                    po = O_PROBS + ((j // 2) % 3) * 1024 + (j % 2) * 512
                    sP.inc(t.matmul(
                        PS[4], ABF.v(O_V + j * 2048 + h * 128, 128),
                        ABF.v(po, 512), start=(j == 0), stop=(j == 15)))
                    sP.inc(t.matmul(
                        PS5[0:1, 0:512], sb_small["onescol"][:],
                        ABF.v(po, 512), start=(j == 0), stop=(j == 15),
                        skip_group_check=True))
                def _t5a(t, h=h, mk_qb=mk_qb):
                    sV.wait(t, mk_qb)
                    for p in range(2):
                        for s in range(2):
                            sP.inc(t.matmul(
                                SP2[p % 2][:, s * 512:(s + 1) * 512],
                                Kv(0, 128, h * 2048 + (2 * p + s) * 128, 128),
                                ABF.v(O_QALL + h * 512, 512),
                                start=True, stop=True))
                EMIT("tensor", _t5a)
                if st.get("fin") is not None:
                    emit_finish(*st["fin"])
                    st["fin"] = None
                war_ov = st.get("p2_ov", 0)
                war_rcl = st.get("p2_rec", 0)
                p2b = sP.n
                a0 = sA.n
                def _t5b(t, h=h, a0=a0, war_ov=war_ov, war_rcl=war_rcl):
                    for p in range(2, 8):
                        sA.wait(t, a0 + p - 1)
                        for s in range(2):
                            sP.inc(t.matmul(
                                SP2[p % 2][:, s * 512:(s + 1) * 512],
                                Kv(0, 128, h * 2048 + (2 * p + s) * 128, 128),
                                ABF.v(O_QALL + h * 512, 512),
                                start=True, stop=True))
                        pvl(t, 2 * p - 4, h, war_ov, war_rcl)
                        pvl(t, 2 * p - 3, h, war_ov, war_rcl)
                    sA.wait(t, a0 + 7)
                    pvl(t, 12, h, war_ov, war_rcl)
                    pvl(t, 13, h, war_ov, war_rcl)
                    sA.wait(t, a0 + 8)
                    pvl(t, 14, h, war_ov, war_rcl)
                    pvl(t, 15, h, war_ov, war_rcl)
                EMIT("tensor", _t5b)
                mk_pv = sP.n
                def _a4(a, a0=a0, p2start=p2start, p2b=p2b):
                    for p in range(8):
                        if p < 2:
                            sP.wait(a, p2start + 2 * (p + 1))
                        else:
                            sP.wait(a, p2b + 6 * p - 10)
                        sA.inc(a.activation(
                            ABF.v(O_PROBS + (p % 3) * 1024, 1024),
                            SP2[p % 2][:, 0:1024], AF.Exp,
                            bias=0.0, scale=CSC))
                EMIT("scalar", _a4)
                st["fin"] = (h, mk_pv)
                if h == 8 and qc + 1 < n_qc:
                    load_cq(qc + 1, st["qb_end"], st["qb_v"])
            if st.get("fin") is not None:
                emit_finish(*st["fin"])
                st["fin"] = None

            tp = tap_d.get("Ost")
            if tp is not None and qc == int(st.get("tap_qc", 0)):
                def _syo(sy):
                    sV.wait(sy)
                    qL.inc(sy.dma_start(tp.ap(), ABF.v(O_OST, 8192)), 16)
                EMIT("sync", _syo)
                st["ost_tapped"] = qL.n
            if stage == "p2":
                continue

            mk_allO = st["p2_ov"]
            for dc in range(16):
                wob = O_WOA if dc % 2 == 0 else O_WOB
                war_wo = st.get(f"wow{dc % 2}", 0)
                def _sy2(sy, dc=dc, wob=wob, war_wo=war_wo, mk_allO=mk_allO):
                    sP.wait(sy, war_wo)
                    sV.wait(sy, mk_allO)
                    qB.wait(sy, st.get("p0wo", 0))
                    qL.inc(sy.dma_start(
                        ABF.v(wob, 2048),
                        wo_bf.ap()[dc].rearrange("p h j -> p (h j)")), 16)
                EMIT("sync", _sy2)
                mk_wol = qL.n
                wps = PS[4] if dc % 2 == 0 else PS[5]
                war_ps3 = st.get(f"projps{dc % 2}", 0)
                def _t8(t, dc=dc, wob=wob, wps=wps, mk_wol=mk_wol,
                        war_ps3=war_ps3, mk_allO=mk_allO):
                    qL.wait(t, mk_wol)
                    sV.wait(t, mk_allO)
                    sA.wait(t, war_ps3)
                    for h in range(NH):
                        sP.inc(t.matmul(wps[:, 0:512],
                                        ABF.v(wob + h * 128, 128),
                                        ABF.v(O_OST + h * 512, 512),
                                        start=(h == 0), stop=(h == 15)))
                    st[f"wow{dc % 2}"] = sP.n
                EMIT("tensor", _t8)
                pend = st.pop("wopend", None)
                if pend is not None:
                    emit_prj_tp(*pend)
                mk_wmm = sP.n
                def _a6(a, dc=dc, wps=wps, mk_wmm=mk_wmm):
                    sP.wait(a, mk_wmm)
                    sV.wait(a, st.get(f"projtmp{dc % 2}", 0))
                    sA.inc(a.activation(
                        sb_small["pevac"][:, (dc % 2) * 512:(dc % 2) * 512 + 512],
                        wps[:, 0:512],
                        AF.Identity,
                        bias=sb_small["bo"][:, dc:dc + 1],
                        scale=1.0))
                EMIT("scalar", _a6)
                st[f"projps{dc % 2}"] = sA.n
                st["wopend"] = (dc, sA.n)
                if dc == 15 and qc + 1 < n_qc:
                    load_wuq(qc + 1, sP.n)
            emit_prj_tp(*st.pop("wopend"))

            mk_prj = sV.n
            def _sy3(sy, q0=q0, mk_prj=mk_prj):
                sV.wait(sy, mk_prj)
                qL.wait(sy, st.get("ost_tapped", 0))
                for ts in range(4):
                    qL.inc(sy.dma_start(
                        proj_scr.ap()[q0 + ts * 128:q0 + (ts + 1) * 128, :],
                        ABF.v(O_PRJ + ts * 2048, 2048)), 16)
            EMIT("sync", _sy3)
            st["prjspill"] = qL.n
            st["p1c_qread"] = sP.n

    tap("x1", [S, D], F32)
    tap("Qall", [128, 8192], BF16)
    tap("Ost", [128, 8192], BF16)

    F_XFM2, F_XHAD, F_HBUF, F_XLN, F_DMP1, F_DMP2 = (0, 32768, 65536, 69632,
                                                     71680, 73728)
    F_XLN2, F_LNA, F_PRJT = 75776, 77824, 79872

    def f1():
        for tt in range(NT):
            xb = AFP.v((tt % 2) * 2048, 2048)
            prjt = ABF.v(F_PRJT + (tt % 2) * 2048, 2048)
            war = st.get(f"f1xb{tt % 2}", 0)
            war_p = st.get(f"f1pj{tt % 2}", 0)
            def _sy(sy, tt=tt, xb=xb, war=war):
                sA.wait(sy, war)
                qL.inc(sy.dma_start(xb, x_d.ap()[tt * 128:(tt + 1) * 128, :]),
                       16)
            EMIT("sync", _sy)
            def _gp(g, tt=tt, prjt=prjt, war_p=war_p):
                sA.wait(g, war_p)
                qC.inc(g.dma_start(
                    prjt, proj_scr.ap()[tt * 128:(tt + 1) * 128, :]), 16)
            EMIT("gpsimd", _gp)
            ln_deps = [(qC, qC.n), (qL, qL.n)]
            mk_lna = emit_ln(prjt, ABF.v(F_LNA, 2048), ABF.v(F_DMP1, 2048),
                             ABF.v(F_DMP2, 2048), ln_deps,
                             out_war=[(sV, st.get("f1add", 0))])
            st[f"f1pj{tt % 2}"] = mk_lna
            def _vadd(v, xb=xb, mk_lna=mk_lna):
                sA.wait(v, mk_lna)
                qL.wait(v)
                sV.inc(v.tensor_add(xb, xb, ABF.v(F_LNA, 2048)))
            EMIT("vector", _vadd)
            st["f1add"] = sV.n
            mk_x1 = sV.n
            def _syx1(sy, tt=tt, xb=xb, mk_x1=mk_x1):
                sV.wait(sy, mk_x1)
                qL.inc(sy.dma_start(x1_scr.ap()[tt * 128:(tt + 1) * 128, :],
                                    xb), 16)
                if "x1" in taps:
                    qL.inc(sy.dma_start(
                        tap_d["x1"].ap()[tt * 128:(tt + 1) * 128, :], xb), 16)
            EMIT("sync", _syx1)
            xln = ABF.v(F_XLN if tt % 2 == 0 else F_XLN2, 2048)
            mk_ln = emit_ln(xb, xln, ABF.v(F_DMP1, 2048), ABF.v(F_DMP2, 2048),
                            [(sV, mk_x1)],
                            out_war=[(sP, st.get(f"f1xlnP{tt % 2}", 0))])
            st[f"f1xb{tt % 2}"] = mk_ln
            p0t, v0t = sP.n, sV.n
            def _t(t, xln=xln, mk_ln=mk_ln, v0t=v0t):
                sA.wait(t, mk_ln)
                for ft in range(16):
                    sV.wait(t, v0t if ft < 2 else v0t + ft - 1)
                    sP.inc(t.transpose(PT[ft % 2][:, 0:128],
                                       xln[:, ft * 128:(ft + 1) * 128],
                                       sb_small["identb"][:]))
            EMIT("tensor", _t)
            st[f"f1xlnP{tt % 2}"] = sP.n
            def _v(v, tt=tt, p0t=p0t):
                for ft in range(16):
                    sP.wait(v, p0t + ft + 1)
                    sV.inc(v.tensor_scalar(
                        ABF.v(F_XFM2 + ft * 2048 + tt * 128, 128),
                        PT[ft % 2][:, 0:128],
                        sb_small["suug"][:, ft:ft + 1], None, op0=ALU.mult))
            EMIT("vector", _v)
        mk_xfm2 = sV.n

        for fpc in range(16):
            hb = ABF.v(F_HBUF + (fpc % 2) * 2048, 2048)
            war = st.get(f"f1hb{fpc % 2}", 0)
            def _sy2(sy, fpc=fpc, hb=hb, war=war):
                sP.wait(sy, war)
                qL.inc(sy.dma_start(
                    hb.rearrange("p (ft j) -> p ft j", j=128),
                    con["h2048t"].ap()[fpc]), 16)
            EMIT("sync", _sy2)
            mk_h = qL.n
            for tc in range(4):
                psm = PS[tc % 2]
                war_ps = st.get(f"f1ps{tc % 2}", 0)
                def _t2(t, fpc=fpc, tc=tc, hb=hb, psm=psm, mk_h=mk_h,
                        war_ps=war_ps, mk_xfm2=mk_xfm2):
                    qL.wait(t, mk_h)
                    sV.wait(t, mk_xfm2)
                    sA.wait(t, war_ps)
                    for ft in range(16):
                        sP.inc(t.matmul(
                            psm[:, 0:512],
                            hb[:, ft * 128:(ft + 1) * 128],
                            ABF.v(F_XFM2 + ft * 2048 + tc * 512, 512),
                            start=(ft == 0), stop=(ft == 15)))
                EMIT("tensor", _t2)
                st[f"f1hb{fpc % 2}"] = sP.n
                mk_mm = sP.n
                def _a(a, fpc=fpc, tc=tc, psm=psm, mk_mm=mk_mm):
                    sP.wait(a, mk_mm)
                    sA.inc(a.activation(
                        ABF.v(F_XHAD + fpc * 2048 + tc * 512, 512),
                        psm[:, 0:512], AF.Identity, bias=0.0,
                        scale=1.0 / math.sqrt(2048.0)))
                    st[f"f1ps{tc % 2}"] = sA.n
                EMIT("scalar", _a)
        tp = tap_d.get("xhad")
        if tp is not None:
            def _syx(sy):
                sA.wait(sy)
                qL.inc(sy.dma_start(tp.ap(), ABF.v(F_XHAD, 32768)), 16)
            EMIT("sync", _syx)

    F_WUP, F_WG, F_USB, F_GSB, F_YSB, F_ZSB = (0, 4096, 8192, 9216, 10240, 11264)

    def f2():
        def emit_h128(Ap, tcp):
            it = Ap * 4 + tcp
            mk_y = st["f2y"]
            def _t2(t, it=it, mk_y=mk_y):
                sV.wait(t, mk_y)
                sA.wait(t, st.get("f2zev", 0))
                sP.inc(t.matmul(PS[4], sb_small["h128"][:],
                                ABF.v(F_YSB + (it % 2) * 512, 512),
                                start=True, stop=True))
            EMIT("tensor", _t2)
            st["f2hps"] = sP.n
            mk_z = sP.n
            zsb = ABF.v(F_ZSB + (it % 2) * 512, 512)
            war_z = st.get(f"f2z{it % 2}", 0)
            def _a2(a, mk_z=mk_z, zsb=zsb, war_z=war_z):
                sP.wait(a, mk_z)
                qL.wait(a, war_z)
                sA.inc(a.activation(zsb, PS[4], AF.Identity,
                                    bias=0.0, scale=1.0 / math.sqrt(128.0)))
            EMIT("scalar", _a2)
            st["f2zev"] = sA.n
            mk_ze = sA.n
            def _sy(sy, Ap=Ap, tcp=tcp, zsb=zsb, mk_ze=mk_ze):
                sA.wait(sy, mk_ze)
                qL.inc(sy.dma_start(
                    z_scr.ap()[Ap, :, tcp * 512:(tcp + 1) * 512], zsb), 16)
            EMIT("sync", _sy)
            st[f"f2z{it % 2}"] = qL.n

        for A in range(64):
            wub = ABF.v(F_WUP + (A % 2) * 2048, 2048)
            wgb = ABF.v(F_WG + (A % 2) * 2048, 2048)
            war = st.get(f"f2w{A % 2}", 0)
            def _g(g, A=A, wub=wub, wgb=wgb, war=war):
                sP.wait(g, war)
                src = w["WupT"].ap()[:, A * 128:(A + 1) * 128].rearrange(
                    "(ft p) i -> p ft i", p=128)
                qC.inc(g.dma_start(
                    wub.rearrange("p (ft i) -> p ft i", i=128), src), 16)
                src = w["WgT"].ap()[:, A * 128:(A + 1) * 128].rearrange(
                    "(ft p) i -> p ft i", p=128)
                qC.inc(g.dma_start(
                    wgb.rearrange("p (ft i) -> p ft i", i=128), src), 16)
            EMIT("gpsimd", _g)
            mk_w = qC.n
            for tc in range(4):
                it = A * 4 + tc
                bu, bg = PS[2 * (it % 2)], PS[2 * (it % 2) + 1]
                war_u = st.get(f"f2ev{it % 2}", 0)
                def _t(t, tc=tc, wub=wub, wgb=wgb, bu=bu, bg=bg, mk_w=mk_w,
                       war_u=war_u):
                    qC.wait(t, mk_w)
                    sA.wait(t, war_u)
                    for ft in range(16):
                        sP.inc(t.matmul(bu[:, 0:512],
                                        wub[:, ft * 128:(ft + 1) * 128],
                                        ABF.v(F_XHAD + ft * 2048 + tc * 512, 512),
                                        start=(ft == 0), stop=(ft == 15)))
                    for ft in range(16):
                        sP.inc(t.matmul(bg[:, 0:512],
                                        wgb[:, ft * 128:(ft + 1) * 128],
                                        ABF.v(F_XHAD + ft * 2048 + tc * 512, 512),
                                        start=(ft == 0), stop=(ft == 15)))
                EMIT("tensor", _t)
                st[f"f2w{A % 2}"] = sP.n
                mk_ug = sP.n
                if it > 0:
                    emit_h128((it - 1) // 4, (it - 1) % 4)
                war_y = st.get(f"f2yw{it % 2}", 0)
                def _a(a, A=A, it=it, bu=bu, bg=bg, mk_ug=mk_ug, war_y=war_y):
                    sP.wait(a, mk_ug - 16)
                    sV.wait(a, war_y)
                    sA.inc(a.activation(ABF.v(F_USB + (it % 2) * 512, 512),
                                        bu[:, 0:512],
                                        AF.Identity, bias=0.0,
                                        scale=sb_small["wsu"][:, A:A + 1]))
                    sP.wait(a, mk_ug)
                    sA.inc(a.activation(ABF.v(F_GSB + (it % 2) * 512, 512),
                                        bg[:, 0:512],
                                        AF.Silu, bias=0.0,
                                        scale=sb_small["wsg"][:, A:A + 1]))
                EMIT("scalar", _a)
                st[f"f2ev{it % 2}"] = sA.n
                mk_ug2 = sA.n
                def _v(v, A=A, it=it, mk_ug2=mk_ug2):
                    sA.wait(v, mk_ug2)
                    sP.wait(v, st.get("f2hps", 0))
                    sV.inc(v.scalar_tensor_tensor(
                        ABF.v(F_YSB + (it % 2) * 512, 512),
                        ABF.v(F_USB + (it % 2) * 512, 512),
                        sb_small["sudp"][:, A:A + 1],
                        ABF.v(F_GSB + (it % 2) * 512, 512),
                        op0=ALU.mult, op1=ALU.mult))
                EMIT("vector", _v)
                st[f"f2yw{it % 2}"] = sV.n
                st["f2y"] = sV.n
        emit_h128(63, 3)

    F_ZLD, F_YST = 0, 49152

    def f3a():
        for bg in range(4):
            for tc in range(4):
                it = bg * 4 + tc
                zb = ABF.v(F_ZLD + (it % 3) * 16384, 16384)
                ys = F_YST + (it % 2) * 16384
                war = st.get(f"f3az{it % 3}", 0)
                def _g(g, bg=bg, tc=tc, zb=zb, war=war):
                    sP.wait(g, war)
                    qC.inc(g.dma_start(
                        zb.rearrange("p (b t) -> p b t", t=512)[0:64],
                        z_scr.ap()[:, bg * 32:(bg + 1) * 32,
                                   tc * 512:(tc + 1) * 512]), 16)
                EMIT("gpsimd", _g)
                mk_zl = qC.n
                p0f = sP.n
                a0f, v0f = sA.n, sV.n
                war_st = st.get(f"f3stg{it % 2}", 0)
                def _t(t, zb=zb, mk_zl=mk_zl, a0f=a0f, v0f=v0f):
                    qC.wait(t, mk_zl)
                    sA.wait(t, st.get("f3aev", 0))
                    sV.wait(t, st.get("f3avv", 0))
                    zv = zb.rearrange("p (b t) -> p b t", t=512)
                    for b2 in range(32):
                        if b2 >= 4:
                            if (b2 - 4) % 2 == 0:
                                sA.wait(t, a0f + (b2 - 4) // 2 + 1)
                            else:
                                sV.wait(t, v0f + (b2 - 4) // 2 + 1)
                        sP.inc(t.matmul(PS[b2 % 4][0:64, 0:512],
                                        sb_small["h64"][:], zv[0:64, b2],
                                        start=True, stop=True))
                EMIT("tensor", _t)
                st[f"f3az{it % 3}"] = sP.n
                def _a(a, p0f=p0f, war_st=war_st, ys=ys):
                    qL.wait(a, war_st)
                    for b2 in range(0, 32, 2):
                        sP.wait(a, p0f + b2 + 1)
                        sA.inc(a.activation(ABF.v(ys + b2 * 512, 512, 0, 64),
                                            PS[b2 % 4][0:64, 0:512], AF.Identity,
                                            bias=0.0, scale=0.125))
                EMIT("scalar", _a)
                def _v(v, p0f=p0f, war_st=war_st, ys=ys):
                    qL.wait(v, war_st)
                    for b2 in range(1, 32, 2):
                        sP.wait(v, p0f + b2 + 1)
                        sV.inc(v.tensor_scalar(
                            ABF.v(ys + b2 * 512, 512, 0, 64),
                            PS[b2 % 4][0:64, 0:512], 0.125, None, op0=ALU.mult))
                EMIT("vector", _v)
                st["f3aev"], st["f3avv"] = sA.n, sV.n
                mk_ev, mk_vv = sA.n, sV.n
                def _sy2(sy, bg=bg, tc=tc, mk_ev=mk_ev, mk_vv=mk_vv, ys=ys):
                    sA.wait(sy, mk_ev)
                    sV.wait(sy, mk_vv)
                    qL.inc(sy.dma_start(
                        yt_scr.ap()[:, bg * 32:(bg + 1) * 32,
                                    tc * 512:(tc + 1) * 512],
                        ABF.v(ys, 16384).rearrange(
                            "p (b t) -> p b t", t=512)[0:64]), 16)
                EMIT("sync", _sy2)
                st[f"f3stg{it % 2}"] = qL.n

    F_YT, F_WDN, F_FFM, F_FTM, F_LD1, F_LD2 = (0, 32768, 57344,
                                               65536, 73728, 75776)

    def f3b():
        def emit_ffm_tp(dc, mk_ev):
            p0t, v0t = sP.n, sV.n
            def _t2(t, dc=dc, mk_ev=mk_ev, v0t=v0t):
                sA.wait(t, mk_ev)
                for ts in range(4):
                    sV.wait(t, v0t if ts < 2 else v0t + ts - 1)
                    sP.inc(t.transpose(
                        PT[ts % 2][:, 0:128],
                        ABF.v(F_FFM + dc * 512 + ts * 128, 128),
                        sb_small["identb"][:]))
            EMIT("tensor", _t2)
            def _v(v, dc=dc, p0t=p0t):
                sA.wait(v, st.get("f3bln", 0))
                for ts in range(4):
                    sP.wait(v, p0t + ts + 1)
                    sV.inc(v.tensor_copy(
                        ABF.v(F_FTM + ts * 2048 + dc * 128, 128),
                        PT[ts % 2][:, 0:128]))
            EMIT("vector", _v)

        def emit_wload(tc, dc):
            it = tc * 16 + dc
            wdb = ABF.v(F_WDN + (it % 3) * 8192, 8192)
            war_w = st.get(f"f3bw{it % 3}", 0)
            wq = qC if dc % 2 == 0 else qL
            def _sy2(sy, dc=dc, wdb=wdb, war_w=war_w, wq=wq):
                sP.wait(sy, war_w)
                qB.wait(sy, st.get("p0done", 0))
                wq.inc(sy.dma_start(
                    wdb.rearrange("p (a j) -> p a j", j=128),
                    wdn_bf.ap()[dc]), 16)
            EMIT("gpsimd" if dc % 2 == 0 else "sync", _sy2)
            return (wq, wq.n)

        for tc in range(4):
            war = st.get("f3byt", 0)
            wl = {}
            for dc0 in range(3):
                wl[dc0] = emit_wload(tc, dc0)
            def _sy(sy, tc=tc, war=war):
                sP.wait(sy, war)
                for ck in range(4):
                    qC.inc(sy.dma_start(
                        ABF.v(F_YT + ck * 8192, 8192).rearrange(
                            "p (a t) -> p a t", t=512),
                        yt_scr.ap()[ck * 16:(ck + 1) * 16, :,
                                    tc * 512:(tc + 1) * 512].rearrange(
                            "a b t -> b a t")), 16)
            EMIT("gpsimd", _sy)
            mk_yt = qC.n
            mk_yts = [mk_yt - 48, mk_yt - 32, mk_yt - 16, mk_yt]
            for dc in range(16):
                it = tc * 16 + dc
                wdb = ABF.v(F_WDN + (it % 3) * 8192, 8192)
                wq, mk_wl = wl[dc]
                psm = PS[dc % 2]
                war_ps = st.get(f"f3bps{dc % 2}", 0)
                def _t(t, dc=dc, wdb=wdb, psm=psm, mk_wl=mk_wl, mk_yt=mk_yt,
                       war_ps=war_ps, wq=wq, mk_yts=mk_yts):
                    wq.wait(t, mk_wl)
                    sA.wait(t, war_ps)
                    if dc > 0:
                        qC.wait(t, mk_yt)
                    for a2 in range(64):
                        if dc == 0 and a2 % 16 == 0:
                            qC.wait(t, mk_yts[a2 // 16])
                        sP.inc(t.matmul(psm[:, 0:512],
                                        wdb[:, a2 * 128:(a2 + 1) * 128],
                                        ABF.v(F_YT + a2 * 512, 512),
                                        start=(a2 == 0), stop=(a2 == 63)))
                    st[f"f3bw{it % 3}"] = sP.n
                EMIT("tensor", _t)
                if dc + 3 <= 15:
                    wl[dc + 3] = emit_wload(tc, dc + 3)
                pend = st.pop("f3bpend", None)
                if pend is not None:
                    emit_ffm_tp(*pend)
                mk_mm = sP.n
                def _a(a, dc=dc, psm=psm, mk_mm=mk_mm):
                    sP.wait(a, mk_mm)
                    sA.inc(a.activation(ABF.v(F_FFM + dc * 512, 512),
                                        psm[:, 0:512], AF.Identity, bias=0.0,
                                        scale=sb_small["wdp"][:, dc:dc + 1]))
                    st[f"f3bps{dc % 2}"] = sA.n
                EMIT("scalar", _a)
                st["f3bpend"] = (dc, sA.n)
            emit_ffm_tp(*st.pop("f3bpend"))
            mk_ftm = sV.n
            for ts in range(4):
                r0 = tc * 512 + ts * 128
                def _sy3(sy, r0=r0):
                    qL.wait(sy, st.get("f3bst", 0))
                    sV.wait(sy, st.get("f3badd", 0))
                    qL.inc(sy.dma_start(AFP.v(2048, 2048),
                                        x1_scr.ap()[r0:r0 + 128, :]), 16)
                EMIT("sync", _sy3)
                deps = [(sV, mk_ftm), (qL, qL.n)]
                mk_ln = emit_ln(ABF.v(F_FTM + ts * 2048, 2048),
                                AFP.v(0, 2048), ABF.v(F_LD1, 2048),
                                ABF.v(F_LD2, 2048), deps,
                                out_war=[(sV, st.get("f3badd", 0))])
                def _v2(v, mk_ln=mk_ln):
                    sA.wait(v, mk_ln)
                    qL.wait(v)
                    sV.inc(v.tensor_add(AFP.v(2048, 2048), AFP.v(2048, 2048),
                                        AFP.v(0, 2048)))
                EMIT("vector", _v2)
                st["f3badd"] = sV.n
                mk_x2 = sV.n
                def _sy4(sy, r0=r0, mk_x2=mk_x2):
                    sV.wait(sy, mk_x2)
                    qL.inc(sy.dma_start(out_d.ap()[r0:r0 + 128, :],
                                        AFP.v(2048, 2048)), 16)
                EMIT("sync", _sy4)
                st["f3bst"] = qL.n
                st["f3bln"] = mk_ln
            st["f3byt"] = sP.n

    tap("xhad", [128, 32768], BF16)

    phases = [("p0", p0), ("p1a", p1a), ("p1b", p1b), ("p1c", p1c),
              ("f1", f1), ("f2", f2), ("f3a", f3a), ("f3b", f3b)]
    PHASE_MARKS.clear()
    for name, fn in phases:
        st["phase"] = name
        fn()
        phase_fence()
        PHASE_MARKS[name] = nc.n_instructions()
        if stop_after == name:
            break

    def _fin(sy):
        for s in ALLS:
            s.wait(sy)
        qB.wait(sy)
    EMIT("sync", _fin)
    blk.__exit__(None, None, None)
    check_deadlock()
    return nc, tap_d



def _prep_inputs(inputs):
    g = lambda k: np.asarray(inputs[k])
    hc = _host_consts()
    per_core_common = {
        "Wdkv": g("L0_attn_W_dkv"), "Wdq": g("L0_attn_W_dq"),
        "Wuk": g("L0_attn_W_uk"), "Wuv": g("L0_attn_W_uv"),
        "Wuq": g("L0_attn_W_uq"), "Wqr": g("L0_attn_W_qr"),
        "Wkr": g("L0_attn_W_kr"), "Wo": g("L0_attn_W_o"),
        "WupT": np.ascontiguousarray(g("L0_ffn_W_up").T),
        "WgT": np.ascontiguousarray(g("L0_ffn_W_gate").T),
        "WdnT": np.ascontiguousarray(g("L0_ffn_W_down").T),
        "bdkv": np.ascontiguousarray(g("L0_attn_b_dkv").reshape(4, 128).T),
        "bdq": np.ascontiguousarray(g("L0_attn_b_dq").reshape(4, 128).T),
        "bkr": np.ascontiguousarray(g("L0_attn_b_kr").reshape(4, 128).T),
        "buk": np.ascontiguousarray(g("L0_attn_b_uk").reshape(16, 96).T),
        "buq": np.ascontiguousarray(g("L0_attn_b_uq").reshape(16, 96).T),
        "bqr": np.ascontiguousarray(g("L0_attn_b_qr").reshape(16, 32).T),
        "buv": np.ascontiguousarray(g("L0_attn_b_uv").reshape(1, 2048)),
        "bo": np.ascontiguousarray(g("L0_attn_b_o").reshape(16, 128).T),
        "suug": np.ascontiguousarray(g("L0_ffn_SU_ug").reshape(16, 128).T),
        "wsu": np.ascontiguousarray(
            g("L0_ffn_Wscale_ug")[:INTER].reshape(64, 128).T),
        "wsg": np.ascontiguousarray(
            g("L0_ffn_Wscale_ug")[INTER:].reshape(64, 128).T),
        "sudp": np.ascontiguousarray(g("L0_ffn_SU_dp").reshape(64, 128).T),
        "wdp": np.ascontiguousarray(g("L0_ffn_Wscale_dp").reshape(16, 128).T),
    }
    for k in ["h128", "h64", "h2048t", "mrotT", "cosk", "sink",
              "onesb", "onesf", "onescol", "identb", "identf"]:
        per_core_common[k] = hc[k]
    x = np.asarray(inputs["L0_attn_x"])
    in_maps = []
    for c in range(8):
        m = dict(per_core_common)
        m["x"] = np.ascontiguousarray(x[c])
        in_maps.append(m)
    return in_maps


def kernel(**inputs):
    nc, _ = build_program()
    in_maps = _prep_inputs(inputs)
    res = run_bass_kernel_spmd(nc, in_maps, core_ids=list(range(8)))
    return np.stack([np.asarray(r["out"], dtype=np.float32)
                     for r in res.results], axis=0)
